# version 01
# baseline (speedup 1.0000x reference)
import numpy as np
import ml_dtypes
import concourse.bass as bass
import concourse.mybir as mybir
from concourse.bass_utils import run_bass_kernel_spmd

F32 = mybir.dt.float32
BF16 = mybir.dt.bfloat16
AF = mybir.ActivationFunctionType
ALU = mybir.AluOpType
AX = mybir.AxisListType

NCORE = 8
T = 2048
D = 1024
E = 256
NEXP = 8
HD = 64
DH = 4096
NTC = T // 128
NHC = D // 128
LN_EPS = 1e-5


class Prog:

    def __init__(self, nc):
        self.nc = nc
        self.s = {}
        self.c = {}

    def sems(self, *names):
        for n in names:
            self.s[n] = self.nc.alloc_semaphore(n)
            self.c[n] = 0

    def inc(self, instr, name, k=1):
        instr.then_inc(self.s[name], k)
        self.c[name] += k
        return self.c[name]

    def wait(self, eng, name, val=None):
        v = self.c[name] if val is None else val
        if v > 0:
            eng.wait_ge(self.s[name], v)

    def selfw(self, eng, name):
        if self.c[name] > 0:
            eng.wait_ge(self.s[name], self.c[name])


import os
_LN_MODE = os.environ.get("LN_MODE", "ts2")
_LN_DELAY = bool(int(os.environ.get("LN_DELAY", "0")))


def ln_stats_batch(p, b, x_tm, st, ntc):
    def act(a):
        p.wait(a, "dve")
        for tc in range(ntc):
            p.selfw(a, "act")
            p.inc(a.activation(st["dump"][:], x_tm[:, tc, :], AF.Square,
                               bias=0.0, scale=1.0,
                               accum_out=st["sq"][:, tc:tc + 1]), "act")
    b.scalar(act)
    def dve(v):
        p.selfw(v, "dve")
        for tc in range(ntc):
            p.inc(v.reduce_sum(st["sum"][:, tc:tc + 1], x_tm[:, tc, :], AX.X), "dve")
        p.wait(v, "act")
        s = slice(0, ntc)
        p.selfw(v, "dve")
        p.inc(v.tensor_scalar(st["m"][:, s], st["sum"][:, s], 1.0 / D, 0.0,
                              ALU.mult, ALU.add), "dve")
        p.inc(v.tensor_scalar(st["ex2"][:, s], st["sq"][:, s], 1.0 / D, 0.0,
                              ALU.mult, ALU.add), "dve")
        p.selfw(v, "dve")
        p.inc(v.tensor_mul(st["msq"][:, s], st["m"][:, s], st["m"][:, s]), "dve")
        p.selfw(v, "dve")
        p.inc(v.tensor_sub(st["var"][:, s], st["ex2"][:, s], st["msq"][:, s]), "dve")
    b.vector(dve)
    def act2(a):
        p.wait(a, "dve")
        s = slice(0, ntc)
        p.inc(a.activation(st["lnv"][:, s], st["var"][:, s], AF.Ln,
                           bias=st["eps"][:], scale=1.0), "act")
        p.selfw(a, "act")
        p.inc(a.activation(st["rstd"][:, s], st["lnv"][:, s], AF.Exp,
                           bias=0.0, scale=st["mhalf"][:]), "act")
    b.scalar(act2)
    def dve2(v):
        p.wait(v, "act")
        s = slice(0, ntc)
        p.inc(v.tensor_mul(st["mh"][:, s], st["m"][:, s], st["rstd"][:, s]), "dve")
        p.selfw(v, "dve")
        p.inc(v.tensor_scalar(st["nmh"][:, s], st["mh"][:, s], -1.0, 0.0,
                              ALU.mult, ALU.add), "dve")
    b.vector(dve2)


def ln_apply_and_fm(p, b, x_tm, x_tmb, x_fmb, st, ntc):
    def dve(v):
        for tc in range(ntc):
            p.selfw(v, "dve")
            p.inc(v.tensor_scalar(x_tm[:, tc, :], x_tm[:, tc, :],
                                  st["rstd"][:, tc:tc + 1], st["nmh"][:, tc:tc + 1],
                                  ALU.mult, ALU.add), "dve")
            p.selfw(v, "dve")
            p.inc(v.tensor_copy(x_tmb[:, tc, :], x_tm[:, tc, :]), "dve")
    b.vector(dve)
    def sy(s):
        p.wait(s, "dve")
        for tc in range(ntc):
            for hc in range(NHC):
                p.selfw(s, "sy")
                p.inc(s.dma_start(
                    x_fmb[:, hc, tc * 128:(tc + 1) * 128],
                    x_tmb[:, tc, hc * 128:(hc + 1) * 128], transpose=True), "sy", 16)
    b.sync(sy)


def ln_tile(p, b, x_ap, st, slot):
    sl = slice(slot, slot + 1)
    def act(a):
        p.wait(a, "dve")
        p.inc(a.activation(st["dump"][:], x_ap, AF.Square, bias=0.0, scale=1.0,
                           accum_out=st["sq"][:, sl]), "act")
    b.scalar(act)
    def dve(v):
        p.selfw(v, "dve")
        p.inc(v.reduce_sum(st["sum"][:, sl], x_ap, AX.X), "dve")
        p.wait(v, "act")
        p.selfw(v, "dve")
        p.inc(v.tensor_scalar(st["m"][:, sl], st["sum"][:, sl], 1.0 / D, 0.0,
                              ALU.mult, ALU.add), "dve")
        p.inc(v.tensor_scalar(st["ex2"][:, sl], st["sq"][:, sl], 1.0 / D, 0.0,
                              ALU.mult, ALU.add), "dve")
        p.selfw(v, "dve")
        p.inc(v.tensor_mul(st["msq"][:, sl], st["m"][:, sl], st["m"][:, sl]), "dve")
        p.selfw(v, "dve")
        p.inc(v.tensor_sub(st["var"][:, sl], st["ex2"][:, sl], st["msq"][:, sl]), "dve")
    b.vector(dve)
    def act2(a):
        p.wait(a, "dve")
        p.inc(a.activation(st["lnv"][:, sl], st["var"][:, sl], AF.Ln,
                           bias=st["eps"][:], scale=1.0), "act")
        p.selfw(a, "act")
        p.inc(a.activation(st["rstd"][:, sl], st["lnv"][:, sl], AF.Exp,
                           bias=0.0, scale=st["mhalf"][:]), "act")
    b.scalar(act2)
    def dve2(v):
        p.wait(v, "act")
        p.inc(v.tensor_mul(st["mh"][:, sl], st["m"][:, sl], st["rstd"][:, sl]), "dve")
        p.selfw(v, "dve")
        p.inc(v.tensor_scalar(st["nmh"][:, sl], st["mh"][:, sl], -1.0, 0.0,
                              ALU.mult, ALU.add), "dve")
        p.selfw(v, "dve")
        p.inc(v.tensor_scalar(x_ap, x_ap, st["rstd"][:, sl], st["nmh"][:, sl],
                              ALU.mult, ALU.add), "dve")
    b.vector(dve2)


_BOOT = bool(int(os.environ.get("KBOOT", "1")))


def boot_clear(b, nc, p):
    if not _BOOT:
        return
    """Clear OUR kernel semaphores at program start (crashed programs leave
    stale values; a stale sem makes every wait_ge pass early). gpsimd clears;
    the other engines sit in a timed nop longer than the clear, then check a
    boot sem that the clear zeroed before it was incremented. Bass's own
    barrier sems are left alone (their protocol is self-normalizing)."""
    bs = nc.alloc_semaphore("bootsem")
    lo = min(h.num for h in list(p.s.values()) + [bs])
    rng_ = range(lo, nc._kernel_sem_range.stop)
    def gp(g):
        g.sem_clear(rng_)
        g.sem_inc(bs, 1)
    b.gpsimd(gp)
    for name in ("tensor", "vector", "scalar", "sync"):
        def w(e):
            e.nop(cycle_cnt=20000)
            e.wait_ge(bs, 1)
        getattr(b, name)(w)


def st_alloc(nc):
    st = {n: nc.alloc_sbuf_tensor("st_" + n, [128, NTC], F32)
          for n in ("sum", "sq", "m", "ex2", "msq", "var", "lnv", "rstd", "mh", "nmh")}
    st["dump"] = nc.alloc_sbuf_tensor("st_dump", [128, D], BF16)
    st["eps"] = nc.alloc_sbuf_tensor("st_eps", [128, 1], F32)
    st["mhalf"] = nc.alloc_sbuf_tensor("st_mhalf", [128, 1], F32)
    return st


def st_init(p, b, st):
    def dv(v):
        p.inc(v.memset(st["eps"][:], LN_EPS), "dve")
        p.inc(v.memset(st["mhalf"][:], -0.5), "dve")
    b.vector(dv)



class _Stop(Exception):
    pass


def _chk(upto, n):
    if upto < n:
        raise _Stop


def build_A(upto=99):
    nc = bass.Bass("TRN2", target_bir_lowering=False)
    xa = nc.dram_tensor("xa", [T, D], F32, kind="ExternalInput")
    xaT = nc.dram_tensor("xaT", [D, T], F32, kind="ExternalInput")
    qp = nc.dram_tensor("qp", [D, E], F32, kind="ExternalInput")
    kp = nc.dram_tensor("kp", [D, E], F32, kind="ExternalInput")
    gTd = nc.dram_tensor("gT", [D, NEXP], F32, kind="ExternalInput")
    ewT = nc.dram_tensor("ewT", [NEXP, D, D], F32, kind="ExternalInput")
    wqT = nc.dram_tensor("wqT", [D, HD], F32, kind="ExternalInput")
    wkT = nc.dram_tensor("wkT", [D, HD], F32, kind="ExternalInput")
    wvT = nc.dram_tensor("wvT", [D, HD], F32, kind="ExternalInput")
    bqd = nc.dram_tensor("bq", [HD, 1], F32, kind="ExternalInput")
    bkd = nc.dram_tensor("bk", [HD, 1], F32, kind="ExternalInput")
    bvd = nc.dram_tensor("bv", [HD, 1], F32, kind="ExternalInput")
    eyeb_d = nc.dram_tensor("eyeb", [128, 128], BF16, kind="ExternalInput")
    eyef_d = nc.dram_tensor("eyef", [128, 128], F32, kind="ExternalInput")
    sel8_d = nc.dram_tensor("sel8", [NEXP, NEXP * 128], BF16, kind="ExternalInput")

    x3_o = nc.dram_tensor("x3_o", [T, D], F32, kind="ExternalOutput")
    q_o = nc.dram_tensor("q_o", [HD, T], BF16, kind="ExternalOutput")
    k_o = nc.dram_tensor("k_o", [HD, T], BF16, kind="ExternalOutput")
    v_o = nc.dram_tensor("v_o", [T, HD], BF16, kind="ExternalOutput")
    qkm_o = nc.dram_tensor("qkm_o", [1, 2], F32, kind="ExternalOutput")

    x_tm = nc.alloc_sbuf_tensor("x_tm", [128, NTC, D], F32)
    x_tmb = nc.alloc_sbuf_tensor("x_tmb", [128, NTC, D], BF16)
    x_fmb = nc.alloc_sbuf_tensor("x_fmb", [128, NHC, T], BF16)
    qpb = nc.alloc_sbuf_tensor("qpb", [128, NHC, E], BF16)
    kpb = nc.alloc_sbuf_tensor("kpb", [128, NHC, E], BF16)
    wq_b = qpb[:, :, 0:HD]
    wk_b = qpb[:, :, HD:2 * HD]
    wv_b = qpb[:, :, 2 * HD:3 * HD]
    pool_a = nc.alloc_sbuf_tensor("pool_a", [128, 4096], BF16)
    qe = pool_a[:].rearrange("p (a t) -> p a t", a=2)
    mb_all = pool_a[:].rearrange("p (e t) -> p e t", e=NEXP)
    pool_b = nc.alloc_sbuf_tensor("pool_b", [128, 8192], BF16)
    ke = pool_b[:, 0:4096].rearrange("p (a e) -> p a e", a=NTC)
    moe_fm2 = pool_b[:].rearrange("p (s f t) -> p s f t", s=2, f=NEXP)
    kv_b = nc.alloc_sbuf_tensor("kv_b", [128, 2, D], BF16)
    zrow = nc.alloc_sbuf_tensor("zrow", [1, T], F32)
    z_tm = nc.alloc_sbuf_tensor("z_tm", [128, NTC], F32)
    ones_b = nc.alloc_sbuf_tensor("ones_b", [128, 1], BF16)
    onesr_b = nc.alloc_sbuf_tensor("onesr_b", [1, 128], BF16)
    eyeb = nc.alloc_sbuf_tensor("eyeb_t", [128, 128], BF16)
    eyef = nc.alloc_sbuf_tensor("eyef_t", [128, 128], F32)
    st = st_alloc(nc)
    gTb = nc.alloc_sbuf_tensor("gTb", [128, NHC, NEXP], BF16)
    scor = nc.alloc_sbuf_tensor("scor", [128, NTC, NEXP], F32)
    gmax = nc.alloc_sbuf_tensor("gmax", [128, NTC], F32)
    pre = nc.alloc_sbuf_tensor("pre", [128, NTC], F32)
    tta = nc.alloc_sbuf_tensor("tta", [128, NTC], F32)
    ttb = nc.alloc_sbuf_tensor("ttb", [128, NTC], F32)
    masks_tm = nc.alloc_sbuf_tensor("masks_tm", [128, NTC, NEXP], BF16)
    shr_a = nc.alloc_sbuf_tensor("shr_a", [HD, T], BF16)
    masks_f = shr_a[0:NEXP, :]
    mtile = nc.alloc_sbuf_tensor("mtile", [128, 512], BF16)
    w_moe = nc.alloc_sbuf_tensor("w_moe", [128, 2, 1024], BF16)
    moe_tm2 = nc.alloc_sbuf_tensor("moe_tm2", [128, 2, D], BF16)
    bq_t = nc.alloc_sbuf_tensor("bq_t", [HD, 1], F32)
    bk_t = nc.alloc_sbuf_tensor("bk_t", [HD, 1], F32)
    bv_t = nc.alloc_sbuf_tensor("bv_t", [HD, 1], F32)
    qf = nc.alloc_sbuf_tensor("qf", [HD, T], BF16)
    kf = nc.alloc_sbuf_tensor("kf", [HD, T], BF16)
    vf = nc.alloc_sbuf_tensor("vf", [HD, T], BF16)
    sqn = shr_a[:, :]
    v_tmb = nc.alloc_sbuf_tensor("v_tmb", [128, NTC, HD], BF16)
    qkm_t = nc.alloc_sbuf_tensor("qkm_t", [1, 2], F32)
    tmp_res = nc.alloc_sbuf_tensor("tmp_res", [128, 512], F32)
    sel8 = tmp_res[:].bitcast(BF16)[0:NEXP, :]

    PB = [nc.alloc_psum_tensor(f"pb{i}", [128, 512], F32) for i in range(8)]
    PBb = [t[:].bitcast(BF16) for t in PB]

    p = Prog(nc)
    p.sems("sy", "gp", "pe", "act", "dve")

    with nc.Block() as b:
      try:
        boot_clear(b, nc, p)
        def sy0(s):
            p.selfw(s, "sy")
            p.inc(s.dma_start(x_tm[:], xa[:].rearrange("(c p) h -> p c h", p=128)), "sy", 16)
            p.selfw(s, "sy")
            p.inc(s.dma_start(eyeb[:], eyeb_d[:]), "sy", 16)
            p.selfw(s, "sy")
            p.inc(s.dma_start(eyef[:], eyef_d[:]), "sy", 16)
            p.selfw(s, "sy")
            p.inc(s.dma_start(bq_t[:], bqd[:]), "sy", 16)
            p.selfw(s, "sy")
            p.inc(s.dma_start(bk_t[:], bkd[:]), "sy", 16)
            p.selfw(s, "sy")
            p.inc(s.dma_start(bv_t[:], bvd[:]), "sy", 16)
        b.sync(sy0)
        def gp0(g):
            p.selfw(g, "gp")
            p.inc(g.dma_start(x_tmb[:], xa[:].rearrange("(c p) h -> p c h", p=128)), "gp", 16)
            p.selfw(g, "gp")
            p.inc(g.dma_start(x_fmb[:], xaT[:].rearrange("(c p) t -> p c t", p=128)), "gp", 16)
            p.selfw(g, "gp")
            p.inc(g.dma_start(qpb[:], qp[:].rearrange("(c p) e -> p c e", p=128)), "gp", 16)
            p.selfw(g, "gp")
            p.inc(g.dma_start(kpb[:], kp[:].rearrange("(c p) e -> p c e", p=128)), "gp", 16)
            p.selfw(g, "gp")
            p.inc(g.dma_start(gTb[:], gTd[:].rearrange("(c p) e -> p c e", p=128)), "gp", 16)
        b.gpsimd(gp0)
        def dv0(v):
            p.inc(v.memset(ones_b[:], 1.0), "dve")
            p.inc(v.memset(onesr_b[:], 1.0), "dve")
        b.vector(dv0)
        st_init(p, b, st)
        gp_ld = p.c["gp"]

        _chk(upto, 2)
        act0 = p.c["act"]
        qpe, kpe = [], []
        def pe_qk(t):
            p.wait(t, "gp", gp_ld)
            p.wait(t, "sy")
            for ec in range(2):
                for tn in range(4):
                    g = ec * 4 + tn
                    if g >= 2:
                        t.wait_ge(p.s["act"], act0 + (g - 2) + 1)
                    bank = PB[g % 2]
                    for hc in range(NHC):
                        mm = t.matmul(bank[:], qpb[:, hc, ec * 128:(ec + 1) * 128],
                                      x_fmb[:, hc, tn * 512:(tn + 1) * 512],
                                      start=(hc == 0), stop=(hc == NHC - 1))
                    qpe.append(p.inc(mm, "pe"))
            for tc in range(NTC):
                if tc >= 2:
                    t.wait_ge(p.s["act"], act0 + 8 + (tc - 2) + 1)
                bank = PB[2 + tc % 2]
                for hc in range(NHC):
                    mm = t.matmul(bank[:, 0:E], x_fmb[:, hc, tc * 128:(tc + 1) * 128],
                                  kpb[:, hc, :], start=(hc == 0), stop=(hc == NHC - 1))
                kpe.append(p.inc(mm, "pe"))
        b.tensor(pe_qk)
        def act_qk(a):
            for ec in range(2):
                for tn in range(4):
                    g = ec * 4 + tn
                    p.wait(a, "pe", qpe[g])
                    p.selfw(a, "act")
                    p.inc(a.activation(qe[:, ec, tn * 512:(tn + 1) * 512],
                                       PB[g % 2][:], AF.Exp, bias=0.0, scale=1.0), "act")
            for tc in range(NTC):
                p.wait(a, "pe", kpe[tc])
                p.selfw(a, "act")
                p.inc(a.activation(ke[:, tc, :], PB[2 + tc % 2][:, 0:E],
                                   AF.Exp, bias=0.0, scale=1.0), "act")
        b.scalar(act_qk)

        _chk(upto, 3)
        dve0 = p.c["dve"]
        zpe = []
        def pe_z(t):
            p.wait(t, "act")
            t.wait_ge(p.s["dve"], 2)
            for tn in range(4):
                if tn >= 1:
                    t.wait_ge(p.s["dve"], dve0 + tn)
                for ec in range(2):
                    mm = t.matmul(PB[4][0:1, :], ones_b[:, 0:1],
                                  qe[:, ec, tn * 512:(tn + 1) * 512],
                                  start=(ec == 0), stop=(ec == 1))
                zpe.append(p.inc(mm, "pe"))
        b.tensor(pe_z)
        def dv_z(v):
            for tn in range(4):
                p.wait(v, "pe", zpe[tn])
                p.inc(v.tensor_copy(zrow[0:1, tn * 512:(tn + 1) * 512], PB[4][0:1, :]),
                      "dve")
            p.selfw(v, "dve")
            p.inc(v.tensor_single_scalar(zrow[:], zrow[:], 1e-8, ALU.add), "dve")
            p.selfw(v, "dve")
            p.inc(v.reciprocal(zrow[:], zrow[:]), "dve")
        b.vector(dv_z)
        dve1 = p.c["dve"]
        ztr = []
        def pe_zt(t):
            p.wait(t, "dve", dve1)
            for tc in range(NTC):
                if tc >= 2:
                    t.wait_ge(p.s["dve"], dve1 + (tc - 2) + 1)
                tr = t.transpose(PB[4 + tc % 2][:, 0:1],
                                 zrow[0:1, tc * 128:(tc + 1) * 128], eyef[0:1, 0:1])
                ztr.append(p.inc(tr, "pe"))
        b.tensor(pe_zt)
        def dv_zt(v):
            for tc in range(NTC):
                p.wait(v, "pe", ztr[tc])
                p.inc(v.tensor_copy(z_tm[:, tc:tc + 1], PB[4 + tc % 2][:, 0:1]), "dve")
        b.vector(dv_zt)

        _chk(upto, 4)
        dve2 = p.c["dve"]
        kvpe = []
        def pe_kv(t):
            for g in range(4):
                ec, hn = divmod(g, 2)
                if g >= 2:
                    t.wait_ge(p.s["dve"], dve2 + (g - 2) + 1)
                bank = PB[6 + g % 2]
                for tc in range(NTC):
                    mm = t.matmul(bank[:], ke[:, tc, ec * 128:(ec + 1) * 128],
                                  x_tmb[:, tc, hn * 512:(hn + 1) * 512],
                                  start=(tc == 0), stop=(tc == NTC - 1))
                kvpe.append(p.inc(mm, "pe"))
        b.tensor(pe_kv)
        def dv_kv(v):
            for g in range(4):
                ec, hn = divmod(g, 2)
                p.wait(v, "pe", kvpe[g])
                p.inc(v.tensor_copy(kv_b[:, ec, hn * 512:(hn + 1) * 512],
                                    PB[6 + g % 2][:]), "dve")
        b.vector(dv_kv)

        _chk(upto, 5)
        act1 = p.c["act"]
        numpe = []
        def pe_num(t):
            p.wait(t, "dve")
            for g in range(NTC * 2):
                tc, hn = divmod(g, 2)
                if g >= 2:
                    t.wait_ge(p.s["act"], act1 + (g - 2) + 1)
                bank = PB[g % 2]
                for ec in range(2):
                    mm = t.matmul(bank[:], qe[:, ec, tc * 128:(tc + 1) * 128],
                                  kv_b[:, ec, hn * 512:(hn + 1) * 512],
                                  start=(ec == 0), stop=(ec == 1))
                numpe.append(p.inc(mm, "pe"))
        b.tensor(pe_num)
        dve3 = p.c["dve"]
        numact = []
        def act_num(a):
            for g in range(NTC * 2):
                tc, hn = divmod(g, 2)
                p.wait(a, "pe", numpe[g])
                if g >= 1:
                    a.wait_ge(p.s["dve"], dve3 + g)
                p.selfw(a, "act")
                numact.append(p.inc(a.activation(tmp_res[:], PB[g % 2][:],
                                                 AF.Copy, bias=0.0,
                                                 scale=z_tm[:, tc:tc + 1]), "act"))
        b.scalar(act_num)
        def dv_num(v):
            for g in range(NTC * 2):
                tc, hn = divmod(g, 2)
                p.wait(v, "act", numact[g])
                p.selfw(v, "dve")
                p.inc(v.tensor_add(x_tm[:, tc, hn * 512:(hn + 1) * 512],
                                   x_tm[:, tc, hn * 512:(hn + 1) * 512],
                                   tmp_res[:]), "dve")
        b.vector(dv_num)

        _chk(upto, 6)
        ln_stats_batch(p, b, x_tm, st, NTC)
        ln_apply_and_fm(p, b, x_tm, x_tmb, x_fmb, st, NTC)

        _chk(upto, 7)
        scpe = []
        dve4 = p.c["dve"]
        def pe_sc(t):
            p.wait(t, "sy")
            for tc in range(NTC):
                if tc >= 2:
                    t.wait_ge(p.s["dve"], dve4 + (tc - 2) + 1)
                bank = PB[2 + tc % 2]
                for hc in range(NHC):
                    mm = t.matmul(bank[:, 0:NEXP], x_fmb[:, hc, tc * 128:(tc + 1) * 128],
                                  gTb[:, hc, :], start=(hc == 0), stop=(hc == NHC - 1))
                scpe.append(p.inc(mm, "pe"))
        b.tensor(pe_sc)
        def dv_sc(v):
            for tc in range(NTC):
                p.wait(v, "pe", scpe[tc])
                p.inc(v.tensor_copy(scor[:, tc, :], PB[2 + tc % 2][:, 0:NEXP]), "dve")
            p.selfw(v, "dve")
            p.inc(v.reduce_max(gmax[:], scor[:], AX.X), "dve")
            for e in range(NEXP):
                se = scor[:, :, e]
                p.selfw(v, "dve")
                p.inc(v.tensor_tensor(tta[:], se, gmax[:], ALU.is_ge), "dve")
                if e == 0:
                    p.selfw(v, "dve")
                    p.inc(v.tensor_copy(masks_tm[:, :, 0], tta[:]), "dve")
                    p.inc(v.tensor_copy(pre[:], se), "dve")
                else:
                    p.selfw(v, "dve")
                    p.inc(v.tensor_tensor(ttb[:], pre[:], gmax[:], ALU.is_lt), "dve")
                    p.selfw(v, "dve")
                    p.inc(v.tensor_mul(tta[:], tta[:], ttb[:]), "dve")
                    p.selfw(v, "dve")
                    p.inc(v.tensor_copy(masks_tm[:, :, e], tta[:]), "dve")
                    if e < NEXP - 1:
                        p.selfw(v, "dve")
                        p.inc(v.tensor_max(pre[:], pre[:], se), "dve")
        b.vector(dv_sc)
        _chk(upto, 8)
        dve5 = p.c["dve"]
        mtr = []
        def pe_mt(t):
            p.wait(t, "dve", dve5)
            for tc in range(NTC):
                if tc >= 2:
                    t.wait_ge(p.s["dve"], dve5 + (tc - 2) + 1)
                tr = t.transpose(PBb[4 + tc % 2][0:NEXP, 0:128], masks_tm[:, tc, :],
                                 eyeb[:, :])
                mtr.append(p.inc(tr, "pe"))
        b.tensor(pe_mt)
        def dv_mt(v):
            for tc in range(NTC):
                p.wait(v, "pe", mtr[tc])
                p.inc(v.tensor_copy(masks_f[0:NEXP, tc * 128:(tc + 1) * 128],
                                    PBb[4 + tc % 2][0:NEXP, 0:128]), "dve")
        b.vector(dv_mt)

        _chk(upto, 9)
        def sy_sel(s):
            p.wait(s, "dve")
            p.selfw(s, "sy")
            p.inc(s.dma_start(sel8, sel8_d[:]), "sy", 16)
        b.sync(sy_sel)

        for tn in range(4):
            for e in range(NEXP):
                def pe_mb(t, e=e, tn=tn):
                    p.wait(t, "dve")
                    p.wait(t, "sy")
                    p.inc(t.matmul(PB[6][:], sel8[:, e * 128:(e + 1) * 128],
                                   masks_f[0:NEXP, tn * 512:(tn + 1) * 512],
                                   start=True, stop=True), "pe")
                b.tensor(pe_mb)
                def dv_mb(v, e=e):
                    p.wait(v, "pe")
                    p.selfw(v, "dve")
                    p.inc(v.tensor_copy(mb_all[:, e, :], PB[6][:]), "dve")
                b.vector(dv_mb)
            pe_moe0 = p.c["pe"]
            gp_moe0 = p.c["gp"]
            dve_moe0 = p.c["dve"]
            def gp_w(g, tn=tn):
                for i in range(NEXP * NHC):
                    e, hc = divmod(i, NHC)
                    if i >= 2:
                        g.wait_ge(p.s["pe"], pe_moe0 + i)
                    elif pe_moe0 > 0:
                        g.wait_ge(p.s["pe"], pe_moe0)
                    p.selfw(g, "gp")
                    p.inc(g.dma_start(w_moe[:, i % 2, :],
                                      ewT[e, hc * 128:(hc + 1) * 128, :]), "gp", 16)
            b.gpsimd(gp_w)
            def dv_mm(v, tn=tn):
                for i in range(NEXP * NHC):
                    e, hc = divmod(i, NHC)
                    if i >= 1:
                        v.wait_ge(p.s["pe"], pe_moe0 + i)
                    p.selfw(v, "dve")
                    p.inc(v.tensor_tensor(mtile[:], x_fmb[:, hc, tn * 512:(tn + 1) * 512],
                                          mb_all[:, e, :], ALU.mult), "dve")
            b.vector(dv_mm)
            def pe_moe(t, tn=tn):
                for i in range(NEXP * NHC):
                    e, hc = divmod(i, NHC)
                    t.wait_ge(p.s["gp"], gp_moe0 + (i + 1) * 16)
                    t.wait_ge(p.s["dve"], dve_moe0 + i + 1)
                    for fc in range(8):
                        mm = t.matmul(PB[fc][:], w_moe[:, i % 2, fc * 128:(fc + 1) * 128],
                                      mtile[:], start=(i == 0),
                                      stop=(i == NEXP * NHC - 1))
                    p.inc(mm, "pe")
            b.tensor(pe_moe)
            slot = tn % 2
            def dv_ev(v, tn=tn, slot=slot):
                p.wait(v, "pe")
                p.wait(v, "sy")
                p.selfw(v, "dve")
                for fc in range(8):
                    p.inc(v.tensor_copy(moe_fm2[:, slot, fc, :], PB[fc][:]), "dve")
            b.vector(dv_ev)
            for tj in range(4):
                tc = tn * 4 + tj
                def sy_tr(s, tn=tn, slot=slot, tj=tj, tc=tc):
                    p.wait(s, "dve")
                    for hc in range(NHC):
                        p.selfw(s, "sy")
                        p.inc(s.dma_start(
                            moe_tm2[:, tc % 2, hc * 128:(hc + 1) * 128],
                            moe_fm2[:, slot, hc, tj * 128:(tj + 1) * 128],
                            transpose=True), "sy", 16)
                b.sync(sy_tr)
                def dv_res(v, tc=tc):
                    p.wait(v, "sy")
                    p.selfw(v, "dve")
                    p.inc(v.tensor_scalar(x_tm[:, tc, :], x_tm[:, tc, :], 2.0, 0.0,
                                          ALU.mult, ALU.add), "dve")
                    p.selfw(v, "dve")
                    p.inc(v.tensor_add(x_tm[:, tc, :], x_tm[:, tc, :],
                                       moe_tm2[:, tc % 2, :]), "dve")
                b.vector(dv_res)


        _chk(upto, 10)
        ln_stats_batch(p, b, x_tm, st, NTC)
        ln_apply_and_fm(p, b, x_tm, x_tmb, x_fmb, st, NTC)

        def sy_x3(s):
            p.wait(s, "dve")
            p.selfw(s, "sy")
            p.inc(s.dma_start(x3_o[:].rearrange("(c p) h -> p c h", p=128), x_tm[:]),
                  "sy", 16)
        b.sync(sy_x3)

        _chk(upto, 11)
        def gp_w2(g):
            p.wait(g, "pe")
            p.selfw(g, "gp")
            p.inc(g.dma_start(wq_b, wqT[:].rearrange("(c p) e -> p c e", p=128)), "gp", 16)
            p.selfw(g, "gp")
            p.inc(g.dma_start(wk_b, wkT[:].rearrange("(c p) e -> p c e", p=128)), "gp", 16)
            p.selfw(g, "gp")
            p.inc(g.dma_start(wv_b, wvT[:].rearrange("(c p) e -> p c e", p=128)), "gp", 16)
        b.gpsimd(gp_w2)
        gp_ld2 = p.c["gp"]
        act2c = p.c["act"]
        prpe = []
        def pe_pr(t):
            p.wait(t, "sy")
            p.wait(t, "gp", gp_ld2)
            for j, wt in enumerate((wq_b, wk_b, wv_b)):
                for tn in range(4):
                    g = j * 4 + tn
                    if g >= 2:
                        t.wait_ge(p.s["act"], act2c + (g - 2) + 1)
                    bank = PB[g % 2]
                    for hc in range(NHC):
                        mm = t.matmul(bank[0:HD, :], wt[:, hc, :],
                                      x_fmb[:, hc, tn * 512:(tn + 1) * 512],
                                      start=(hc == 0), stop=(hc == NHC - 1))
                    prpe.append(p.inc(mm, "pe"))
        b.tensor(pe_pr)
        def act_pr(a):
            outs = (qf, kf, vf)
            biases = (bq_t, bk_t, bv_t)
            for j in range(3):
                for tn in range(4):
                    g = j * 4 + tn
                    p.wait(a, "pe", prpe[g])
                    p.selfw(a, "act")
                    p.inc(a.activation(outs[j][:, tn * 512:(tn + 1) * 512],
                                       PB[g % 2][0:HD, :], AF.Identity,
                                       bias=biases[j][:], scale=1.0), "act")
        b.scalar(act_pr)

        _chk(upto, 12)
        dve6 = p.c["dve"]
        vtr = []
        def pe_vt(t):
            p.wait(t, "act")
            for tc in range(NTC):
                if tc >= 2:
                    t.wait_ge(p.s["dve"], dve6 + (tc - 2) + 1)
                tr = t.transpose(PBb[2 + tc % 2][0:128, 0:HD],
                                 vf[0:HD, tc * 128:(tc + 1) * 128], eyeb[0:HD, 0:HD])
                vtr.append(p.inc(tr, "pe"))
        b.tensor(pe_vt)
        def dv_vt(v):
            for tc in range(NTC):
                p.wait(v, "pe", vtr[tc])
                p.inc(v.tensor_copy(v_tmb[:, tc, :], PBb[2 + tc % 2][0:128, 0:HD]),
                      "dve")
        b.vector(dv_vt)

        _chk(upto, 13)
        dve7 = p.c["dve"]
        nrmpe = []
        def dv_n1(v):
            p.wait(v, "act")
            p.selfw(v, "dve")
            p.inc(v.tensor_mul(sqn[:], qf[:], qf[:]), "dve")
        b.vector(dv_n1)
        def pe_nrm(t):
            for j in range(2):
                for tn in range(4):
                    g = j * 4 + tn
                    if g == 0:
                        t.wait_ge(p.s["dve"], dve7 + 1)
                    elif j == 0:
                        t.wait_ge(p.s["dve"], dve7 + 1 + g)
                    elif tn == 0:
                        t.wait_ge(p.s["dve"], dve7 + 7)
                    else:
                        t.wait_ge(p.s["dve"], dve7 + 7 + tn)
                    mm = t.matmul(PB[4][0:1, :], ones_b[0:HD, 0:1],
                                  sqn[:, tn * 512:(tn + 1) * 512], start=True, stop=True)
                    nrmpe.append(p.inc(mm, "pe"))
        b.tensor(pe_nrm)
        def dv_n2(v):
            for tn in range(4):
                p.wait(v, "pe", nrmpe[tn])
                p.selfw(v, "dve")
                p.inc(v.tensor_copy(zrow[0:1, tn * 512:(tn + 1) * 512], PB[4][0:1, :]),
                      "dve")
            p.selfw(v, "dve")
            p.inc(v.tensor_reduce(qkm_t[0:1, 0:1], zrow[:], AX.X, ALU.max), "dve")
            p.selfw(v, "dve")
            p.inc(v.tensor_mul(sqn[:], kf[:], kf[:]), "dve")
            for tn in range(4):
                p.wait(v, "pe", nrmpe[4 + tn])
                p.selfw(v, "dve")
                p.inc(v.tensor_copy(zrow[0:1, tn * 512:(tn + 1) * 512], PB[4][0:1, :]),
                      "dve")
            p.selfw(v, "dve")
            p.inc(v.tensor_reduce(qkm_t[0:1, 1:2], zrow[:], AX.X, ALU.max), "dve")
        b.vector(dv_n2)

        _chk(upto, 14)
        def sy_out(s):
            p.wait(s, "dve")
            p.wait(s, "act")
            p.selfw(s, "sy")
            p.inc(s.dma_start(q_o[:], qf[:]), "sy", 16)
            p.selfw(s, "sy")
            p.inc(s.dma_start(k_o[:], kf[:]), "sy", 16)
            p.selfw(s, "sy")
            p.inc(s.dma_start(v_o[:].rearrange("(c p) d -> p c d", p=128), v_tmb[:]),
                  "sy", 16)
            p.selfw(s, "sy")
            p.inc(s.dma_start(qkm_o[:], qkm_t[:]), "sy", 16)
            p.wait(s, "sy")
        b.sync(sy_out)
      except _Stop:
        def sy_tail(s):
            p.wait(s, "dve")
            p.wait(s, "act")
            p.wait(s, "pe")
            p.selfw(s, "sy")
            p.inc(s.dma_start(x3_o[:].rearrange("(c p) h -> p c h", p=128),
                              x_tm[:]), "sy", 16)
            p.selfw(s, "sy")
            p.inc(s.dma_start(q_o[:], qf[:]), "sy", 16)
            p.wait(s, "sy")
        b.sync(sy_tail)
    return nc



def build_B():
    nc = bass.Bass("TRN2", target_bir_lowering=False)
    x3 = nc.dram_tensor("x3", [T, D], F32, kind="ExternalInput")
    qfd = nc.dram_tensor("qf", [HD, T], BF16, kind="ExternalInput")
    kfull = nc.dram_tensor("kfull", [HD, NCORE * T], BF16, kind="ExternalInput")
    vfull = nc.dram_tensor("vfull", [NCORE * T, HD], BF16, kind="ExternalInput")
    qkmd = nc.dram_tensor("qkm", [1, 16], F32, kind="ExternalInput")
    w1T = nc.dram_tensor("w1T", [D, DH], F32, kind="ExternalInput")
    w2T = nc.dram_tensor("w2T", [D, DH], F32, kind="ExternalInput")
    h1T = nc.dram_tensor("h1T", [DH, DH], F32, kind="ExternalInput")
    h2T = nc.dram_tensor("h2T", [DH, DH], F32, kind="ExternalInput")
    w3T = nc.dram_tensor("w3T", [DH, D], F32, kind="ExternalInput")
    s1d = nc.dram_tensor("s1d", [128, 32], F32, kind="ExternalInput")
    s2d = nc.dram_tensor("s2d", [128, 32], F32, kind="ExternalInput")
    h1sd = nc.dram_tensor("h1sd", [128, 32], F32, kind="ExternalInput")
    h2sd = nc.dram_tensor("h2sd", [128, 32], F32, kind="ExternalInput")
    s3d = nc.dram_tensor("s3d", [1, D], F32, kind="ExternalInput")
    eyef_d = nc.dram_tensor("eyef", [128, 128], F32, kind="ExternalInput")
    out_o = nc.dram_tensor("out_o", [T, D], F32, kind="ExternalOutput")
    x4_d = nc.dram_tensor("x4_d", [T, D], F32, kind="Internal")

    NK = (NCORE * T) // 128

    arena = nc.alloc_sbuf_tensor("arena", [128, 65536], BF16)
    cur = [0]
    def carve(n):
        cur[0] = (cur[0] + 15) // 16 * 16
        off = cur[0]
        cur[0] += n
        return off
    o_k = carve(16384)
    k_sb = arena[:, o_k:o_k + 16384][0:HD, :]
    o_v = carve(NK * 65 + 7)
    v65 = arena[:, o_v:o_v + NK * 65].rearrange("p (c d) -> p c d", d=65)
    o_pt = carve(2048)
    pt_a = arena[:, o_pt:o_pt + 2048].rearrange("p (s t) -> p s t", s=2)
    o_y = carve(2048)
    y_sb = arena[:, o_y:o_y + 2048].bitcast(F32)
    o_q = carve(2048)
    qf_b = arena[:, o_q:o_q + 2048][0:HD, :]
    o_x3 = carve(4096)
    x3t = arena[:, o_x3:o_x3 + 4096].bitcast(F32).rearrange(
        "p (s h) -> p s h", s=2)
    h_a = arena[:, 0:32768].rearrange("p (c t) -> p c t", c=32)
    h_b = arena[:, 32768:65536].rearrange("p (c t) -> p c t", c=32)

    x_fmb = nc.alloc_sbuf_tensor("x_fmb", [128, NHC, T], BF16)
    x4t = nc.alloc_sbuf_tensor("x4t", [128, 1, D], F32)
    xtmb = nc.alloc_sbuf_tensor("xtmb", [128, 1, D], BF16)
    yT_sb = nc.alloc_sbuf_tensor("yT_sb", [128, 65], F32)
    z1 = nc.alloc_sbuf_tensor("z1", [128, 1], F32)
    ybar = nc.alloc_sbuf_tensor("ybar", [128, HD], F32)
    st = st_alloc(nc)
    qkm_t = nc.alloc_sbuf_tensor("qkm_t", [1, 16], F32)
    csc = nc.alloc_sbuf_tensor("cs", [1, 4], F32)
    cb = nc.alloc_sbuf_tensor("cb", [128, 1], F32)
    ncb = nc.alloc_sbuf_tensor("ncb", [128, 1], F32)
    inv32 = nc.alloc_sbuf_tensor("inv32", [128, 1], F32)
    onesf = nc.alloc_sbuf_tensor("onesf", [1, 128], F32)
    eyef = nc.alloc_sbuf_tensor("eyef_t", [128, 128], F32)
    s1_t = nc.alloc_sbuf_tensor("s1_t", [128, 32], F32)
    s2_t = nc.alloc_sbuf_tensor("s2_t", [128, 32], F32)
    h1s_t = nc.alloc_sbuf_tensor("h1s_t", [128, 32], F32)
    h2s_t = nc.alloc_sbuf_tensor("h2s_t", [128, 32], F32)
    s3r = nc.alloc_sbuf_tensor("s3r", [1, D], F32)
    s3_bb = nc.alloc_sbuf_tensor("s3_bb", [128, D], F32)
    hw_b = nc.alloc_sbuf_tensor("hw_b", [128, 2, 32, 128], BF16)
    w1_b = hw_b[:, :, 0:NHC, :]
    w2_b = hw_b[:, :, NHC:2 * NHC, :]
    w3_b = hw_b[:].rearrange("p s c f -> p s (c f)")[:, :, 0:1024]
    silu_s = nc.alloc_sbuf_tensor("silu_s", [128, 1024], F32)
    t2_s = nc.alloc_sbuf_tensor("t2_s", [128, 1024], F32)
    o_sb = silu_s[:]

    psSA = nc.alloc_psum_tensor("psSA", [128, 1024], F32)
    psSB = nc.alloc_psum_tensor("psSB", [128, 1024], F32)
    psY = nc.alloc_psum_tensor("psY", [128, 1024], F32)
    psC = nc.alloc_psum_tensor("psC", [128, 512], F32)
    psT = nc.alloc_psum_tensor("psT", [128, 512], F32)

    p = Prog(nc)
    p.sems("sy", "gp", "pe", "act", "dve")

    with nc.Block() as b:
        boot_clear(b, nc, p)
        def sy0(s):
            p.selfw(s, "sy")
            p.inc(s.dma_start(k_sb[:, :], kfull[:]), "sy", 16)
            p.selfw(s, "sy")
            p.inc(s.dma_start(qf_b[:, :], qfd[:]), "sy", 16)
            p.selfw(s, "sy")
            p.inc(s.dma_start(qkm_t[:], qkmd[:]), "sy", 16)
            p.selfw(s, "sy")
            p.inc(s.dma_start(eyef[:], eyef_d[:]), "sy", 16)
            p.selfw(s, "sy")
            p.inc(s.dma_start(s1_t[:], s1d[:]), "sy", 16)
            p.selfw(s, "sy")
            p.inc(s.dma_start(s2_t[:], s2d[:]), "sy", 16)
            p.selfw(s, "sy")
            p.inc(s.dma_start(h1s_t[:], h1sd[:]), "sy", 16)
            p.selfw(s, "sy")
            p.inc(s.dma_start(h2s_t[:], h2sd[:]), "sy", 16)
            p.selfw(s, "sy")
            p.inc(s.dma_start(s3r[:], s3d[:]), "sy", 16)
        b.sync(sy0)
        def dv0(v):
            p.inc(v.memset(onesf[:], 1.0), "dve")
            p.inc(v.memset(v65[:, :, 64:65], 1.0), "dve")
            p.inc(v.memset(inv32[:], 1.0 / 32.0), "dve")
        b.vector(dv0)
        st_init(p, b, st)
        def sy1(s):
            p.wait(s, "dve")
            p.selfw(s, "sy")
            p.inc(s.dma_start(v65[:, :, 0:64],
                              vfull[:].rearrange("(c p) d -> p c d", p=128)), "sy", 16)
        b.sync(sy1)
        def dv_c(v):
            p.wait(v, "sy")
            p.inc(v.tensor_reduce(csc[0:1, 0:2],
                                  qkm_t[:].rearrange("o (a c) -> o a c", a=2),
                                  AX.X, ALU.max), "dve")
            p.selfw(v, "dve")
            p.inc(v.tensor_add(csc[0:1, 2:3], csc[0:1, 0:1], csc[0:1, 1:2]), "dve")
            p.selfw(v, "dve")
            p.inc(v.tensor_single_scalar(csc[0:1, 3:4], csc[0:1, 2:3], 1.0 / 64.0,
                                         ALU.mult), "dve")
        b.vector(dv_c)
        def pe_c(t):
            p.wait(t, "dve")
            p.inc(t.matmul(psC[:, 0:1], onesf[0:1, :], csc[0:1, 3:4],
                           start=True, stop=True), "pe")
        b.tensor(pe_c)
        def dv_c2(v):
            p.wait(v, "pe")
            p.inc(v.tensor_copy(cb[:], psC[:, 0:1]), "dve")
            p.selfw(v, "dve")
            p.inc(v.tensor_single_scalar(ncb[:], cb[:], -1.0, ALU.mult), "dve")
        b.vector(dv_c2)

        for th in range(2):
            toff = th * 1024
            act_b = p.c["act"]
            spe, vpe = [], []
            def pe_at(t, toff=toff, act_b=act_b, spe=spe, vpe=vpe):
                p.wait(t, "sy")
                p.wait(t, "dve")
                for kc in range(NK):
                    bank = psSA if kc % 2 == 0 else psSB
                    if kc >= 2:
                        t.wait_ge(p.s["act"], act_b + (kc - 2) + 1)
                    for nn in range(2):
                        mm = t.matmul(bank[:, nn * 512:(nn + 1) * 512],
                                      k_sb[:, kc * 128:(kc + 1) * 128],
                                      qf_b[:, toff + nn * 512:toff + (nn + 1) * 512],
                                      start=True, stop=True)
                    spe.append(p.inc(mm, "pe"))
                    if kc >= 1:
                        pk = kc - 1
                        t.wait_ge(p.s["act"], act_b + pk + 1)
                        for nn in range(2):
                            mm = t.matmul(psY[0:65, nn * 512:(nn + 1) * 512],
                                          v65[:, pk, :],
                                          pt_a[:, pk % 2, nn * 512:(nn + 1) * 512],
                                          start=(pk == 0), stop=False)
                        vpe.append(p.inc(mm, "pe"))
                t.wait_ge(p.s["act"], act_b + NK)
                for nn in range(2):
                    mm = t.matmul(psY[0:65, nn * 512:(nn + 1) * 512], v65[:, NK - 1, :],
                                  pt_a[:, (NK - 1) % 2, nn * 512:(nn + 1) * 512],
                                  start=False, stop=True)
                vpe.append(p.inc(mm, "pe"))
            b.tensor(pe_at)
            def act_at(a, spe=spe, vpe=vpe):
                for kc in range(NK):
                    p.wait(a, "pe", spe[kc])
                    if kc >= 2:
                        p.wait(a, "pe", vpe[kc - 2])
                    bank = psSA if kc % 2 == 0 else psSB
                    p.selfw(a, "act")
                    p.inc(a.activation(pt_a[:, kc % 2, :], bank[:], AF.Exp,
                                       bias=ncb[:], scale=inv32[:]), "act")
            b.scalar(act_at)
            def dv_y(v):
                p.wait(v, "pe")
                p.inc(v.tensor_copy(y_sb[0:65, :], psY[0:65, :]), "dve")
            b.vector(dv_y)

            for tj in range(8):
                tc = th * 8 + tj
                sl = 0
                def sy_ld(s, tc=tc, sl=sl):
                    p.wait(s, "dve")
                    p.selfw(s, "sy")
                    p.inc(s.dma_start(x3t[:, sl, :], x3[tc * 128:(tc + 1) * 128, :]),
                          "sy", 16)
                b.sync(sy_ld)
                def pe_t(t, tj=tj):
                    p.wait(t, "dve")
                    p.inc(t.transpose(psT[:, 0:65],
                                      y_sb[0:65, tj * 128:(tj + 1) * 128],
                                      eyef[0:65, 0:65]), "pe")
                b.tensor(pe_t)
                def dv_t(v, tc=tc, sl=sl):
                    p.wait(v, "pe")
                    p.wait(v, "sy")
                    p.selfw(v, "dve")
                    p.inc(v.tensor_copy(yT_sb[:], psT[:, 0:65]), "dve")
                    p.selfw(v, "dve")
                    p.inc(v.reciprocal(z1[:], yT_sb[:, 64:65]), "dve")
                    p.selfw(v, "dve")
                    p.inc(v.tensor_scalar(ybar[:], yT_sb[:, 0:64], z1[:], 0.0,
                                          ALU.mult, ALU.add), "dve")
                    p.selfw(v, "dve")
                    p.inc(v.tensor_tensor(
                        x4t[:, sl, :].rearrange("p (r d) -> p r d", r=16),
                        x3t[:, sl, :].rearrange("p (r d) -> p r d", r=16),
                        ybar[:].unsqueeze(1).broadcast_to([128, 16, HD]), ALU.add),
                        "dve")
                b.vector(dv_t)
                ln_tile(p, b, x4t[:, sl, :], st, tc % NTC)
                def dv_cp(v, sl=sl):
                    p.wait(v, "sy")
                    p.selfw(v, "dve")
                    p.inc(v.tensor_copy(xtmb[:, sl, :], x4t[:, sl, :]), "dve")
                b.vector(dv_cp)
                def sy_st(s, tc=tc, sl=sl):
                    p.wait(s, "dve")
                    p.selfw(s, "sy")
                    p.inc(s.dma_start(x4_d[tc * 128:(tc + 1) * 128, :], x4t[:, sl, :]),
                          "sy", 16)
                    for hc in range(NHC):
                        p.selfw(s, "sy")
                        p.inc(s.dma_start(x_fmb[:, hc, tc * 128:(tc + 1) * 128],
                                          xtmb[:, sl, hc * 128:(hc + 1) * 128],
                                          transpose=True), "sy", 16)
                b.sync(sy_st)

        def pe_s3(t):
            p.wait(t, "dve")
            p.inc(t.matmul(psC[:, :], onesf[0:1, :], s3r[0:1, 0:512],
                           start=True, stop=True), "pe")
            p.inc(t.matmul(psT[:, :], onesf[0:1, :], s3r[0:1, 512:1024],
                           start=True, stop=True), "pe")
        b.tensor(pe_s3)
        def dv_s3(v):
            p.wait(v, "pe")
            p.selfw(v, "dve")
            p.inc(v.tensor_copy(s3_bb[:, 0:512], psC[:]), "dve")
            p.inc(v.tensor_copy(s3_bb[:, 512:1024], psT[:]), "dve")
        b.vector(dv_s3)

        for th in range(2):
            toff = th * 1024

            pe_b = p.c["pe"]
            act_b = p.c["act"]
            dve_b = p.c["dve"]
            gp_b = p.c["gp"]
            def gp_w1(g, pe_b=pe_b):
                for fc in range(32):
                    if fc < 2:
                        if pe_b > 0:
                            g.wait_ge(p.s["pe"], pe_b)
                    else:
                        g.wait_ge(p.s["pe"], pe_b + (fc - 2) + 1)
                    p.selfw(g, "gp")
                    p.inc(g.dma_start(w1_b[:, fc % 2, :, :],
                                      w1T[:, fc * 128:(fc + 1) * 128].rearrange(
                                          "(c p) f -> p c f", p=128)), "gp", 16)
                    p.selfw(g, "gp")
                    p.inc(g.dma_start(w2_b[:, fc % 2, :, :],
                                      w2T[:, fc * 128:(fc + 1) * 128].rearrange(
                                          "(c p) f -> p c f", p=128)), "gp", 16)
            b.gpsimd(gp_w1)
            def pe_s1(t, toff=toff, gp_b=gp_b, act_b=act_b):
                p.wait(t, "sy")
                p.wait(t, "dve")
                for fc in range(32):
                    t.wait_ge(p.s["gp"], gp_b + (fc + 1) * 32)
                    if fc >= 1:
                        t.wait_ge(p.s["act"], act_b + 2 * (fc - 1) + 2)
                    for hc in range(NHC):
                        for nn in range(2):
                            mm = t.matmul(psSA[:, nn * 512:(nn + 1) * 512],
                                          w1_b[:, fc % 2, hc, :],
                                          x_fmb[:, hc, toff + nn * 512:toff + (nn + 1) * 512],
                                          start=(hc == 0), stop=(hc == NHC - 1))
                    for hc in range(NHC):
                        for nn in range(2):
                            mm = t.matmul(psSB[:, nn * 512:(nn + 1) * 512],
                                          w2_b[:, fc % 2, hc, :],
                                          x_fmb[:, hc, toff + nn * 512:toff + (nn + 1) * 512],
                                          start=(hc == 0), stop=(hc == NHC - 1))
                    p.inc(mm, "pe")
            b.tensor(pe_s1)
            s1pe_list = list(range(pe_b + 1, pe_b + 33))
            def act_s1(a, dve_b=dve_b, s1pe_list=s1pe_list):
                for fc in range(32):
                    p.wait(a, "pe", s1pe_list[fc])
                    if fc >= 1:
                        a.wait_ge(p.s["dve"], dve_b + fc)
                    p.selfw(a, "act")
                    p.inc(a.activation(silu_s[:], psSA[:], AF.Silu,
                                       bias=0.0, scale=s1_t[:, fc:fc + 1]), "act")
                    p.inc(a.activation(t2_s[:], psSB[:], AF.Copy,
                                       bias=0.0, scale=s2_t[:, fc:fc + 1]), "act")
            b.scalar(act_s1)
            def dv_s1(v, act_b=act_b):
                for fc in range(32):
                    v.wait_ge(p.s["act"], act_b + 2 * fc + 2)
                    p.selfw(v, "dve")
                    p.inc(v.tensor_mul(h_a[:, fc, :], silu_s[:], t2_s[:]), "dve")
            b.vector(dv_s1)

            for wT, ssc, src, dst in ((h1T, h1s_t, h_a, h_b), (h2T, h2s_t, h_b, h_a)):
                pe_b = p.c["pe"]
                act_b = p.c["act"]
                gp_b = p.c["gp"]
                def gp_wh(g, wT=wT, pe_b=pe_b):
                    for fc in range(32):
                        if fc < 2:
                            g.wait_ge(p.s["pe"], pe_b)
                        else:
                            g.wait_ge(p.s["pe"], pe_b + (fc - 2) + 1)
                        p.selfw(g, "gp")
                        p.inc(g.dma_start(hw_b[:, fc % 2, :, :],
                                          wT[:, fc * 128:(fc + 1) * 128].rearrange(
                                              "(c p) f -> p c f", p=128)), "gp", 16)
                b.gpsimd(gp_wh)
                def pe_h(t, src=src, gp_b=gp_b, act_b=act_b):
                    p.wait(t, "dve")
                    p.wait(t, "act")
                    for fc in range(32):
                        t.wait_ge(p.s["gp"], gp_b + (fc + 1) * 16)
                        if fc >= 1:
                            t.wait_ge(p.s["act"], act_b + fc)
                        for kc in range(32):
                            for nn in range(2):
                                mm = t.matmul(psSA[:, nn * 512:(nn + 1) * 512],
                                              hw_b[:, fc % 2, kc, :],
                                              src[:, kc, nn * 512:(nn + 1) * 512],
                                              start=(kc == 0), stop=(kc == 31))
                        p.inc(mm, "pe")
                b.tensor(pe_h)
                hpe_list = list(range(pe_b + 1, pe_b + 33))
                def act_h(a, ssc=ssc, dst=dst, hpe_list=hpe_list):
                    for fc in range(32):
                        p.wait(a, "pe", hpe_list[fc])
                        p.selfw(a, "act")
                        p.inc(a.activation(dst[:, fc, :], psSA[:], AF.Gelu,
                                           bias=0.0, scale=ssc[:, fc:fc + 1]), "act")
                b.scalar(act_h)

            for tg in range(2):
                banks = [(psSA[:, 0:512], psSA[:, 512:1024]),
                         (psSB[:, 0:512], psSB[:, 512:1024]),
                         (psY[:, 0:512], psY[:, 512:1024]),
                         (psC[:, :], psT[:, :])]
                pe_b = p.c["pe"]
                gp_b = p.c["gp"]
                def gp_w3(g, tg=tg, pe_b=pe_b):
                    for kc in range(32):
                        if kc < 2:
                            g.wait_ge(p.s["pe"], pe_b)
                        else:
                            g.wait_ge(p.s["pe"], pe_b + (kc - 2) + 1)
                        p.selfw(g, "gp")
                        p.inc(g.dma_start(w3_b[:, kc % 2, :],
                                          w3T[kc * 128:(kc + 1) * 128, :]), "gp", 16)
                b.gpsimd(gp_w3)
                def pe_w3(t, tg=tg, gp_b=gp_b, banks=banks):
                    p.wait(t, "act")
                    p.wait(t, "dve")
                    for kc in range(32):
                        t.wait_ge(p.s["gp"], gp_b + (kc + 1) * 16)
                        for tcc in range(4):
                            lh = h_a[:, kc,
                                     tg * 512 + tcc * 128:tg * 512 + (tcc + 1) * 128]
                            for nn in range(2):
                                mm = t.matmul(banks[tcc][nn], lh,
                                              w3_b[:, kc % 2, nn * 512:(nn + 1) * 512],
                                              start=(kc == 0), stop=(kc == 31))
                        p.inc(mm, "pe")
                b.tensor(pe_w3)
                for tcc in range(4):
                    tc = th * 8 + tg * 4 + tcc
                    sl = 0
                    def sy_l4(s, tc=tc, sl=sl):
                        p.wait(s, "dve")
                        p.selfw(s, "sy")
                        p.inc(s.dma_start(x4t[:, sl, :],
                                          x4_d[tc * 128:(tc + 1) * 128, :]), "sy", 16)
                    b.sync(sy_l4)
                    def dv_o(v, tcc=tcc, sl=sl, banks=banks):
                        p.wait(v, "pe")
                        p.wait(v, "sy")
                        p.selfw(v, "dve")
                        p.inc(v.tensor_copy(o_sb[:, 0:512], banks[tcc][0]), "dve")
                        p.inc(v.tensor_copy(o_sb[:, 512:1024], banks[tcc][1]), "dve")
                        p.selfw(v, "dve")
                        p.inc(v.tensor_mul(o_sb[:], o_sb[:], s3_bb[:]), "dve")
                        p.selfw(v, "dve")
                        p.inc(v.tensor_add(x4t[:, sl, :], x4t[:, sl, :], o_sb[:]), "dve")
                    b.vector(dv_o)
                    ln_tile(p, b, x4t[:, sl, :], st, tc % NTC)
                    def sy_o(s, tc=tc, sl=sl):
                        p.wait(s, "dve")
                        p.selfw(s, "sy")
                        p.inc(s.dma_start(out_o[tc * 128:(tc + 1) * 128, :],
                                          x4t[:, sl, :]), "sy", 16)
                    b.sync(sy_o)

        def sy_fin(s):
            p.wait(s, "sy")
        b.sync(sy_fin)
    return nc



_bf = ml_dtypes.bfloat16


def sel8_host():
    s = np.zeros((NEXP, NEXP * 128), dtype=_bf)
    for e in range(NEXP):
        s[e, e * 128:(e + 1) * 128] = 1.0
    return s


def kernel(L0_attn_x, L0_attn_q_proj, L0_attn_k_proj, L0_ffn_gate_w, L0_ffn_expert_w,
           L1_attn_Wq, L1_attn_bq, L1_attn_Wk, L1_attn_bk, L1_attn_Wv, L1_attn_bv,
           L1_ffn_w1_w, L1_ffn_w1_s, L1_ffn_w2_w, L1_ffn_w2_s,
           L1_ffn_h1_w, L1_ffn_h1_s, L1_ffn_h2_w, L1_ffn_h2_s,
           L1_ffn_w3_w, L1_ffn_w3_s):
    f32 = np.float32
    ct = np.ascontiguousarray
    x = np.asarray(L0_attn_x, f32)
    eyeb = np.eye(128, dtype=_bf)
    eyef = np.eye(128, dtype=f32)

    common_A = {
        "qp": ct(np.asarray(L0_attn_q_proj, f32)),
        "kp": ct(np.asarray(L0_attn_k_proj, f32)),
        "gT": ct(np.asarray(L0_ffn_gate_w, f32).T),
        "ewT": ct(np.asarray(L0_ffn_expert_w, f32).transpose(0, 2, 1)),
        "wqT": ct(np.asarray(L1_attn_Wq, f32).T),
        "wkT": ct(np.asarray(L1_attn_Wk, f32).T),
        "wvT": ct(np.asarray(L1_attn_Wv, f32).T),
        "bq": ct(np.asarray(L1_attn_bq, f32).reshape(HD, 1)),
        "bk": ct(np.asarray(L1_attn_bk, f32).reshape(HD, 1)),
        "bv": ct(np.asarray(L1_attn_bv, f32).reshape(HD, 1)),
        "eyeb": eyeb, "eyef": eyef, "sel8": sel8_host(),
    }
    in_maps_A = []
    for c in range(NCORE):
        m = dict(common_A)
        m["xa"] = ct(x[c])
        m["xaT"] = ct(x[c].T)
        in_maps_A.append(m)

    ncA = build_A()
    resA = run_bass_kernel_spmd(ncA, in_maps_A, core_ids=list(range(NCORE)))
    ra = resA.results

    kfull = ct(np.concatenate([np.asarray(ra[c]["k_o"]) for c in range(NCORE)], axis=1))
    vfull = ct(np.concatenate([np.asarray(ra[c]["v_o"]) for c in range(NCORE)], axis=0))
    qkm = ct(np.concatenate([np.asarray(ra[c]["qkm_o"], f32) for c in range(NCORE)],
                            axis=0).T.reshape(1, 16))

    sc = lambda s: ct(np.asarray(s, f32).reshape(-1, 128).T)
    common_B = {
        "kfull": kfull, "vfull": vfull, "qkm": qkm,
        "w1T": ct(np.asarray(L1_ffn_w1_w, f32).T),
        "w2T": ct(np.asarray(L1_ffn_w2_w, f32).T),
        "h1T": ct(np.asarray(L1_ffn_h1_w, f32).T),
        "h2T": ct(np.asarray(L1_ffn_h2_w, f32).T),
        "w3T": ct(np.asarray(L1_ffn_w3_w, f32).T),
        "s1d": sc(L1_ffn_w1_s), "s2d": sc(L1_ffn_w2_s),
        "h1sd": sc(L1_ffn_h1_s), "h2sd": sc(L1_ffn_h2_s),
        "s3d": ct(np.asarray(L1_ffn_w3_s, f32).reshape(1, D)),
        "eyef": eyef,
    }
    in_maps_B = []
    for c in range(NCORE):
        m = dict(common_B)
        m["x3"] = np.asarray(ra[c]["x3_o"], f32)
        m["qf"] = np.asarray(ra[c]["q_o"])
        in_maps_B.append(m)

    ncB = build_B()
    resB = run_bass_kernel_spmd(ncB, in_maps_B, core_ids=list(range(NCORE)))
    out = np.stack([np.asarray(resB.results[c]["out_o"], f32) for c in range(NCORE)],
                   axis=0)
    return out


# version 02
# speedup vs baseline: 1.0116x; 1.0116x over previous
import numpy as np
import ml_dtypes
import concourse.bass as bass
import concourse.mybir as mybir
from concourse.bass_utils import run_bass_kernel_spmd

F32 = mybir.dt.float32
BF16 = mybir.dt.bfloat16
AF = mybir.ActivationFunctionType
ALU = mybir.AluOpType
AX = mybir.AxisListType

NCORE = 8
T = 2048
D = 1024
E = 256
NEXP = 8
HD = 64
DH = 4096
NTC = T // 128
NHC = D // 128
LN_EPS = 1e-5


class Prog:

    def __init__(self, nc):
        self.nc = nc
        self.s = {}
        self.c = {}

    def sems(self, *names):
        for n in names:
            self.s[n] = self.nc.alloc_semaphore(n)
            self.c[n] = 0

    def inc(self, instr, name, k=1):
        instr.then_inc(self.s[name], k)
        self.c[name] += k
        return self.c[name]

    def wait(self, eng, name, val=None):
        v = self.c[name] if val is None else val
        if v > 0:
            eng.wait_ge(self.s[name], v)

    def selfw(self, eng, name):
        if self.c[name] > 0:
            eng.wait_ge(self.s[name], self.c[name])


import os
_LN_MODE = os.environ.get("LN_MODE", "ts2")
_LN_DELAY = bool(int(os.environ.get("LN_DELAY", "0")))


def ln_stats_batch(p, b, x_tm, st, ntc):
    def act(a):
        p.wait(a, "dve")
        for tc in range(ntc):
            p.selfw(a, "act")
            p.inc(a.activation(st["dump"][:], x_tm[:, tc, :], AF.Square,
                               bias=0.0, scale=1.0,
                               accum_out=st["sq"][:, tc:tc + 1]), "act")
    b.scalar(act)
    def dve(v):
        p.selfw(v, "dve")
        for tc in range(ntc):
            p.inc(v.reduce_sum(st["sum"][:, tc:tc + 1], x_tm[:, tc, :], AX.X), "dve")
        p.wait(v, "act")
        s = slice(0, ntc)
        p.selfw(v, "dve")
        p.inc(v.tensor_scalar(st["m"][:, s], st["sum"][:, s], 1.0 / D, 0.0,
                              ALU.mult, ALU.add), "dve")
        p.inc(v.tensor_scalar(st["ex2"][:, s], st["sq"][:, s], 1.0 / D, 0.0,
                              ALU.mult, ALU.add), "dve")
        p.selfw(v, "dve")
        p.inc(v.tensor_mul(st["msq"][:, s], st["m"][:, s], st["m"][:, s]), "dve")
        p.selfw(v, "dve")
        p.inc(v.tensor_sub(st["var"][:, s], st["ex2"][:, s], st["msq"][:, s]), "dve")
    b.vector(dve)
    def act2(a):
        p.wait(a, "dve")
        s = slice(0, ntc)
        p.inc(a.activation(st["lnv"][:, s], st["var"][:, s], AF.Ln,
                           bias=st["eps"][:], scale=1.0), "act")
        p.selfw(a, "act")
        p.inc(a.activation(st["rstd"][:, s], st["lnv"][:, s], AF.Exp,
                           bias=0.0, scale=st["mhalf"][:]), "act")
    b.scalar(act2)
    def dve2(v):
        p.wait(v, "act")
        s = slice(0, ntc)
        p.inc(v.tensor_mul(st["mh"][:, s], st["m"][:, s], st["rstd"][:, s]), "dve")
        p.selfw(v, "dve")
        p.inc(v.tensor_scalar(st["nmh"][:, s], st["mh"][:, s], -1.0, 0.0,
                              ALU.mult, ALU.add), "dve")
    b.vector(dve2)


def ln_apply_and_fm(p, b, x_tm, x_tmb, x_fmb, st, ntc):
    def dve(v):
        for tc in range(ntc):
            p.selfw(v, "dve")
            p.inc(v.tensor_scalar(x_tm[:, tc, :], x_tm[:, tc, :],
                                  st["rstd"][:, tc:tc + 1], st["nmh"][:, tc:tc + 1],
                                  ALU.mult, ALU.add), "dve")
            p.selfw(v, "dve")
            p.inc(v.tensor_copy(x_tmb[:, tc, :], x_tm[:, tc, :]), "dve")
    b.vector(dve)
    def sy(s):
        p.wait(s, "dve")
        for tc in range(ntc):
            for hc in range(NHC):
                p.selfw(s, "sy")
                p.inc(s.dma_start(
                    x_fmb[:, hc, tc * 128:(tc + 1) * 128],
                    x_tmb[:, tc, hc * 128:(hc + 1) * 128], transpose=True), "sy", 16)
    b.sync(sy)


def ln_tile(p, b, x_ap, st, slot):
    sl = slice(slot, slot + 1)
    def act(a):
        p.wait(a, "dve")
        p.inc(a.activation(st["dump"][:], x_ap, AF.Square, bias=0.0, scale=1.0,
                           accum_out=st["sq"][:, sl]), "act")
    b.scalar(act)
    def dve(v):
        p.selfw(v, "dve")
        p.inc(v.reduce_sum(st["sum"][:, sl], x_ap, AX.X), "dve")
        p.wait(v, "act")
        p.selfw(v, "dve")
        p.inc(v.tensor_scalar(st["m"][:, sl], st["sum"][:, sl], 1.0 / D, 0.0,
                              ALU.mult, ALU.add), "dve")
        p.inc(v.tensor_scalar(st["ex2"][:, sl], st["sq"][:, sl], 1.0 / D, 0.0,
                              ALU.mult, ALU.add), "dve")
        p.selfw(v, "dve")
        p.inc(v.tensor_mul(st["msq"][:, sl], st["m"][:, sl], st["m"][:, sl]), "dve")
        p.selfw(v, "dve")
        p.inc(v.tensor_sub(st["var"][:, sl], st["ex2"][:, sl], st["msq"][:, sl]), "dve")
    b.vector(dve)
    def act2(a):
        p.wait(a, "dve")
        p.inc(a.activation(st["lnv"][:, sl], st["var"][:, sl], AF.Ln,
                           bias=st["eps"][:], scale=1.0), "act")
        p.selfw(a, "act")
        p.inc(a.activation(st["rstd"][:, sl], st["lnv"][:, sl], AF.Exp,
                           bias=0.0, scale=st["mhalf"][:]), "act")
    b.scalar(act2)
    def dve2(v):
        p.wait(v, "act")
        p.inc(v.tensor_mul(st["mh"][:, sl], st["m"][:, sl], st["rstd"][:, sl]), "dve")
        p.selfw(v, "dve")
        p.inc(v.tensor_scalar(st["nmh"][:, sl], st["mh"][:, sl], -1.0, 0.0,
                              ALU.mult, ALU.add), "dve")
        p.selfw(v, "dve")
        p.inc(v.tensor_scalar(x_ap, x_ap, st["rstd"][:, sl], st["nmh"][:, sl],
                              ALU.mult, ALU.add), "dve")
    b.vector(dve2)


_BOOT = bool(int(os.environ.get("KBOOT", "1")))


def boot_clear(b, nc, p):
    if not _BOOT:
        return
    """Clear OUR kernel semaphores at program start (crashed programs leave
    stale values; a stale sem makes every wait_ge pass early). gpsimd clears;
    the other engines sit in a timed nop longer than the clear, then check a
    boot sem that the clear zeroed before it was incremented. Bass's own
    barrier sems are left alone (their protocol is self-normalizing)."""
    bs = nc.alloc_semaphore("bootsem")
    lo = min(h.num for h in list(p.s.values()) + [bs])
    rng_ = range(lo, nc._kernel_sem_range.stop)
    def gp(g):
        g.sem_clear(rng_)
        g.sem_inc(bs, 1)
    b.gpsimd(gp)
    for name in ("tensor", "vector", "scalar", "sync"):
        def w(e):
            e.nop(cycle_cnt=20000)
            e.wait_ge(bs, 1)
        getattr(b, name)(w)


def st_alloc(nc):
    st = {n: nc.alloc_sbuf_tensor("st_" + n, [128, NTC], F32)
          for n in ("sum", "sq", "m", "ex2", "msq", "var", "lnv", "rstd", "mh", "nmh")}
    st["dump"] = nc.alloc_sbuf_tensor("st_dump", [128, D], BF16)
    st["eps"] = nc.alloc_sbuf_tensor("st_eps", [128, 1], F32)
    st["mhalf"] = nc.alloc_sbuf_tensor("st_mhalf", [128, 1], F32)
    return st


def st_init(p, b, st):
    def dv(v):
        p.inc(v.memset(st["eps"][:], LN_EPS), "dve")
        p.inc(v.memset(st["mhalf"][:], -0.5), "dve")
    b.vector(dv)



class _Stop(Exception):
    pass


def _chk(upto, n):
    if upto < n:
        raise _Stop


def build_A(upto=99):
    nc = bass.Bass("TRN2", target_bir_lowering=False)
    xa = nc.dram_tensor("xa", [T, D], F32, kind="ExternalInput")
    xaT = nc.dram_tensor("xaT", [D, T], F32, kind="ExternalInput")
    qp = nc.dram_tensor("qp", [D, E], F32, kind="ExternalInput")
    kp = nc.dram_tensor("kp", [D, E], F32, kind="ExternalInput")
    gTd = nc.dram_tensor("gT", [D, NEXP], F32, kind="ExternalInput")
    ewT = nc.dram_tensor("ewT", [NEXP, D, D], F32, kind="ExternalInput")
    wqT = nc.dram_tensor("wqT", [D, HD], F32, kind="ExternalInput")
    wkT = nc.dram_tensor("wkT", [D, HD], F32, kind="ExternalInput")
    wvT = nc.dram_tensor("wvT", [D, HD], F32, kind="ExternalInput")
    bqd = nc.dram_tensor("bq", [HD, 1], F32, kind="ExternalInput")
    bkd = nc.dram_tensor("bk", [HD, 1], F32, kind="ExternalInput")
    bvd = nc.dram_tensor("bv", [HD, 1], F32, kind="ExternalInput")
    eyeb_d = nc.dram_tensor("eyeb", [128, 128], BF16, kind="ExternalInput")
    eyef_d = nc.dram_tensor("eyef", [128, 128], F32, kind="ExternalInput")
    sel8_d = nc.dram_tensor("sel8", [NEXP, NEXP * 128], BF16, kind="ExternalInput")

    x3_o = nc.dram_tensor("x3_o", [T, D], F32, kind="ExternalOutput")
    q_o = nc.dram_tensor("q_o", [HD, T], BF16, kind="ExternalOutput")
    k_o = nc.dram_tensor("k_o", [HD, T], BF16, kind="ExternalOutput")
    v_o = nc.dram_tensor("v_o", [T, HD], BF16, kind="ExternalOutput")
    qkm_o = nc.dram_tensor("qkm_o", [1, 2], F32, kind="ExternalOutput")

    x_tm = nc.alloc_sbuf_tensor("x_tm", [128, NTC, D], F32)
    x_tmb = nc.alloc_sbuf_tensor("x_tmb", [128, NTC, D], BF16)
    x_fmb = nc.alloc_sbuf_tensor("x_fmb", [128, NHC, T], BF16)
    qpb = nc.alloc_sbuf_tensor("qpb", [128, NHC, E], BF16)
    kpb = nc.alloc_sbuf_tensor("kpb", [128, NHC, E], BF16)
    wq_b = qpb[:, :, 0:HD]
    wk_b = qpb[:, :, HD:2 * HD]
    wv_b = qpb[:, :, 2 * HD:3 * HD]
    pool_a = nc.alloc_sbuf_tensor("pool_a", [128, 4096], BF16)
    qe = pool_a[:].rearrange("p (a t) -> p a t", a=2)
    mb_all = pool_a[:].rearrange("p (e t) -> p e t", e=NEXP)
    pool_b = nc.alloc_sbuf_tensor("pool_b", [128, 8192], BF16)
    ke = pool_b[:, 0:4096].rearrange("p (a e) -> p a e", a=NTC)
    moe_fm2 = pool_b[:].rearrange("p (s f t) -> p s f t", s=2, f=NEXP)
    kv_b = nc.alloc_sbuf_tensor("kv_b", [128, 2, D], BF16)
    zrow = nc.alloc_sbuf_tensor("zrow", [1, T], F32)
    z_tm = nc.alloc_sbuf_tensor("z_tm", [128, NTC], F32)
    ones_b = nc.alloc_sbuf_tensor("ones_b", [128, 1], BF16)
    onesr_b = nc.alloc_sbuf_tensor("onesr_b", [1, 128], BF16)
    eyeb = nc.alloc_sbuf_tensor("eyeb_t", [128, 128], BF16)
    eyef = nc.alloc_sbuf_tensor("eyef_t", [128, 128], F32)
    st = st_alloc(nc)
    gTb = nc.alloc_sbuf_tensor("gTb", [128, NHC, NEXP], BF16)
    scor = nc.alloc_sbuf_tensor("scor", [128, NTC, NEXP], F32)
    gmax = nc.alloc_sbuf_tensor("gmax", [128, NTC], F32)
    pre = nc.alloc_sbuf_tensor("pre", [128, NTC], F32)
    tta = nc.alloc_sbuf_tensor("tta", [128, NTC], F32)
    ttb = nc.alloc_sbuf_tensor("ttb", [128, NTC], F32)
    masks_tm = nc.alloc_sbuf_tensor("masks_tm", [128, NTC, NEXP], BF16)
    shr_a = nc.alloc_sbuf_tensor("shr_a", [HD, T], BF16)
    masks_f = shr_a[0:NEXP, :]
    mtile = nc.alloc_sbuf_tensor("mtile", [128, 512], BF16)
    w_moe = nc.alloc_sbuf_tensor("w_moe", [128, 2, 1024], BF16)
    moe_tm2 = nc.alloc_sbuf_tensor("moe_tm2", [128, 2, D], BF16)
    bq_t = nc.alloc_sbuf_tensor("bq_t", [HD, 1], F32)
    bk_t = nc.alloc_sbuf_tensor("bk_t", [HD, 1], F32)
    bv_t = nc.alloc_sbuf_tensor("bv_t", [HD, 1], F32)
    qf = nc.alloc_sbuf_tensor("qf", [HD, T], BF16)
    kf = nc.alloc_sbuf_tensor("kf", [HD, T], BF16)
    vf = nc.alloc_sbuf_tensor("vf", [HD, T], BF16)
    sqn = shr_a[:, :]
    v_tmb = nc.alloc_sbuf_tensor("v_tmb", [128, NTC, HD], BF16)
    qkm_t = nc.alloc_sbuf_tensor("qkm_t", [1, 2], F32)
    tmp_res = nc.alloc_sbuf_tensor("tmp_res", [128, 512], F32)
    sel8 = tmp_res[:].bitcast(BF16)[0:NEXP, :]

    PB = [nc.alloc_psum_tensor(f"pb{i}", [128, 512], F32) for i in range(8)]
    PBb = [t[:].bitcast(BF16) for t in PB]

    p = Prog(nc)
    p.sems("sy", "gp", "pe", "act", "dve")

    with nc.Block() as b:
      try:
        boot_clear(b, nc, p)
        def sy0(s):
            p.selfw(s, "sy")
            p.inc(s.dma_start(x_tm[:], xa[:].rearrange("(c p) h -> p c h", p=128)), "sy", 16)
            p.selfw(s, "sy")
            p.inc(s.dma_start(eyeb[:], eyeb_d[:]), "sy", 16)
            p.selfw(s, "sy")
            p.inc(s.dma_start(eyef[:], eyef_d[:]), "sy", 16)
            p.selfw(s, "sy")
            p.inc(s.dma_start(bq_t[:], bqd[:]), "sy", 16)
            p.selfw(s, "sy")
            p.inc(s.dma_start(bk_t[:], bkd[:]), "sy", 16)
            p.selfw(s, "sy")
            p.inc(s.dma_start(bv_t[:], bvd[:]), "sy", 16)
        b.sync(sy0)
        def gp0(g):
            p.selfw(g, "gp")
            p.inc(g.dma_start(x_tmb[:], xa[:].rearrange("(c p) h -> p c h", p=128)), "gp", 16)
            p.selfw(g, "gp")
            p.inc(g.dma_start(x_fmb[:], xaT[:].rearrange("(c p) t -> p c t", p=128)), "gp", 16)
            p.selfw(g, "gp")
            p.inc(g.dma_start(qpb[:], qp[:].rearrange("(c p) e -> p c e", p=128)), "gp", 16)
            p.selfw(g, "gp")
            p.inc(g.dma_start(kpb[:], kp[:].rearrange("(c p) e -> p c e", p=128)), "gp", 16)
            p.selfw(g, "gp")
            p.inc(g.dma_start(gTb[:], gTd[:].rearrange("(c p) e -> p c e", p=128)), "gp", 16)
        b.gpsimd(gp0)
        def dv0(v):
            p.inc(v.memset(ones_b[:], 1.0), "dve")
            p.inc(v.memset(onesr_b[:], 1.0), "dve")
        b.vector(dv0)
        st_init(p, b, st)
        gp_ld = p.c["gp"]

        _chk(upto, 2)
        act0 = p.c["act"]
        qpe, kpe = [], []
        def pe_qk(t):
            p.wait(t, "gp", gp_ld)
            p.wait(t, "sy")
            for ec in range(2):
                for tn in range(4):
                    g = ec * 4 + tn
                    if g >= 2:
                        t.wait_ge(p.s["act"], act0 + (g - 2) + 1)
                    bank = PB[g % 2]
                    for hc in range(NHC):
                        mm = t.matmul(bank[:], qpb[:, hc, ec * 128:(ec + 1) * 128],
                                      x_fmb[:, hc, tn * 512:(tn + 1) * 512],
                                      start=(hc == 0), stop=(hc == NHC - 1))
                    qpe.append(p.inc(mm, "pe"))
            for tc in range(NTC):
                if tc >= 2:
                    t.wait_ge(p.s["act"], act0 + 8 + (tc - 2) + 1)
                bank = PB[2 + tc % 2]
                for hc in range(NHC):
                    mm = t.matmul(bank[:, 0:E], x_fmb[:, hc, tc * 128:(tc + 1) * 128],
                                  kpb[:, hc, :], start=(hc == 0), stop=(hc == NHC - 1))
                kpe.append(p.inc(mm, "pe"))
        b.tensor(pe_qk)
        def act_qk(a):
            for ec in range(2):
                for tn in range(4):
                    g = ec * 4 + tn
                    p.wait(a, "pe", qpe[g])
                    p.selfw(a, "act")
                    p.inc(a.activation(qe[:, ec, tn * 512:(tn + 1) * 512],
                                       PB[g % 2][:], AF.Exp, bias=0.0, scale=1.0), "act")
            for tc in range(NTC):
                p.wait(a, "pe", kpe[tc])
                p.selfw(a, "act")
                p.inc(a.activation(ke[:, tc, :], PB[2 + tc % 2][:, 0:E],
                                   AF.Exp, bias=0.0, scale=1.0), "act")
        b.scalar(act_qk)

        _chk(upto, 3)
        dve0 = p.c["dve"]
        zpe = []
        def pe_z(t):
            p.wait(t, "act")
            t.wait_ge(p.s["dve"], 2)
            for tn in range(4):
                if tn >= 1:
                    t.wait_ge(p.s["dve"], dve0 + tn)
                for ec in range(2):
                    mm = t.matmul(PB[4][0:1, :], ones_b[:, 0:1],
                                  qe[:, ec, tn * 512:(tn + 1) * 512],
                                  start=(ec == 0), stop=(ec == 1))
                zpe.append(p.inc(mm, "pe"))
        b.tensor(pe_z)
        def dv_z(v):
            for tn in range(4):
                p.wait(v, "pe", zpe[tn])
                p.inc(v.tensor_copy(zrow[0:1, tn * 512:(tn + 1) * 512], PB[4][0:1, :]),
                      "dve")
            p.selfw(v, "dve")
            p.inc(v.tensor_single_scalar(zrow[:], zrow[:], 1e-8, ALU.add), "dve")
            p.selfw(v, "dve")
            p.inc(v.reciprocal(zrow[:], zrow[:]), "dve")
        b.vector(dv_z)
        dve1 = p.c["dve"]
        ztr = []
        def pe_zt(t):
            p.wait(t, "dve", dve1)
            for tc in range(NTC):
                if tc >= 2:
                    t.wait_ge(p.s["dve"], dve1 + (tc - 2) + 1)
                tr = t.transpose(PB[4 + tc % 2][:, 0:1],
                                 zrow[0:1, tc * 128:(tc + 1) * 128], eyef[0:1, 0:1])
                ztr.append(p.inc(tr, "pe"))
        b.tensor(pe_zt)
        def dv_zt(v):
            for tc in range(NTC):
                p.wait(v, "pe", ztr[tc])
                p.inc(v.tensor_copy(z_tm[:, tc:tc + 1], PB[4 + tc % 2][:, 0:1]), "dve")
        b.vector(dv_zt)

        _chk(upto, 4)
        dve2 = p.c["dve"]
        kvpe = []
        def pe_kv(t):
            for g in range(4):
                ec, hn = divmod(g, 2)
                if g >= 2:
                    t.wait_ge(p.s["dve"], dve2 + (g - 2) + 1)
                bank = PB[6 + g % 2]
                for tc in range(NTC):
                    mm = t.matmul(bank[:], ke[:, tc, ec * 128:(ec + 1) * 128],
                                  x_tmb[:, tc, hn * 512:(hn + 1) * 512],
                                  start=(tc == 0), stop=(tc == NTC - 1))
                kvpe.append(p.inc(mm, "pe"))
        b.tensor(pe_kv)
        def dv_kv(v):
            for g in range(4):
                ec, hn = divmod(g, 2)
                p.wait(v, "pe", kvpe[g])
                p.inc(v.tensor_copy(kv_b[:, ec, hn * 512:(hn + 1) * 512],
                                    PB[6 + g % 2][:]), "dve")
        b.vector(dv_kv)

        _chk(upto, 5)
        act1 = p.c["act"]
        numpe = []
        def pe_num(t):
            p.wait(t, "dve")
            for g in range(NTC * 2):
                tc, hn = divmod(g, 2)
                if g >= 2:
                    t.wait_ge(p.s["act"], act1 + (g - 2) + 1)
                bank = PB[g % 2]
                for ec in range(2):
                    mm = t.matmul(bank[:], qe[:, ec, tc * 128:(tc + 1) * 128],
                                  kv_b[:, ec, hn * 512:(hn + 1) * 512],
                                  start=(ec == 0), stop=(ec == 1))
                numpe.append(p.inc(mm, "pe"))
        b.tensor(pe_num)
        dve3 = p.c["dve"]
        numact = []
        def act_num(a):
            for g in range(NTC * 2):
                tc, hn = divmod(g, 2)
                p.wait(a, "pe", numpe[g])
                if g >= 1:
                    a.wait_ge(p.s["dve"], dve3 + g)
                p.selfw(a, "act")
                numact.append(p.inc(a.activation(tmp_res[:], PB[g % 2][:],
                                                 AF.Copy, bias=0.0,
                                                 scale=z_tm[:, tc:tc + 1]), "act"))
        b.scalar(act_num)
        def dv_num(v):
            for g in range(NTC * 2):
                tc, hn = divmod(g, 2)
                p.wait(v, "act", numact[g])
                p.selfw(v, "dve")
                p.inc(v.tensor_add(x_tm[:, tc, hn * 512:(hn + 1) * 512],
                                   x_tm[:, tc, hn * 512:(hn + 1) * 512],
                                   tmp_res[:]), "dve")
        b.vector(dv_num)

        _chk(upto, 6)
        ln_stats_batch(p, b, x_tm, st, NTC)
        ln_apply_and_fm(p, b, x_tm, x_tmb, x_fmb, st, NTC)

        _chk(upto, 7)
        scpe = []
        dve4 = p.c["dve"]
        def pe_sc(t):
            p.wait(t, "sy")
            for tc in range(NTC):
                if tc >= 2:
                    t.wait_ge(p.s["dve"], dve4 + (tc - 2) + 1)
                bank = PB[2 + tc % 2]
                for hc in range(NHC):
                    mm = t.matmul(bank[:, 0:NEXP], x_fmb[:, hc, tc * 128:(tc + 1) * 128],
                                  gTb[:, hc, :], start=(hc == 0), stop=(hc == NHC - 1))
                scpe.append(p.inc(mm, "pe"))
        b.tensor(pe_sc)
        def dv_sc(v):
            for tc in range(NTC):
                p.wait(v, "pe", scpe[tc])
                p.inc(v.tensor_copy(scor[:, tc, :], PB[2 + tc % 2][:, 0:NEXP]), "dve")
            p.selfw(v, "dve")
            p.inc(v.reduce_max(gmax[:], scor[:], AX.X), "dve")
            for e in range(NEXP):
                se = scor[:, :, e]
                p.selfw(v, "dve")
                p.inc(v.tensor_tensor(tta[:], se, gmax[:], ALU.is_ge), "dve")
                if e == 0:
                    p.selfw(v, "dve")
                    p.inc(v.tensor_copy(masks_tm[:, :, 0], tta[:]), "dve")
                    p.inc(v.tensor_copy(pre[:], se), "dve")
                else:
                    p.selfw(v, "dve")
                    p.inc(v.tensor_tensor(ttb[:], pre[:], gmax[:], ALU.is_lt), "dve")
                    p.selfw(v, "dve")
                    p.inc(v.tensor_mul(tta[:], tta[:], ttb[:]), "dve")
                    p.selfw(v, "dve")
                    p.inc(v.tensor_copy(masks_tm[:, :, e], tta[:]), "dve")
                    if e < NEXP - 1:
                        p.selfw(v, "dve")
                        p.inc(v.tensor_max(pre[:], pre[:], se), "dve")
        b.vector(dv_sc)
        _chk(upto, 8)
        dve5 = p.c["dve"]
        mtr = []
        def pe_mt(t):
            p.wait(t, "dve", dve5)
            for tc in range(NTC):
                if tc >= 2:
                    t.wait_ge(p.s["dve"], dve5 + (tc - 2) + 1)
                tr = t.transpose(PBb[4 + tc % 2][0:NEXP, 0:128], masks_tm[:, tc, :],
                                 eyeb[:, :])
                mtr.append(p.inc(tr, "pe"))
        b.tensor(pe_mt)
        def dv_mt(v):
            for tc in range(NTC):
                p.wait(v, "pe", mtr[tc])
                p.inc(v.tensor_copy(masks_f[0:NEXP, tc * 128:(tc + 1) * 128],
                                    PBb[4 + tc % 2][0:NEXP, 0:128]), "dve")
        b.vector(dv_mt)

        _chk(upto, 9)
        def sy_sel(s):
            p.wait(s, "dve")
            p.selfw(s, "sy")
            p.inc(s.dma_start(sel8, sel8_d[:]), "sy", 16)
        b.sync(sy_sel)

        for tn in range(4):
            for e in range(NEXP):
                def pe_mb(t, e=e, tn=tn):
                    p.wait(t, "dve")
                    p.wait(t, "sy")
                    p.inc(t.matmul(PB[6][:], sel8[:, e * 128:(e + 1) * 128],
                                   masks_f[0:NEXP, tn * 512:(tn + 1) * 512],
                                   start=True, stop=True), "pe")
                b.tensor(pe_mb)
                def dv_mb(v, e=e):
                    p.wait(v, "pe")
                    p.selfw(v, "dve")
                    p.inc(v.tensor_copy(mb_all[:, e, :], PB[6][:]), "dve")
                b.vector(dv_mb)
            pe_moe0 = p.c["pe"]
            gp_moe0 = p.c["gp"]
            dve_moe0 = p.c["dve"]
            def gp_w(g, tn=tn):
                for i in range(NEXP * NHC):
                    e, hc = divmod(i, NHC)
                    if i >= 2:
                        g.wait_ge(p.s["pe"], pe_moe0 + i)
                    elif pe_moe0 > 0:
                        g.wait_ge(p.s["pe"], pe_moe0)
                    p.selfw(g, "gp")
                    p.inc(g.dma_start(w_moe[:, i % 2, :],
                                      ewT[e, hc * 128:(hc + 1) * 128, :]), "gp", 16)
            b.gpsimd(gp_w)
            def dv_mm(v, tn=tn):
                for i in range(NEXP * NHC):
                    e, hc = divmod(i, NHC)
                    if i >= 1:
                        v.wait_ge(p.s["pe"], pe_moe0 + i)
                    p.selfw(v, "dve")
                    p.inc(v.tensor_tensor(mtile[:], x_fmb[:, hc, tn * 512:(tn + 1) * 512],
                                          mb_all[:, e, :], ALU.mult), "dve")
            b.vector(dv_mm)
            def pe_moe(t, tn=tn):
                for i in range(NEXP * NHC):
                    e, hc = divmod(i, NHC)
                    t.wait_ge(p.s["gp"], gp_moe0 + (i + 1) * 16)
                    t.wait_ge(p.s["dve"], dve_moe0 + i + 1)
                    for fc in range(8):
                        mm = t.matmul(PB[fc][:], w_moe[:, i % 2, fc * 128:(fc + 1) * 128],
                                      mtile[:], start=(i == 0),
                                      stop=(i == NEXP * NHC - 1))
                    p.inc(mm, "pe")
            b.tensor(pe_moe)
            slot = tn % 2
            def dv_ev(v, tn=tn, slot=slot):
                p.wait(v, "pe")
                p.wait(v, "sy")
                p.selfw(v, "dve")
                for fc in range(8):
                    p.inc(v.tensor_copy(moe_fm2[:, slot, fc, :], PB[fc][:]), "dve")
            b.vector(dv_ev)
            for tj in range(4):
                tc = tn * 4 + tj
                def sy_tr(s, tn=tn, slot=slot, tj=tj, tc=tc):
                    p.wait(s, "dve")
                    for hc in range(NHC):
                        p.selfw(s, "sy")
                        p.inc(s.dma_start(
                            moe_tm2[:, tc % 2, hc * 128:(hc + 1) * 128],
                            moe_fm2[:, slot, hc, tj * 128:(tj + 1) * 128],
                            transpose=True), "sy", 16)
                b.sync(sy_tr)
                def dv_res(v, tc=tc):
                    p.wait(v, "sy")
                    p.selfw(v, "dve")
                    p.inc(v.tensor_scalar(x_tm[:, tc, :], x_tm[:, tc, :], 2.0, 0.0,
                                          ALU.mult, ALU.add), "dve")
                    p.selfw(v, "dve")
                    p.inc(v.tensor_add(x_tm[:, tc, :], x_tm[:, tc, :],
                                       moe_tm2[:, tc % 2, :]), "dve")
                b.vector(dv_res)


        _chk(upto, 10)
        ln_stats_batch(p, b, x_tm, st, NTC)
        ln_apply_and_fm(p, b, x_tm, x_tmb, x_fmb, st, NTC)

        def sy_x3(s):
            p.wait(s, "dve")
            p.selfw(s, "sy")
            p.inc(s.dma_start(x3_o[:].rearrange("(c p) h -> p c h", p=128), x_tm[:]),
                  "sy", 16)
        b.sync(sy_x3)

        _chk(upto, 11)
        def gp_w2(g):
            p.wait(g, "pe")
            p.selfw(g, "gp")
            p.inc(g.dma_start(wq_b, wqT[:].rearrange("(c p) e -> p c e", p=128)), "gp", 16)
            p.selfw(g, "gp")
            p.inc(g.dma_start(wk_b, wkT[:].rearrange("(c p) e -> p c e", p=128)), "gp", 16)
            p.selfw(g, "gp")
            p.inc(g.dma_start(wv_b, wvT[:].rearrange("(c p) e -> p c e", p=128)), "gp", 16)
        b.gpsimd(gp_w2)
        gp_ld2 = p.c["gp"]
        act2c = p.c["act"]
        prpe = []
        def pe_pr(t):
            p.wait(t, "sy")
            p.wait(t, "gp", gp_ld2)
            for j, wt in enumerate((wq_b, wk_b, wv_b)):
                for tn in range(4):
                    g = j * 4 + tn
                    if g >= 2:
                        t.wait_ge(p.s["act"], act2c + (g - 2) + 1)
                    bank = PB[g % 2]
                    for hc in range(NHC):
                        mm = t.matmul(bank[0:HD, :], wt[:, hc, :],
                                      x_fmb[:, hc, tn * 512:(tn + 1) * 512],
                                      start=(hc == 0), stop=(hc == NHC - 1))
                    prpe.append(p.inc(mm, "pe"))
        b.tensor(pe_pr)
        def act_pr(a):
            outs = (qf, kf, vf)
            biases = (bq_t, bk_t, bv_t)
            for j in range(3):
                for tn in range(4):
                    g = j * 4 + tn
                    p.wait(a, "pe", prpe[g])
                    p.selfw(a, "act")
                    p.inc(a.activation(outs[j][:, tn * 512:(tn + 1) * 512],
                                       PB[g % 2][0:HD, :], AF.Identity,
                                       bias=biases[j][:], scale=1.0), "act")
        b.scalar(act_pr)

        _chk(upto, 12)
        dve6 = p.c["dve"]
        vtr = []
        def pe_vt(t):
            p.wait(t, "act")
            for tc in range(NTC):
                if tc >= 2:
                    t.wait_ge(p.s["dve"], dve6 + (tc - 2) + 1)
                tr = t.transpose(PBb[2 + tc % 2][0:128, 0:HD],
                                 vf[0:HD, tc * 128:(tc + 1) * 128], eyeb[0:HD, 0:HD])
                vtr.append(p.inc(tr, "pe"))
        b.tensor(pe_vt)
        def dv_vt(v):
            for tc in range(NTC):
                p.wait(v, "pe", vtr[tc])
                p.inc(v.tensor_copy(v_tmb[:, tc, :], PBb[2 + tc % 2][0:128, 0:HD]),
                      "dve")
        b.vector(dv_vt)

        _chk(upto, 13)
        dve7 = p.c["dve"]
        nrmpe = []
        def dv_n1(v):
            p.wait(v, "act")
            p.selfw(v, "dve")
            p.inc(v.tensor_mul(sqn[:], qf[:], qf[:]), "dve")
        b.vector(dv_n1)
        def pe_nrm(t):
            for j in range(2):
                for tn in range(4):
                    g = j * 4 + tn
                    if g == 0:
                        t.wait_ge(p.s["dve"], dve7 + 1)
                    elif j == 0:
                        t.wait_ge(p.s["dve"], dve7 + 1 + g)
                    elif tn == 0:
                        t.wait_ge(p.s["dve"], dve7 + 7)
                    else:
                        t.wait_ge(p.s["dve"], dve7 + 7 + tn)
                    mm = t.matmul(PB[4][0:1, :], ones_b[0:HD, 0:1],
                                  sqn[:, tn * 512:(tn + 1) * 512], start=True, stop=True)
                    nrmpe.append(p.inc(mm, "pe"))
        b.tensor(pe_nrm)
        def dv_n2(v):
            for tn in range(4):
                p.wait(v, "pe", nrmpe[tn])
                p.selfw(v, "dve")
                p.inc(v.tensor_copy(zrow[0:1, tn * 512:(tn + 1) * 512], PB[4][0:1, :]),
                      "dve")
            p.selfw(v, "dve")
            p.inc(v.tensor_reduce(qkm_t[0:1, 0:1], zrow[:], AX.X, ALU.max), "dve")
            p.selfw(v, "dve")
            p.inc(v.tensor_mul(sqn[:], kf[:], kf[:]), "dve")
            for tn in range(4):
                p.wait(v, "pe", nrmpe[4 + tn])
                p.selfw(v, "dve")
                p.inc(v.tensor_copy(zrow[0:1, tn * 512:(tn + 1) * 512], PB[4][0:1, :]),
                      "dve")
            p.selfw(v, "dve")
            p.inc(v.tensor_reduce(qkm_t[0:1, 1:2], zrow[:], AX.X, ALU.max), "dve")
        b.vector(dv_n2)

        _chk(upto, 14)
        def sy_out(s):
            p.wait(s, "dve")
            p.wait(s, "act")
            p.selfw(s, "sy")
            p.inc(s.dma_start(q_o[:], qf[:]), "sy", 16)
            p.selfw(s, "sy")
            p.inc(s.dma_start(k_o[:], kf[:]), "sy", 16)
            p.selfw(s, "sy")
            p.inc(s.dma_start(v_o[:].rearrange("(c p) d -> p c d", p=128), v_tmb[:]),
                  "sy", 16)
            p.selfw(s, "sy")
            p.inc(s.dma_start(qkm_o[:], qkm_t[:]), "sy", 16)
            p.wait(s, "sy")
        b.sync(sy_out)
      except _Stop:
        def sy_tail(s):
            p.wait(s, "dve")
            p.wait(s, "act")
            p.wait(s, "pe")
            p.selfw(s, "sy")
            p.inc(s.dma_start(x3_o[:].rearrange("(c p) h -> p c h", p=128),
                              x_tm[:]), "sy", 16)
            p.selfw(s, "sy")
            p.inc(s.dma_start(q_o[:], qf[:]), "sy", 16)
            p.wait(s, "sy")
        b.sync(sy_tail)
    return nc



def build_B():
    nc = bass.Bass("TRN2", target_bir_lowering=False)
    x3 = nc.dram_tensor("x3", [T, D], F32, kind="ExternalInput")
    qfd = nc.dram_tensor("qf", [HD, T], BF16, kind="ExternalInput")
    kfull = nc.dram_tensor("kfull", [HD, NCORE * T], BF16, kind="ExternalInput")
    vfull = nc.dram_tensor("vfull", [NCORE * T, HD], BF16, kind="ExternalInput")
    qkmd = nc.dram_tensor("qkm", [1, 16], F32, kind="ExternalInput")
    w1T = nc.dram_tensor("w1T", [32, 128, NHC, 128], F32, kind="ExternalInput")
    w2T = nc.dram_tensor("w2T", [32, 128, NHC, 128], F32, kind="ExternalInput")
    h1T = nc.dram_tensor("h1T", [32, 128, 32, 128], F32, kind="ExternalInput")
    h2T = nc.dram_tensor("h2T", [32, 128, 32, 128], F32, kind="ExternalInput")
    w3T = nc.dram_tensor("w3T", [DH, D], F32, kind="ExternalInput")
    s1d = nc.dram_tensor("s1d", [128, 32], F32, kind="ExternalInput")
    s2d = nc.dram_tensor("s2d", [128, 32], F32, kind="ExternalInput")
    h1sd = nc.dram_tensor("h1sd", [128, 32], F32, kind="ExternalInput")
    h2sd = nc.dram_tensor("h2sd", [128, 32], F32, kind="ExternalInput")
    s3d = nc.dram_tensor("s3d", [1, D], F32, kind="ExternalInput")
    eyef_d = nc.dram_tensor("eyef", [128, 128], F32, kind="ExternalInput")
    out_o = nc.dram_tensor("out_o", [T, D], F32, kind="ExternalOutput")
    x4_d = nc.dram_tensor("x4_d", [T, D], F32, kind="Internal")

    NK = (NCORE * T) // 128

    arena = nc.alloc_sbuf_tensor("arena", [128, 65536], BF16)
    cur = [0]
    def carve(n):
        cur[0] = (cur[0] + 15) // 16 * 16
        off = cur[0]
        cur[0] += n
        return off
    o_k = carve(16384)
    k_sb = arena[:, o_k:o_k + 16384][0:HD, :]
    o_v = carve(NK * 65 + 7)
    v65 = arena[:, o_v:o_v + NK * 65].rearrange("p (c d) -> p c d", d=65)
    o_pt = carve(2048)
    pt_a = arena[:, o_pt:o_pt + 2048].rearrange("p (s t) -> p s t", s=2)
    o_y = carve(2048)
    y_sb = arena[:, o_y:o_y + 2048].bitcast(F32)
    o_q = carve(2048)
    qf_b = arena[:, o_q:o_q + 2048][0:HD, :]
    o_x3 = carve(4096)
    x3t = arena[:, o_x3:o_x3 + 4096].bitcast(F32).rearrange(
        "p (s h) -> p s h", s=2)
    h_a = arena[:, 0:32768].rearrange("p (c t) -> p c t", c=32)
    h_b = arena[:, 32768:65536].rearrange("p (c t) -> p c t", c=32)

    x_fmb = nc.alloc_sbuf_tensor("x_fmb", [128, NHC, T], BF16)
    x4t = nc.alloc_sbuf_tensor("x4t", [128, 1, D], F32)
    xtmb = nc.alloc_sbuf_tensor("xtmb", [128, 1, D], BF16)
    yT_sb = nc.alloc_sbuf_tensor("yT_sb", [128, 65], F32)
    z1 = nc.alloc_sbuf_tensor("z1", [128, 1], F32)
    ybar = nc.alloc_sbuf_tensor("ybar", [128, HD], F32)
    st = st_alloc(nc)
    qkm_t = nc.alloc_sbuf_tensor("qkm_t", [1, 16], F32)
    csc = nc.alloc_sbuf_tensor("cs", [1, 4], F32)
    cb = nc.alloc_sbuf_tensor("cb", [128, 1], F32)
    ncb = nc.alloc_sbuf_tensor("ncb", [128, 1], F32)
    inv32 = nc.alloc_sbuf_tensor("inv32", [128, 1], F32)
    onesf = nc.alloc_sbuf_tensor("onesf", [1, 128], F32)
    eyef = nc.alloc_sbuf_tensor("eyef_t", [128, 128], F32)
    s1_t = nc.alloc_sbuf_tensor("s1_t", [128, 32], F32)
    s2_t = nc.alloc_sbuf_tensor("s2_t", [128, 32], F32)
    h1s_t = nc.alloc_sbuf_tensor("h1s_t", [128, 32], F32)
    h2s_t = nc.alloc_sbuf_tensor("h2s_t", [128, 32], F32)
    s3r = nc.alloc_sbuf_tensor("s3r", [1, D], F32)
    s3_bb = nc.alloc_sbuf_tensor("s3_bb", [128, D], F32)
    hw_b = nc.alloc_sbuf_tensor("hw_b", [128, 2, 32, 128], BF16)
    w1_b = hw_b[:, :, 0:NHC, :]
    w2_b = hw_b[:, :, NHC:2 * NHC, :]
    w3_b = hw_b[:].rearrange("p s c f -> p s (c f)")[:, :, 0:1024]
    silu_s = nc.alloc_sbuf_tensor("silu_s", [128, 1024], F32)
    t2_s = nc.alloc_sbuf_tensor("t2_s", [128, 1024], F32)
    o_sb = silu_s[:]

    psSA = nc.alloc_psum_tensor("psSA", [128, 1024], F32)
    psSB = nc.alloc_psum_tensor("psSB", [128, 1024], F32)
    psY = nc.alloc_psum_tensor("psY", [128, 1024], F32)
    psC = nc.alloc_psum_tensor("psC", [128, 512], F32)
    psT = nc.alloc_psum_tensor("psT", [128, 512], F32)

    p = Prog(nc)
    p.sems("sy", "gp", "pe", "act", "dve")

    with nc.Block() as b:
        boot_clear(b, nc, p)
        def sy0(s):
            p.selfw(s, "sy")
            p.inc(s.dma_start(k_sb[:, :], kfull[:]), "sy", 16)
            p.selfw(s, "sy")
            p.inc(s.dma_start(qf_b[:, :], qfd[:]), "sy", 16)
            p.selfw(s, "sy")
            p.inc(s.dma_start(qkm_t[:], qkmd[:]), "sy", 16)
            p.selfw(s, "sy")
            p.inc(s.dma_start(eyef[:], eyef_d[:]), "sy", 16)
            p.selfw(s, "sy")
            p.inc(s.dma_start(s1_t[:], s1d[:]), "sy", 16)
            p.selfw(s, "sy")
            p.inc(s.dma_start(s2_t[:], s2d[:]), "sy", 16)
            p.selfw(s, "sy")
            p.inc(s.dma_start(h1s_t[:], h1sd[:]), "sy", 16)
            p.selfw(s, "sy")
            p.inc(s.dma_start(h2s_t[:], h2sd[:]), "sy", 16)
            p.selfw(s, "sy")
            p.inc(s.dma_start(s3r[:], s3d[:]), "sy", 16)
        b.sync(sy0)
        def dv0(v):
            p.inc(v.memset(onesf[:], 1.0), "dve")
            p.inc(v.memset(v65[:, :, 64:65], 1.0), "dve")
            p.inc(v.memset(inv32[:], 1.0 / 32.0), "dve")
        b.vector(dv0)
        st_init(p, b, st)
        def sy1(s):
            p.wait(s, "dve")
            p.selfw(s, "sy")
            p.inc(s.dma_start(v65[:, :, 0:64],
                              vfull[:].rearrange("(c p) d -> p c d", p=128)), "sy", 16)
        b.sync(sy1)
        def dv_c(v):
            p.wait(v, "sy")
            p.inc(v.tensor_reduce(csc[0:1, 0:2],
                                  qkm_t[:].rearrange("o (a c) -> o a c", a=2),
                                  AX.X, ALU.max), "dve")
            p.selfw(v, "dve")
            p.inc(v.tensor_add(csc[0:1, 2:3], csc[0:1, 0:1], csc[0:1, 1:2]), "dve")
            p.selfw(v, "dve")
            p.inc(v.tensor_single_scalar(csc[0:1, 3:4], csc[0:1, 2:3], 1.0 / 64.0,
                                         ALU.mult), "dve")
        b.vector(dv_c)
        def pe_c(t):
            p.wait(t, "dve")
            p.inc(t.matmul(psC[:, 0:1], onesf[0:1, :], csc[0:1, 3:4],
                           start=True, stop=True), "pe")
        b.tensor(pe_c)
        def dv_c2(v):
            p.wait(v, "pe")
            p.inc(v.tensor_copy(cb[:], psC[:, 0:1]), "dve")
            p.selfw(v, "dve")
            p.inc(v.tensor_single_scalar(ncb[:], cb[:], -1.0, ALU.mult), "dve")
        b.vector(dv_c2)

        for th in range(2):
            toff = th * 1024
            act_b = p.c["act"]
            spe, vpe = [], []
            def pe_at(t, toff=toff, act_b=act_b, spe=spe, vpe=vpe):
                p.wait(t, "sy")
                p.wait(t, "dve")
                for kc in range(NK):
                    bank = psSA if kc % 2 == 0 else psSB
                    if kc >= 2:
                        t.wait_ge(p.s["act"], act_b + (kc - 2) + 1)
                    for nn in range(2):
                        mm = t.matmul(bank[:, nn * 512:(nn + 1) * 512],
                                      k_sb[:, kc * 128:(kc + 1) * 128],
                                      qf_b[:, toff + nn * 512:toff + (nn + 1) * 512],
                                      start=True, stop=True)
                    spe.append(p.inc(mm, "pe"))
                    if kc >= 1:
                        pk = kc - 1
                        t.wait_ge(p.s["act"], act_b + pk + 1)
                        for nn in range(2):
                            mm = t.matmul(psY[0:65, nn * 512:(nn + 1) * 512],
                                          v65[:, pk, :],
                                          pt_a[:, pk % 2, nn * 512:(nn + 1) * 512],
                                          start=(pk == 0), stop=False)
                        vpe.append(p.inc(mm, "pe"))
                t.wait_ge(p.s["act"], act_b + NK)
                for nn in range(2):
                    mm = t.matmul(psY[0:65, nn * 512:(nn + 1) * 512], v65[:, NK - 1, :],
                                  pt_a[:, (NK - 1) % 2, nn * 512:(nn + 1) * 512],
                                  start=False, stop=True)
                vpe.append(p.inc(mm, "pe"))
            b.tensor(pe_at)
            def act_at(a, spe=spe, vpe=vpe):
                for kc in range(NK):
                    p.wait(a, "pe", spe[kc])
                    if kc >= 2:
                        p.wait(a, "pe", vpe[kc - 2])
                    bank = psSA if kc % 2 == 0 else psSB
                    p.selfw(a, "act")
                    p.inc(a.activation(pt_a[:, kc % 2, :], bank[:], AF.Exp,
                                       bias=ncb[:], scale=inv32[:]), "act")
            b.scalar(act_at)
            def dv_y(v):
                p.wait(v, "pe")
                p.inc(v.tensor_copy(y_sb[0:65, :], psY[0:65, :]), "dve")
            b.vector(dv_y)

            for tj in range(8):
                tc = th * 8 + tj
                sl = 0
                def sy_ld(s, tc=tc, sl=sl):
                    p.wait(s, "dve")
                    p.selfw(s, "sy")
                    p.inc(s.dma_start(x3t[:, sl, :], x3[tc * 128:(tc + 1) * 128, :]),
                          "sy", 16)
                b.sync(sy_ld)
                def pe_t(t, tj=tj):
                    p.wait(t, "dve")
                    p.inc(t.transpose(psT[:, 0:65],
                                      y_sb[0:65, tj * 128:(tj + 1) * 128],
                                      eyef[0:65, 0:65]), "pe")
                b.tensor(pe_t)
                def dv_t(v, tc=tc, sl=sl):
                    p.wait(v, "pe")
                    p.wait(v, "sy")
                    p.selfw(v, "dve")
                    p.inc(v.tensor_copy(yT_sb[:], psT[:, 0:65]), "dve")
                    p.selfw(v, "dve")
                    p.inc(v.reciprocal(z1[:], yT_sb[:, 64:65]), "dve")
                    p.selfw(v, "dve")
                    p.inc(v.tensor_scalar(ybar[:], yT_sb[:, 0:64], z1[:], 0.0,
                                          ALU.mult, ALU.add), "dve")
                    p.selfw(v, "dve")
                    p.inc(v.tensor_tensor(
                        x4t[:, sl, :].rearrange("p (r d) -> p r d", r=16),
                        x3t[:, sl, :].rearrange("p (r d) -> p r d", r=16),
                        ybar[:].unsqueeze(1).broadcast_to([128, 16, HD]), ALU.add),
                        "dve")
                b.vector(dv_t)
                ln_tile(p, b, x4t[:, sl, :], st, tc % NTC)
                def dv_cp(v, sl=sl):
                    p.wait(v, "sy")
                    p.selfw(v, "dve")
                    p.inc(v.tensor_copy(xtmb[:, sl, :], x4t[:, sl, :]), "dve")
                b.vector(dv_cp)
                def sy_st(s, tc=tc, sl=sl):
                    p.wait(s, "dve")
                    p.selfw(s, "sy")
                    p.inc(s.dma_start(x4_d[tc * 128:(tc + 1) * 128, :], x4t[:, sl, :]),
                          "sy", 16)
                    for hc in range(NHC):
                        p.selfw(s, "sy")
                        p.inc(s.dma_start(x_fmb[:, hc, tc * 128:(tc + 1) * 128],
                                          xtmb[:, sl, hc * 128:(hc + 1) * 128],
                                          transpose=True), "sy", 16)
                b.sync(sy_st)

        def pe_s3(t):
            p.wait(t, "dve")
            p.inc(t.matmul(psC[:, :], onesf[0:1, :], s3r[0:1, 0:512],
                           start=True, stop=True), "pe")
            p.inc(t.matmul(psT[:, :], onesf[0:1, :], s3r[0:1, 512:1024],
                           start=True, stop=True), "pe")
        b.tensor(pe_s3)
        def dv_s3(v):
            p.wait(v, "pe")
            p.selfw(v, "dve")
            p.inc(v.tensor_copy(s3_bb[:, 0:512], psC[:]), "dve")
            p.inc(v.tensor_copy(s3_bb[:, 512:1024], psT[:]), "dve")
        b.vector(dv_s3)

        for th in range(2):
            toff = th * 1024

            pe_b = p.c["pe"]
            act_b = p.c["act"]
            dve_b = p.c["dve"]
            gp_b = p.c["gp"]
            def gp_w1(g, pe_b=pe_b):
                for fc in range(32):
                    if fc < 2:
                        if pe_b > 0:
                            g.wait_ge(p.s["pe"], pe_b)
                    else:
                        g.wait_ge(p.s["pe"], pe_b + (fc - 2) + 1)
                    p.selfw(g, "gp")
                    p.inc(g.dma_start(w1_b[:, fc % 2, :, :], w1T[fc]), "gp", 16)
                    p.selfw(g, "gp")
                    p.inc(g.dma_start(w2_b[:, fc % 2, :, :], w2T[fc]), "gp", 16)
            b.gpsimd(gp_w1)
            def pe_s1(t, toff=toff, gp_b=gp_b, act_b=act_b):
                p.wait(t, "sy")
                p.wait(t, "dve")
                for fc in range(32):
                    t.wait_ge(p.s["gp"], gp_b + (fc + 1) * 32)
                    if fc >= 1:
                        t.wait_ge(p.s["act"], act_b + 2 * (fc - 1) + 2)
                    for hc in range(NHC):
                        for nn in range(2):
                            mm = t.matmul(psSA[:, nn * 512:(nn + 1) * 512],
                                          w1_b[:, fc % 2, hc, :],
                                          x_fmb[:, hc, toff + nn * 512:toff + (nn + 1) * 512],
                                          start=(hc == 0), stop=(hc == NHC - 1))
                    for hc in range(NHC):
                        for nn in range(2):
                            mm = t.matmul(psSB[:, nn * 512:(nn + 1) * 512],
                                          w2_b[:, fc % 2, hc, :],
                                          x_fmb[:, hc, toff + nn * 512:toff + (nn + 1) * 512],
                                          start=(hc == 0), stop=(hc == NHC - 1))
                    p.inc(mm, "pe")
            b.tensor(pe_s1)
            s1pe_list = list(range(pe_b + 1, pe_b + 33))
            def act_s1(a, dve_b=dve_b, s1pe_list=s1pe_list):
                for fc in range(32):
                    p.wait(a, "pe", s1pe_list[fc])
                    if fc >= 1:
                        a.wait_ge(p.s["dve"], dve_b + fc)
                    p.selfw(a, "act")
                    p.inc(a.activation(silu_s[:], psSA[:], AF.Silu,
                                       bias=0.0, scale=s1_t[:, fc:fc + 1]), "act")
                    p.inc(a.activation(t2_s[:], psSB[:], AF.Copy,
                                       bias=0.0, scale=s2_t[:, fc:fc + 1]), "act")
            b.scalar(act_s1)
            def dv_s1(v, act_b=act_b):
                for fc in range(32):
                    v.wait_ge(p.s["act"], act_b + 2 * fc + 2)
                    p.selfw(v, "dve")
                    p.inc(v.tensor_mul(h_a[:, fc, :], silu_s[:], t2_s[:]), "dve")
            b.vector(dv_s1)

            for wT, ssc, src, dst in ((h1T, h1s_t, h_a, h_b), (h2T, h2s_t, h_b, h_a)):
                pe_b = p.c["pe"]
                act_b = p.c["act"]
                gp_b = p.c["gp"]
                def gp_wh(g, wT=wT, pe_b=pe_b):
                    for fc in range(32):
                        if fc < 2:
                            g.wait_ge(p.s["pe"], pe_b)
                        else:
                            g.wait_ge(p.s["pe"], pe_b + (fc - 2) + 1)
                        p.selfw(g, "gp")
                        p.inc(g.dma_start(hw_b[:, fc % 2, :, :], wT[fc]), "gp", 16)
                b.gpsimd(gp_wh)
                def pe_h(t, src=src, gp_b=gp_b, act_b=act_b):
                    p.wait(t, "dve")
                    p.wait(t, "act")
                    for fc in range(32):
                        t.wait_ge(p.s["gp"], gp_b + (fc + 1) * 16)
                        if fc >= 1:
                            t.wait_ge(p.s["act"], act_b + fc)
                        for kc in range(32):
                            for nn in range(2):
                                mm = t.matmul(psSA[:, nn * 512:(nn + 1) * 512],
                                              hw_b[:, fc % 2, kc, :],
                                              src[:, kc, nn * 512:(nn + 1) * 512],
                                              start=(kc == 0), stop=(kc == 31))
                        p.inc(mm, "pe")
                b.tensor(pe_h)
                hpe_list = list(range(pe_b + 1, pe_b + 33))
                def act_h(a, ssc=ssc, dst=dst, hpe_list=hpe_list):
                    for fc in range(32):
                        p.wait(a, "pe", hpe_list[fc])
                        p.selfw(a, "act")
                        p.inc(a.activation(dst[:, fc, :], psSA[:], AF.Gelu,
                                           bias=0.0, scale=ssc[:, fc:fc + 1]), "act")
                b.scalar(act_h)

            for tg in range(2):
                banks = [(psSA[:, 0:512], psSA[:, 512:1024]),
                         (psSB[:, 0:512], psSB[:, 512:1024]),
                         (psY[:, 0:512], psY[:, 512:1024]),
                         (psC[:, :], psT[:, :])]
                pe_b = p.c["pe"]
                gp_b = p.c["gp"]
                def gp_w3(g, tg=tg, pe_b=pe_b):
                    for kc in range(32):
                        if kc < 2:
                            g.wait_ge(p.s["pe"], pe_b)
                        else:
                            g.wait_ge(p.s["pe"], pe_b + (kc - 2) + 1)
                        p.selfw(g, "gp")
                        p.inc(g.dma_start(w3_b[:, kc % 2, :],
                                          w3T[kc * 128:(kc + 1) * 128, :]), "gp", 16)
                b.gpsimd(gp_w3)
                def pe_w3(t, tg=tg, gp_b=gp_b, banks=banks):
                    p.wait(t, "act")
                    p.wait(t, "dve")
                    for kc in range(32):
                        t.wait_ge(p.s["gp"], gp_b + (kc + 1) * 16)
                        for tcc in range(4):
                            lh = h_a[:, kc,
                                     tg * 512 + tcc * 128:tg * 512 + (tcc + 1) * 128]
                            for nn in range(2):
                                mm = t.matmul(banks[tcc][nn], lh,
                                              w3_b[:, kc % 2, nn * 512:(nn + 1) * 512],
                                              start=(kc == 0), stop=(kc == 31))
                        p.inc(mm, "pe")
                b.tensor(pe_w3)
                for tcc in range(4):
                    tc = th * 8 + tg * 4 + tcc
                    sl = 0
                    def sy_l4(s, tc=tc, sl=sl):
                        p.wait(s, "dve")
                        p.selfw(s, "sy")
                        p.inc(s.dma_start(x4t[:, sl, :],
                                          x4_d[tc * 128:(tc + 1) * 128, :]), "sy", 16)
                    b.sync(sy_l4)
                    def dv_o(v, tcc=tcc, sl=sl, banks=banks):
                        p.wait(v, "pe")
                        p.wait(v, "sy")
                        p.selfw(v, "dve")
                        p.inc(v.tensor_copy(o_sb[:, 0:512], banks[tcc][0]), "dve")
                        p.inc(v.tensor_copy(o_sb[:, 512:1024], banks[tcc][1]), "dve")
                        p.selfw(v, "dve")
                        p.inc(v.tensor_mul(o_sb[:], o_sb[:], s3_bb[:]), "dve")
                        p.selfw(v, "dve")
                        p.inc(v.tensor_add(x4t[:, sl, :], x4t[:, sl, :], o_sb[:]), "dve")
                    b.vector(dv_o)
                    ln_tile(p, b, x4t[:, sl, :], st, tc % NTC)
                    def sy_o(s, tc=tc, sl=sl):
                        p.wait(s, "dve")
                        p.selfw(s, "sy")
                        p.inc(s.dma_start(out_o[tc * 128:(tc + 1) * 128, :],
                                          x4t[:, sl, :]), "sy", 16)
                    b.sync(sy_o)

        def sy_fin(s):
            p.wait(s, "sy")
        b.sync(sy_fin)
    return nc



_bf = ml_dtypes.bfloat16


def _tile_w(w, nkc):
    a = np.asarray(w, np.float32).reshape(32, 128, nkc, 128)
    return np.ascontiguousarray(a.transpose(0, 3, 2, 1))


def sel8_host():
    s = np.zeros((NEXP, NEXP * 128), dtype=_bf)
    for e in range(NEXP):
        s[e, e * 128:(e + 1) * 128] = 1.0
    return s


def kernel(L0_attn_x, L0_attn_q_proj, L0_attn_k_proj, L0_ffn_gate_w, L0_ffn_expert_w,
           L1_attn_Wq, L1_attn_bq, L1_attn_Wk, L1_attn_bk, L1_attn_Wv, L1_attn_bv,
           L1_ffn_w1_w, L1_ffn_w1_s, L1_ffn_w2_w, L1_ffn_w2_s,
           L1_ffn_h1_w, L1_ffn_h1_s, L1_ffn_h2_w, L1_ffn_h2_s,
           L1_ffn_w3_w, L1_ffn_w3_s):
    f32 = np.float32
    ct = np.ascontiguousarray
    x = np.asarray(L0_attn_x, f32)
    eyeb = np.eye(128, dtype=_bf)
    eyef = np.eye(128, dtype=f32)

    common_A = {
        "qp": ct(np.asarray(L0_attn_q_proj, f32)),
        "kp": ct(np.asarray(L0_attn_k_proj, f32)),
        "gT": ct(np.asarray(L0_ffn_gate_w, f32).T),
        "ewT": ct(np.asarray(L0_ffn_expert_w, f32).transpose(0, 2, 1)),
        "wqT": ct(np.asarray(L1_attn_Wq, f32).T),
        "wkT": ct(np.asarray(L1_attn_Wk, f32).T),
        "wvT": ct(np.asarray(L1_attn_Wv, f32).T),
        "bq": ct(np.asarray(L1_attn_bq, f32).reshape(HD, 1)),
        "bk": ct(np.asarray(L1_attn_bk, f32).reshape(HD, 1)),
        "bv": ct(np.asarray(L1_attn_bv, f32).reshape(HD, 1)),
        "eyeb": eyeb, "eyef": eyef, "sel8": sel8_host(),
    }
    in_maps_A = []
    for c in range(NCORE):
        m = dict(common_A)
        m["xa"] = ct(x[c])
        m["xaT"] = ct(x[c].T)
        in_maps_A.append(m)

    ncA = build_A()
    resA = run_bass_kernel_spmd(ncA, in_maps_A, core_ids=list(range(NCORE)))
    ra = resA.results

    kfull = ct(np.concatenate([np.asarray(ra[c]["k_o"]) for c in range(NCORE)], axis=1))
    vfull = ct(np.concatenate([np.asarray(ra[c]["v_o"]) for c in range(NCORE)], axis=0))
    qkm = ct(np.concatenate([np.asarray(ra[c]["qkm_o"], f32) for c in range(NCORE)],
                            axis=0).T.reshape(1, 16))

    sc = lambda s: ct(np.asarray(s, f32).reshape(-1, 128).T)
    common_B = {
        "kfull": kfull, "vfull": vfull, "qkm": qkm,
        "w1T": _tile_w(L1_ffn_w1_w, NHC),
        "w2T": _tile_w(L1_ffn_w2_w, NHC),
        "h1T": _tile_w(L1_ffn_h1_w, 32),
        "h2T": _tile_w(L1_ffn_h2_w, 32),
        "w3T": ct(np.asarray(L1_ffn_w3_w, f32).T),
        "s1d": sc(L1_ffn_w1_s), "s2d": sc(L1_ffn_w2_s),
        "h1sd": sc(L1_ffn_h1_s), "h2sd": sc(L1_ffn_h2_s),
        "s3d": ct(np.asarray(L1_ffn_w3_s, f32).reshape(1, D)),
        "eyef": eyef,
    }
    in_maps_B = []
    for c in range(NCORE):
        m = dict(common_B)
        m["x3"] = np.asarray(ra[c]["x3_o"], f32)
        m["qf"] = np.asarray(ra[c]["q_o"])
        in_maps_B.append(m)

    ncB = build_B()
    resB = run_bass_kernel_spmd(ncB, in_maps_B, core_ids=list(range(NCORE)))
    out = np.stack([np.asarray(resB.results[c]["out_o"], f32) for c in range(NCORE)],
                   axis=0)
    return out


# version 04
# speedup vs baseline: 1.0399x; 1.0280x over previous
import numpy as np
import ml_dtypes
import concourse.bass as bass
import concourse.mybir as mybir
from concourse.bass_utils import run_bass_kernel_spmd

F32 = mybir.dt.float32
BF16 = mybir.dt.bfloat16
AF = mybir.ActivationFunctionType
ALU = mybir.AluOpType
AX = mybir.AxisListType

NCORE = 8
T = 2048
D = 1024
E = 256
NEXP = 8
HD = 64
DH = 4096
NTC = T // 128
NHC = D // 128
LN_EPS = 1e-5


class Prog:

    def __init__(self, nc):
        self.nc = nc
        self.s = {}
        self.c = {}

    def sems(self, *names):
        for n in names:
            self.s[n] = self.nc.alloc_semaphore(n)
            self.c[n] = 0

    def inc(self, instr, name, k=1):
        instr.then_inc(self.s[name], k)
        self.c[name] += k
        return self.c[name]

    def wait(self, eng, name, val=None):
        v = self.c[name] if val is None else val
        if v > 0:
            eng.wait_ge(self.s[name], v)

    def selfw(self, eng, name):
        if self.c[name] > 0:
            eng.wait_ge(self.s[name], self.c[name])


import os
_LN_MODE = os.environ.get("LN_MODE", "ts2")
_LN_DELAY = bool(int(os.environ.get("LN_DELAY", "0")))


def ln_stats_batch(p, b, x_tm, st, ntc):
    def act(a):
        p.wait(a, "dve")
        for tc in range(ntc):
            p.selfw(a, "act")
            p.inc(a.activation(st["dump"][:], x_tm[:, tc, :], AF.Square,
                               bias=0.0, scale=1.0,
                               accum_out=st["sq"][:, tc:tc + 1]), "act")
    b.scalar(act)
    def dve(v):
        p.selfw(v, "dve")
        for tc in range(ntc):
            p.inc(v.reduce_sum(st["sum"][:, tc:tc + 1], x_tm[:, tc, :], AX.X), "dve")
        p.wait(v, "act")
        s = slice(0, ntc)
        p.selfw(v, "dve")
        p.inc(v.tensor_scalar(st["m"][:, s], st["sum"][:, s], 1.0 / D, 0.0,
                              ALU.mult, ALU.add), "dve")
        p.inc(v.tensor_scalar(st["ex2"][:, s], st["sq"][:, s], 1.0 / D, 0.0,
                              ALU.mult, ALU.add), "dve")
        p.selfw(v, "dve")
        p.inc(v.tensor_mul(st["msq"][:, s], st["m"][:, s], st["m"][:, s]), "dve")
        p.selfw(v, "dve")
        p.inc(v.tensor_sub(st["var"][:, s], st["ex2"][:, s], st["msq"][:, s]), "dve")
    b.vector(dve)
    def act2(a):
        p.wait(a, "dve")
        s = slice(0, ntc)
        p.inc(a.activation(st["lnv"][:, s], st["var"][:, s], AF.Ln,
                           bias=st["eps"][:], scale=1.0), "act")
        p.selfw(a, "act")
        p.inc(a.activation(st["rstd"][:, s], st["lnv"][:, s], AF.Exp,
                           bias=0.0, scale=st["mhalf"][:]), "act")
    b.scalar(act2)
    def dve2(v):
        p.wait(v, "act")
        s = slice(0, ntc)
        p.inc(v.tensor_mul(st["mh"][:, s], st["m"][:, s], st["rstd"][:, s]), "dve")
        p.selfw(v, "dve")
        p.inc(v.tensor_scalar(st["nmh"][:, s], st["mh"][:, s], -1.0, 0.0,
                              ALU.mult, ALU.add), "dve")
    b.vector(dve2)


def ln_apply_and_fm(p, b, x_tm, x_tmb, x_fmb, st, ntc):
    def dve(v):
        for tc in range(ntc):
            p.selfw(v, "dve")
            p.inc(v.tensor_scalar(x_tm[:, tc, :], x_tm[:, tc, :],
                                  st["rstd"][:, tc:tc + 1], st["nmh"][:, tc:tc + 1],
                                  ALU.mult, ALU.add), "dve")
            p.selfw(v, "dve")
            p.inc(v.tensor_copy(x_tmb[:, tc, :], x_tm[:, tc, :]), "dve")
    b.vector(dve)
    def sy(s):
        p.wait(s, "dve")
        for tc in range(ntc):
            for hc in range(NHC):
                p.selfw(s, "sy")
                p.inc(s.dma_start(
                    x_fmb[:, hc, tc * 128:(tc + 1) * 128],
                    x_tmb[:, tc, hc * 128:(hc + 1) * 128], transpose=True), "sy", 16)
    b.sync(sy)


def ln_tile(p, b, x_ap, st, slot):
    sl = slice(slot, slot + 1)
    def act(a):
        p.wait(a, "dve")
        p.inc(a.activation(st["dump"][:], x_ap, AF.Square, bias=0.0, scale=1.0,
                           accum_out=st["sq"][:, sl]), "act")
    b.scalar(act)
    def dve(v):
        p.selfw(v, "dve")
        p.inc(v.reduce_sum(st["sum"][:, sl], x_ap, AX.X), "dve")
        p.wait(v, "act")
        p.selfw(v, "dve")
        p.inc(v.tensor_scalar(st["m"][:, sl], st["sum"][:, sl], 1.0 / D, 0.0,
                              ALU.mult, ALU.add), "dve")
        p.inc(v.tensor_scalar(st["ex2"][:, sl], st["sq"][:, sl], 1.0 / D, 0.0,
                              ALU.mult, ALU.add), "dve")
        p.selfw(v, "dve")
        p.inc(v.tensor_mul(st["msq"][:, sl], st["m"][:, sl], st["m"][:, sl]), "dve")
        p.selfw(v, "dve")
        p.inc(v.tensor_sub(st["var"][:, sl], st["ex2"][:, sl], st["msq"][:, sl]), "dve")
    b.vector(dve)
    def act2(a):
        p.wait(a, "dve")
        p.inc(a.activation(st["lnv"][:, sl], st["var"][:, sl], AF.Ln,
                           bias=st["eps"][:], scale=1.0), "act")
        p.selfw(a, "act")
        p.inc(a.activation(st["rstd"][:, sl], st["lnv"][:, sl], AF.Exp,
                           bias=0.0, scale=st["mhalf"][:]), "act")
    b.scalar(act2)
    def dve2(v):
        p.wait(v, "act")
        p.inc(v.tensor_mul(st["mh"][:, sl], st["m"][:, sl], st["rstd"][:, sl]), "dve")
        p.selfw(v, "dve")
        p.inc(v.tensor_scalar(st["nmh"][:, sl], st["mh"][:, sl], -1.0, 0.0,
                              ALU.mult, ALU.add), "dve")
        p.selfw(v, "dve")
        p.inc(v.tensor_scalar(x_ap, x_ap, st["rstd"][:, sl], st["nmh"][:, sl],
                              ALU.mult, ALU.add), "dve")
    b.vector(dve2)


_BOOT = bool(int(os.environ.get("KBOOT", "1")))


def boot_clear(b, nc, p):
    if not _BOOT:
        return
    """Clear OUR kernel semaphores at program start (crashed programs leave
    stale values; a stale sem makes every wait_ge pass early). gpsimd clears;
    the other engines sit in a timed nop longer than the clear, then check a
    boot sem that the clear zeroed before it was incremented. Bass's own
    barrier sems are left alone (their protocol is self-normalizing)."""
    bs = nc.alloc_semaphore("bootsem")
    lo = min(h.num for h in list(p.s.values()) + [bs])
    rng_ = range(lo, nc._kernel_sem_range.stop)
    def gp(g):
        g.sem_clear(rng_)
        g.sem_inc(bs, 1)
    b.gpsimd(gp)
    for name in ("tensor", "vector", "scalar", "sync"):
        def w(e):
            e.nop(cycle_cnt=20000)
            e.wait_ge(bs, 1)
        getattr(b, name)(w)


def st_alloc(nc):
    st = {n: nc.alloc_sbuf_tensor("st_" + n, [128, NTC], F32)
          for n in ("sum", "sq", "m", "ex2", "msq", "var", "lnv", "rstd", "mh", "nmh")}
    st["dump"] = nc.alloc_sbuf_tensor("st_dump", [128, D], BF16)
    st["eps"] = nc.alloc_sbuf_tensor("st_eps", [128, 1], F32)
    st["mhalf"] = nc.alloc_sbuf_tensor("st_mhalf", [128, 1], F32)
    return st


def st_init(p, b, st):
    def dv(v):
        p.inc(v.memset(st["eps"][:], LN_EPS), "dve")
        p.inc(v.memset(st["mhalf"][:], -0.5), "dve")
    b.vector(dv)



class _Stop(Exception):
    pass


def _chk(upto, n):
    if upto < n:
        raise _Stop


def build_A(upto=99):
    nc = bass.Bass("TRN2", target_bir_lowering=False)
    xa = nc.dram_tensor("xa", [T, D], F32, kind="ExternalInput")
    xaT = nc.dram_tensor("xaT", [D, T], F32, kind="ExternalInput")
    qp = nc.dram_tensor("qp", [D, E], F32, kind="ExternalInput")
    kp = nc.dram_tensor("kp", [D, E], F32, kind="ExternalInput")
    gTd = nc.dram_tensor("gT", [D, NEXP], F32, kind="ExternalInput")
    ewT = nc.dram_tensor("ewT", [NEXP, D, D], F32, kind="ExternalInput")
    wqT = nc.dram_tensor("wqT", [D, HD], F32, kind="ExternalInput")
    wkT = nc.dram_tensor("wkT", [D, HD], F32, kind="ExternalInput")
    wvT = nc.dram_tensor("wvT", [D, HD], F32, kind="ExternalInput")
    bqd = nc.dram_tensor("bq", [HD, 1], F32, kind="ExternalInput")
    bkd = nc.dram_tensor("bk", [HD, 1], F32, kind="ExternalInput")
    bvd = nc.dram_tensor("bv", [HD, 1], F32, kind="ExternalInput")
    eyeb_d = nc.dram_tensor("eyeb", [128, 128], BF16, kind="ExternalInput")
    eyef_d = nc.dram_tensor("eyef", [128, 128], F32, kind="ExternalInput")
    sel8_d = nc.dram_tensor("sel8", [NEXP, NEXP * 128], BF16, kind="ExternalInput")

    x3_o = nc.dram_tensor("x3_o", [T, D], F32, kind="ExternalOutput")
    q_o = nc.dram_tensor("q_o", [HD, T], BF16, kind="ExternalOutput")
    k_o = nc.dram_tensor("k_o", [HD, T], BF16, kind="ExternalOutput")
    v_o = nc.dram_tensor("v_o", [T, HD], BF16, kind="ExternalOutput")
    qkm_o = nc.dram_tensor("qkm_o", [1, 2], F32, kind="ExternalOutput")

    x_tm = nc.alloc_sbuf_tensor("x_tm", [128, NTC, D], F32)
    x_tmb = nc.alloc_sbuf_tensor("x_tmb", [128, NTC, D], BF16)
    x_fmb = nc.alloc_sbuf_tensor("x_fmb", [128, NHC, T], BF16)
    qpb = nc.alloc_sbuf_tensor("qpb", [128, NHC, E], BF16)
    kpb = nc.alloc_sbuf_tensor("kpb", [128, NHC, E], BF16)
    wq_b = qpb[:, :, 0:HD]
    wk_b = qpb[:, :, HD:2 * HD]
    wv_b = qpb[:, :, 2 * HD:3 * HD]
    pool_a = nc.alloc_sbuf_tensor("pool_a", [128, 4096], BF16)
    qe = pool_a[:].rearrange("p (a t) -> p a t", a=2)
    mb_all = pool_a[:].rearrange("p (e t) -> p e t", e=NEXP)
    pool_b = nc.alloc_sbuf_tensor("pool_b", [128, 8192], BF16)
    ke = pool_b[:, 0:4096].rearrange("p (a e) -> p a e", a=NTC)
    moe_fm2 = pool_b[:].rearrange("p (s f t) -> p s f t", s=2, f=NEXP)
    kv_b = nc.alloc_sbuf_tensor("kv_b", [128, 2, D], BF16)
    zrow = nc.alloc_sbuf_tensor("zrow", [1, T], F32)
    z_tm = nc.alloc_sbuf_tensor("z_tm", [128, NTC], F32)
    ones_b = nc.alloc_sbuf_tensor("ones_b", [128, 1], BF16)
    onesr_b = nc.alloc_sbuf_tensor("onesr_b", [1, 128], BF16)
    eyeb = nc.alloc_sbuf_tensor("eyeb_t", [128, 128], BF16)
    eyef = nc.alloc_sbuf_tensor("eyef_t", [128, 128], F32)
    st = st_alloc(nc)
    gTb = nc.alloc_sbuf_tensor("gTb", [128, NHC, NEXP], BF16)
    scor = nc.alloc_sbuf_tensor("scor", [128, NTC, NEXP], F32)
    gmax = nc.alloc_sbuf_tensor("gmax", [128, NTC], F32)
    pre = nc.alloc_sbuf_tensor("pre", [128, NTC], F32)
    tta = nc.alloc_sbuf_tensor("tta", [128, NTC], F32)
    ttb = nc.alloc_sbuf_tensor("ttb", [128, NTC], F32)
    masks_tm = nc.alloc_sbuf_tensor("masks_tm", [128, NTC, NEXP], BF16)
    shr_a = nc.alloc_sbuf_tensor("shr_a", [HD, T], BF16)
    masks_f = shr_a[0:NEXP, :]
    mtile = nc.alloc_sbuf_tensor("mtile", [128, 512], BF16)
    w_moe = nc.alloc_sbuf_tensor("w_moe", [128, 2, 1024], BF16)
    moe_tm2 = nc.alloc_sbuf_tensor("moe_tm2", [128, 2, D], BF16)
    bq_t = nc.alloc_sbuf_tensor("bq_t", [HD, 1], F32)
    bk_t = nc.alloc_sbuf_tensor("bk_t", [HD, 1], F32)
    bv_t = nc.alloc_sbuf_tensor("bv_t", [HD, 1], F32)
    qf = nc.alloc_sbuf_tensor("qf", [HD, T], BF16)
    kf = nc.alloc_sbuf_tensor("kf", [HD, T], BF16)
    vf = nc.alloc_sbuf_tensor("vf", [HD, T], BF16)
    sqn = shr_a[:, :]
    v_tmb = nc.alloc_sbuf_tensor("v_tmb", [128, NTC, HD], BF16)
    qkm_t = nc.alloc_sbuf_tensor("qkm_t", [1, 2], F32)
    tmp_res = nc.alloc_sbuf_tensor("tmp_res", [128, 512], F32)
    sel8 = tmp_res[:].bitcast(BF16)[0:NEXP, :]

    PB = [nc.alloc_psum_tensor(f"pb{i}", [128, 512], F32) for i in range(8)]
    PBb = [t[:].bitcast(BF16) for t in PB]

    p = Prog(nc)
    p.sems("sy", "gp", "pe", "act", "dve")

    with nc.Block() as b:
      try:
        boot_clear(b, nc, p)
        def sy0(s):
            p.inc(s.dma_start(x_tm[:], xa[:].rearrange("(c p) h -> p c h", p=128)), "sy", 16)
            p.inc(s.dma_start(eyeb[:], eyeb_d[:]), "sy", 16)
            p.inc(s.dma_start(eyef[:], eyef_d[:]), "sy", 16)
            p.inc(s.dma_start(bq_t[:], bqd[:]), "sy", 16)
            p.inc(s.dma_start(bk_t[:], bkd[:]), "sy", 16)
            p.inc(s.dma_start(bv_t[:], bvd[:]), "sy", 16)
        b.sync(sy0)
        def gp0(g):
            p.inc(g.dma_start(x_tmb[:], xa[:].rearrange("(c p) h -> p c h", p=128)), "gp", 16)
            p.inc(g.dma_start(x_fmb[:], xaT[:].rearrange("(c p) t -> p c t", p=128)), "gp", 16)
            p.inc(g.dma_start(qpb[:], qp[:].rearrange("(c p) e -> p c e", p=128)), "gp", 16)
            p.inc(g.dma_start(kpb[:], kp[:].rearrange("(c p) e -> p c e", p=128)), "gp", 16)
            p.inc(g.dma_start(gTb[:], gTd[:].rearrange("(c p) e -> p c e", p=128)), "gp", 16)
        b.gpsimd(gp0)
        def dv0(v):
            p.inc(v.memset(ones_b[:], 1.0), "dve")
            p.inc(v.memset(onesr_b[:], 1.0), "dve")
        b.vector(dv0)
        st_init(p, b, st)
        gp_ld = p.c["gp"]

        _chk(upto, 2)
        _sc_a = nc.named_scope("A_l0attn"); _sc_a.__enter__()
        act0 = p.c["act"]
        qpe, kpe = [], []
        def pe_qk(t):
            p.wait(t, "gp", gp_ld)
            p.wait(t, "sy")
            for ec in range(2):
                for tn in range(4):
                    g = ec * 4 + tn
                    if g >= 2:
                        t.wait_ge(p.s["act"], act0 + (g - 2) + 1)
                    bank = PB[g % 2]
                    for hc in range(NHC):
                        mm = t.matmul(bank[:], qpb[:, hc, ec * 128:(ec + 1) * 128],
                                      x_fmb[:, hc, tn * 512:(tn + 1) * 512],
                                      start=(hc == 0), stop=(hc == NHC - 1))
                    qpe.append(p.inc(mm, "pe"))
            for tc in range(NTC):
                if tc >= 2:
                    t.wait_ge(p.s["act"], act0 + 8 + (tc - 2) + 1)
                bank = PB[2 + tc % 2]
                for hc in range(NHC):
                    mm = t.matmul(bank[:, 0:E], x_fmb[:, hc, tc * 128:(tc + 1) * 128],
                                  kpb[:, hc, :], start=(hc == 0), stop=(hc == NHC - 1))
                kpe.append(p.inc(mm, "pe"))
        b.tensor(pe_qk)
        def act_qk(a):
            for ec in range(2):
                for tn in range(4):
                    g = ec * 4 + tn
                    p.wait(a, "pe", qpe[g])
                    p.selfw(a, "act")
                    p.inc(a.activation(qe[:, ec, tn * 512:(tn + 1) * 512],
                                       PB[g % 2][:], AF.Exp, bias=0.0, scale=1.0), "act")
            for tc in range(NTC):
                p.wait(a, "pe", kpe[tc])
                p.selfw(a, "act")
                p.inc(a.activation(ke[:, tc, :], PB[2 + tc % 2][:, 0:E],
                                   AF.Exp, bias=0.0, scale=1.0), "act")
        b.scalar(act_qk)

        _chk(upto, 3)
        dve0 = p.c["dve"]
        zpe = []
        def pe_z(t):
            p.wait(t, "act")
            t.wait_ge(p.s["dve"], 2)
            for tn in range(4):
                if tn >= 1:
                    t.wait_ge(p.s["dve"], dve0 + tn)
                for ec in range(2):
                    mm = t.matmul(PB[4][0:1, :], ones_b[:, 0:1],
                                  qe[:, ec, tn * 512:(tn + 1) * 512],
                                  start=(ec == 0), stop=(ec == 1))
                zpe.append(p.inc(mm, "pe"))
        b.tensor(pe_z)
        def dv_z(v):
            for tn in range(4):
                p.wait(v, "pe", zpe[tn])
                p.inc(v.tensor_copy(zrow[0:1, tn * 512:(tn + 1) * 512], PB[4][0:1, :]),
                      "dve")
            p.selfw(v, "dve")
            p.inc(v.tensor_single_scalar(zrow[:], zrow[:], 1e-8, ALU.add), "dve")
            p.selfw(v, "dve")
            p.inc(v.reciprocal(zrow[:], zrow[:]), "dve")
        b.vector(dv_z)
        dve1 = p.c["dve"]
        ztr = []
        def pe_zt(t):
            p.wait(t, "dve", dve1)
            for tc in range(NTC):
                if tc >= 2:
                    t.wait_ge(p.s["dve"], dve1 + (tc - 2) + 1)
                tr = t.transpose(PB[4 + tc % 2][:, 0:1],
                                 zrow[0:1, tc * 128:(tc + 1) * 128], eyef[0:1, 0:1])
                ztr.append(p.inc(tr, "pe"))
        b.tensor(pe_zt)
        def dv_zt(v):
            for tc in range(NTC):
                p.wait(v, "pe", ztr[tc])
                p.inc(v.tensor_copy(z_tm[:, tc:tc + 1], PB[4 + tc % 2][:, 0:1]), "dve")
        b.vector(dv_zt)

        _chk(upto, 4)
        dve2 = p.c["dve"]
        kvpe = []
        def pe_kv(t):
            for g in range(4):
                ec, hn = divmod(g, 2)
                if g >= 2:
                    t.wait_ge(p.s["dve"], dve2 + (g - 2) + 1)
                bank = PB[6 + g % 2]
                for tc in range(NTC):
                    mm = t.matmul(bank[:], ke[:, tc, ec * 128:(ec + 1) * 128],
                                  x_tmb[:, tc, hn * 512:(hn + 1) * 512],
                                  start=(tc == 0), stop=(tc == NTC - 1))
                kvpe.append(p.inc(mm, "pe"))
        b.tensor(pe_kv)
        def dv_kv(v):
            for g in range(4):
                ec, hn = divmod(g, 2)
                p.wait(v, "pe", kvpe[g])
                p.inc(v.tensor_copy(kv_b[:, ec, hn * 512:(hn + 1) * 512],
                                    PB[6 + g % 2][:]), "dve")
        b.vector(dv_kv)

        _chk(upto, 5)
        act1 = p.c["act"]
        numpe = []
        def pe_num(t):
            p.wait(t, "dve")
            for g in range(NTC * 2):
                tc, hn = divmod(g, 2)
                if g >= 2:
                    t.wait_ge(p.s["act"], act1 + (g - 2) + 1)
                bank = PB[g % 2]
                for ec in range(2):
                    mm = t.matmul(bank[:], qe[:, ec, tc * 128:(tc + 1) * 128],
                                  kv_b[:, ec, hn * 512:(hn + 1) * 512],
                                  start=(ec == 0), stop=(ec == 1))
                numpe.append(p.inc(mm, "pe"))
        b.tensor(pe_num)
        dve3 = p.c["dve"]
        numact = []
        def act_num(a):
            for g in range(NTC * 2):
                tc, hn = divmod(g, 2)
                p.wait(a, "pe", numpe[g])
                if g >= 1:
                    a.wait_ge(p.s["dve"], dve3 + g)
                p.selfw(a, "act")
                numact.append(p.inc(a.activation(tmp_res[:], PB[g % 2][:],
                                                 AF.Copy, bias=0.0,
                                                 scale=z_tm[:, tc:tc + 1]), "act"))
        b.scalar(act_num)
        def dv_num(v):
            for g in range(NTC * 2):
                tc, hn = divmod(g, 2)
                p.wait(v, "act", numact[g])
                p.selfw(v, "dve")
                p.inc(v.tensor_add(x_tm[:, tc, hn * 512:(hn + 1) * 512],
                                   x_tm[:, tc, hn * 512:(hn + 1) * 512],
                                   tmp_res[:]), "dve")
        b.vector(dv_num)

        _sc_a.__exit__(None, None, None)
        _chk(upto, 6)
        _sc_b = nc.named_scope("A_ln1"); _sc_b.__enter__()
        ln_stats_batch(p, b, x_tm, st, NTC)
        ln_apply_and_fm(p, b, x_tm, x_tmb, x_fmb, st, NTC)

        _sc_b.__exit__(None, None, None)
        _chk(upto, 7)
        _sc_c = nc.named_scope("A_moe"); _sc_c.__enter__()
        scpe = []
        dve4 = p.c["dve"]
        def pe_sc(t):
            p.wait(t, "sy")
            for tc in range(NTC):
                if tc >= 2:
                    t.wait_ge(p.s["dve"], dve4 + (tc - 2) + 1)
                bank = PB[2 + tc % 2]
                for hc in range(NHC):
                    mm = t.matmul(bank[:, 0:NEXP], x_fmb[:, hc, tc * 128:(tc + 1) * 128],
                                  gTb[:, hc, :], start=(hc == 0), stop=(hc == NHC - 1))
                scpe.append(p.inc(mm, "pe"))
        b.tensor(pe_sc)
        def dv_sc(v):
            for tc in range(NTC):
                p.wait(v, "pe", scpe[tc])
                p.inc(v.tensor_copy(scor[:, tc, :], PB[2 + tc % 2][:, 0:NEXP]), "dve")
            p.selfw(v, "dve")
            p.inc(v.reduce_max(gmax[:], scor[:], AX.X), "dve")
            for e in range(NEXP):
                se = scor[:, :, e]
                p.selfw(v, "dve")
                p.inc(v.tensor_tensor(tta[:], se, gmax[:], ALU.is_ge), "dve")
                if e == 0:
                    p.selfw(v, "dve")
                    p.inc(v.tensor_copy(masks_tm[:, :, 0], tta[:]), "dve")
                    p.inc(v.tensor_copy(pre[:], se), "dve")
                else:
                    p.selfw(v, "dve")
                    p.inc(v.tensor_tensor(ttb[:], pre[:], gmax[:], ALU.is_lt), "dve")
                    p.selfw(v, "dve")
                    p.inc(v.tensor_mul(tta[:], tta[:], ttb[:]), "dve")
                    p.selfw(v, "dve")
                    p.inc(v.tensor_copy(masks_tm[:, :, e], tta[:]), "dve")
                    if e < NEXP - 1:
                        p.selfw(v, "dve")
                        p.inc(v.tensor_max(pre[:], pre[:], se), "dve")
        b.vector(dv_sc)
        _chk(upto, 8)
        dve5 = p.c["dve"]
        mtr = []
        def pe_mt(t):
            p.wait(t, "dve", dve5)
            for tc in range(NTC):
                if tc >= 2:
                    t.wait_ge(p.s["dve"], dve5 + (tc - 2) + 1)
                tr = t.transpose(PBb[4 + tc % 2][0:NEXP, 0:128], masks_tm[:, tc, :],
                                 eyeb[:, :])
                mtr.append(p.inc(tr, "pe"))
        b.tensor(pe_mt)
        def dv_mt(v):
            for tc in range(NTC):
                p.wait(v, "pe", mtr[tc])
                p.inc(v.tensor_copy(masks_f[0:NEXP, tc * 128:(tc + 1) * 128],
                                    PBb[4 + tc % 2][0:NEXP, 0:128]), "dve")
        b.vector(dv_mt)

        _chk(upto, 9)
        def sy_sel(s):
            p.wait(s, "dve")
            p.inc(s.dma_start(sel8, sel8_d[:]), "sy", 16)
        b.sync(sy_sel)

        for tn in range(4):
            for e in range(NEXP):
                def pe_mb(t, e=e, tn=tn):
                    p.wait(t, "dve")
                    p.wait(t, "sy")
                    p.inc(t.matmul(PB[6][:], sel8[:, e * 128:(e + 1) * 128],
                                   masks_f[0:NEXP, tn * 512:(tn + 1) * 512],
                                   start=True, stop=True), "pe")
                b.tensor(pe_mb)
                def dv_mb(v, e=e):
                    p.wait(v, "pe")
                    p.selfw(v, "dve")
                    p.inc(v.tensor_copy(mb_all[:, e, :], PB[6][:]), "dve")
                b.vector(dv_mb)
            pe_moe0 = p.c["pe"]
            gp_moe0 = p.c["gp"]
            dve_moe0 = p.c["dve"]
            def gp_w(g, tn=tn):
                for i in range(NEXP * NHC):
                    e, hc = divmod(i, NHC)
                    if i >= 2:
                        g.wait_ge(p.s["pe"], pe_moe0 + i)
                    elif pe_moe0 > 0:
                        g.wait_ge(p.s["pe"], pe_moe0)
                    p.inc(g.dma_start(w_moe[:, i % 2, :],
                                      ewT[e, hc * 128:(hc + 1) * 128, :]), "gp", 16)
            b.gpsimd(gp_w)
            def dv_mm(v, tn=tn):
                for i in range(NEXP * NHC):
                    e, hc = divmod(i, NHC)
                    if i >= 1:
                        v.wait_ge(p.s["pe"], pe_moe0 + i)
                    p.selfw(v, "dve")
                    p.inc(v.tensor_tensor(mtile[:], x_fmb[:, hc, tn * 512:(tn + 1) * 512],
                                          mb_all[:, e, :], ALU.mult), "dve")
            b.vector(dv_mm)
            def pe_moe(t, tn=tn):
                for i in range(NEXP * NHC):
                    e, hc = divmod(i, NHC)
                    t.wait_ge(p.s["gp"], gp_moe0 + (i + 1) * 16)
                    t.wait_ge(p.s["dve"], dve_moe0 + i + 1)
                    for fc in range(8):
                        mm = t.matmul(PB[fc][:], w_moe[:, i % 2, fc * 128:(fc + 1) * 128],
                                      mtile[:], start=(i == 0),
                                      stop=(i == NEXP * NHC - 1))
                    p.inc(mm, "pe")
            b.tensor(pe_moe)
            slot = tn % 2
            def dv_ev(v, tn=tn, slot=slot):
                p.wait(v, "pe")
                p.wait(v, "sy")
                p.selfw(v, "dve")
                for fc in range(8):
                    p.inc(v.tensor_copy(moe_fm2[:, slot, fc, :], PB[fc][:]), "dve")
            b.vector(dv_ev)
            for tj in range(4):
                tc = tn * 4 + tj
                def sy_tr(s, tn=tn, slot=slot, tj=tj, tc=tc):
                    p.wait(s, "dve")
                    for hc in range(NHC):
                        p.selfw(s, "sy")
                        p.inc(s.dma_start(
                            moe_tm2[:, tc % 2, hc * 128:(hc + 1) * 128],
                            moe_fm2[:, slot, hc, tj * 128:(tj + 1) * 128],
                            transpose=True), "sy", 16)
                b.sync(sy_tr)
                def dv_res(v, tc=tc):
                    p.wait(v, "sy")
                    p.selfw(v, "dve")
                    p.inc(v.tensor_scalar(x_tm[:, tc, :], x_tm[:, tc, :], 2.0, 0.0,
                                          ALU.mult, ALU.add), "dve")
                    p.selfw(v, "dve")
                    p.inc(v.tensor_add(x_tm[:, tc, :], x_tm[:, tc, :],
                                       moe_tm2[:, tc % 2, :]), "dve")
                b.vector(dv_res)


        _sc_c.__exit__(None, None, None)
        _chk(upto, 10)
        _sc_d = nc.named_scope("A_ln2_out"); _sc_d.__enter__()
        ln_stats_batch(p, b, x_tm, st, NTC)
        ln_apply_and_fm(p, b, x_tm, x_tmb, x_fmb, st, NTC)

        def sy_x3(s):
            p.wait(s, "dve")
            p.inc(s.dma_start(x3_o[:].rearrange("(c p) h -> p c h", p=128), x_tm[:]),
                  "sy", 16)
        b.sync(sy_x3)

        _chk(upto, 11)
        def gp_w2(g):
            p.wait(g, "pe")
            p.inc(g.dma_start(wq_b, wqT[:].rearrange("(c p) e -> p c e", p=128)), "gp", 16)
            p.inc(g.dma_start(wk_b, wkT[:].rearrange("(c p) e -> p c e", p=128)), "gp", 16)
            p.inc(g.dma_start(wv_b, wvT[:].rearrange("(c p) e -> p c e", p=128)), "gp", 16)
        b.gpsimd(gp_w2)
        gp_ld2 = p.c["gp"]
        act2c = p.c["act"]
        prpe = []
        def pe_pr(t):
            p.wait(t, "sy")
            p.wait(t, "gp", gp_ld2)
            for j, wt in enumerate((wq_b, wk_b, wv_b)):
                for tn in range(4):
                    g = j * 4 + tn
                    if g >= 2:
                        t.wait_ge(p.s["act"], act2c + (g - 2) + 1)
                    bank = PB[g % 2]
                    for hc in range(NHC):
                        mm = t.matmul(bank[0:HD, :], wt[:, hc, :],
                                      x_fmb[:, hc, tn * 512:(tn + 1) * 512],
                                      start=(hc == 0), stop=(hc == NHC - 1))
                    prpe.append(p.inc(mm, "pe"))
        b.tensor(pe_pr)
        def act_pr(a):
            outs = (qf, kf, vf)
            biases = (bq_t, bk_t, bv_t)
            for j in range(3):
                for tn in range(4):
                    g = j * 4 + tn
                    p.wait(a, "pe", prpe[g])
                    p.selfw(a, "act")
                    p.inc(a.activation(outs[j][:, tn * 512:(tn + 1) * 512],
                                       PB[g % 2][0:HD, :], AF.Identity,
                                       bias=biases[j][:], scale=1.0), "act")
        b.scalar(act_pr)

        _chk(upto, 12)
        dve6 = p.c["dve"]
        vtr = []
        def pe_vt(t):
            p.wait(t, "act")
            for tc in range(NTC):
                if tc >= 2:
                    t.wait_ge(p.s["dve"], dve6 + (tc - 2) + 1)
                tr = t.transpose(PBb[2 + tc % 2][0:128, 0:HD],
                                 vf[0:HD, tc * 128:(tc + 1) * 128], eyeb[0:HD, 0:HD])
                vtr.append(p.inc(tr, "pe"))
        b.tensor(pe_vt)
        def dv_vt(v):
            for tc in range(NTC):
                p.wait(v, "pe", vtr[tc])
                p.inc(v.tensor_copy(v_tmb[:, tc, :], PBb[2 + tc % 2][0:128, 0:HD]),
                      "dve")
        b.vector(dv_vt)

        _chk(upto, 13)
        dve7 = p.c["dve"]
        nrmpe = []
        def dv_n1(v):
            p.wait(v, "act")
            p.selfw(v, "dve")
            p.inc(v.tensor_mul(sqn[:], qf[:], qf[:]), "dve")
        b.vector(dv_n1)
        def pe_nrm(t):
            for j in range(2):
                for tn in range(4):
                    g = j * 4 + tn
                    if g == 0:
                        t.wait_ge(p.s["dve"], dve7 + 1)
                    elif j == 0:
                        t.wait_ge(p.s["dve"], dve7 + 1 + g)
                    elif tn == 0:
                        t.wait_ge(p.s["dve"], dve7 + 7)
                    else:
                        t.wait_ge(p.s["dve"], dve7 + 7 + tn)
                    mm = t.matmul(PB[4][0:1, :], ones_b[0:HD, 0:1],
                                  sqn[:, tn * 512:(tn + 1) * 512], start=True, stop=True)
                    nrmpe.append(p.inc(mm, "pe"))
        b.tensor(pe_nrm)
        def dv_n2(v):
            for tn in range(4):
                p.wait(v, "pe", nrmpe[tn])
                p.selfw(v, "dve")
                p.inc(v.tensor_copy(zrow[0:1, tn * 512:(tn + 1) * 512], PB[4][0:1, :]),
                      "dve")
            p.selfw(v, "dve")
            p.inc(v.tensor_reduce(qkm_t[0:1, 0:1], zrow[:], AX.X, ALU.max), "dve")
            p.selfw(v, "dve")
            p.inc(v.tensor_mul(sqn[:], kf[:], kf[:]), "dve")
            for tn in range(4):
                p.wait(v, "pe", nrmpe[4 + tn])
                p.selfw(v, "dve")
                p.inc(v.tensor_copy(zrow[0:1, tn * 512:(tn + 1) * 512], PB[4][0:1, :]),
                      "dve")
            p.selfw(v, "dve")
            p.inc(v.tensor_reduce(qkm_t[0:1, 1:2], zrow[:], AX.X, ALU.max), "dve")
        b.vector(dv_n2)

        _chk(upto, 14)
        def sy_out(s):
            p.wait(s, "dve")
            p.wait(s, "act")
            p.inc(s.dma_start(q_o[:], qf[:]), "sy", 16)
            p.inc(s.dma_start(k_o[:], kf[:]), "sy", 16)
            p.inc(s.dma_start(v_o[:].rearrange("(c p) d -> p c d", p=128), v_tmb[:]),
                  "sy", 16)
            p.inc(s.dma_start(qkm_o[:], qkm_t[:]), "sy", 16)
            p.wait(s, "sy")
        b.sync(sy_out)
        _sc_d.__exit__(None, None, None)
      except _Stop:
        def sy_tail(s):
            p.wait(s, "dve")
            p.wait(s, "act")
            p.wait(s, "pe")
            p.inc(s.dma_start(x3_o[:].rearrange("(c p) h -> p c h", p=128),
                              x_tm[:]), "sy", 16)
            p.inc(s.dma_start(q_o[:], qf[:]), "sy", 16)
            p.wait(s, "sy")
        b.sync(sy_tail)
    return nc



def build_B():
    nc = bass.Bass("TRN2", target_bir_lowering=False)
    x3 = nc.dram_tensor("x3", [T, D], F32, kind="ExternalInput")
    qfd = nc.dram_tensor("qf", [HD, T], BF16, kind="ExternalInput")
    kfull = nc.dram_tensor("kfull", [HD, NCORE * T], BF16, kind="ExternalInput")
    vfull = nc.dram_tensor("vfull", [NCORE * T, HD], BF16, kind="ExternalInput")
    qkmd = nc.dram_tensor("qkm", [1, 16], F32, kind="ExternalInput")
    w1T = nc.dram_tensor("w1T", [32, 128, NHC, 128], F32, kind="ExternalInput")
    w2T = nc.dram_tensor("w2T", [32, 128, NHC, 128], F32, kind="ExternalInput")
    h1T = nc.dram_tensor("h1T", [32, 128, 32, 128], F32, kind="ExternalInput")
    h2T = nc.dram_tensor("h2T", [32, 128, 32, 128], F32, kind="ExternalInput")
    w3T = nc.dram_tensor("w3T", [DH, D], F32, kind="ExternalInput")
    s1d = nc.dram_tensor("s1d", [128, 32], F32, kind="ExternalInput")
    s2d = nc.dram_tensor("s2d", [128, 32], F32, kind="ExternalInput")
    h1sd = nc.dram_tensor("h1sd", [128, 32], F32, kind="ExternalInput")
    h2sd = nc.dram_tensor("h2sd", [128, 32], F32, kind="ExternalInput")
    s3d = nc.dram_tensor("s3d", [1, D], F32, kind="ExternalInput")
    eyef_d = nc.dram_tensor("eyef", [128, 128], F32, kind="ExternalInput")
    out_o = nc.dram_tensor("out_o", [T, D], F32, kind="ExternalOutput")
    x4_d = nc.dram_tensor("x4_d", [T, D], F32, kind="Internal")

    NK = (NCORE * T) // 128

    arena = nc.alloc_sbuf_tensor("arena", [128, 65536], BF16)
    cur = [0]
    def carve(n):
        cur[0] = (cur[0] + 15) // 16 * 16
        off = cur[0]
        cur[0] += n
        return off
    o_k = carve(16384)
    k_sb = arena[:, o_k:o_k + 16384][0:HD, :]
    o_v = carve(NK * 65 + 7)
    v65 = arena[:, o_v:o_v + NK * 65].rearrange("p (c d) -> p c d", d=65)
    o_pt = carve(2048)
    pt_a = arena[:, o_pt:o_pt + 2048].rearrange("p (s t) -> p s t", s=2)
    o_y = carve(2048)
    y_sb = arena[:, o_y:o_y + 2048].bitcast(F32)
    o_q = carve(2048)
    qf_b = arena[:, o_q:o_q + 2048][0:HD, :]
    o_x3 = carve(4096)
    x3t = arena[:, o_x3:o_x3 + 4096].bitcast(F32).rearrange(
        "p (s h) -> p s h", s=2)
    h_a = arena[:, 0:32768].rearrange("p (c t) -> p c t", c=32)
    h_b = arena[:, 32768:65536].rearrange("p (c t) -> p c t", c=32)

    x_fmb = nc.alloc_sbuf_tensor("x_fmb", [128, NHC, T], BF16)
    x4t = nc.alloc_sbuf_tensor("x4t", [128, 1, D], F32)
    xtmb = nc.alloc_sbuf_tensor("xtmb", [128, 1, D], BF16)
    yT_sb = nc.alloc_sbuf_tensor("yT_sb", [128, 65], F32)
    z1 = nc.alloc_sbuf_tensor("z1", [128, 1], F32)
    ybar = nc.alloc_sbuf_tensor("ybar", [128, HD], F32)
    st = st_alloc(nc)
    qkm_t = nc.alloc_sbuf_tensor("qkm_t", [1, 16], F32)
    csc = nc.alloc_sbuf_tensor("cs", [1, 4], F32)
    cb = nc.alloc_sbuf_tensor("cb", [128, 1], F32)
    ncb = nc.alloc_sbuf_tensor("ncb", [128, 1], F32)
    inv32 = nc.alloc_sbuf_tensor("inv32", [128, 1], F32)
    onesf = nc.alloc_sbuf_tensor("onesf", [1, 128], F32)
    eyef = nc.alloc_sbuf_tensor("eyef_t", [128, 128], F32)
    s1_t = nc.alloc_sbuf_tensor("s1_t", [128, 32], F32)
    s2_t = nc.alloc_sbuf_tensor("s2_t", [128, 32], F32)
    h1s_t = nc.alloc_sbuf_tensor("h1s_t", [128, 32], F32)
    h2s_t = nc.alloc_sbuf_tensor("h2s_t", [128, 32], F32)
    s3r = nc.alloc_sbuf_tensor("s3r", [1, D], F32)
    s3_bb = nc.alloc_sbuf_tensor("s3_bb", [128, D], F32)
    hw_b = nc.alloc_sbuf_tensor("hw_b", [128, 2, 32, 128], BF16)
    w1_b = hw_b[:, :, 0:NHC, :]
    w2_b = hw_b[:, :, NHC:2 * NHC, :]
    w3_b = hw_b[:].rearrange("p s c f -> p s (c f)")[:, :, 0:1024]
    silu_s = nc.alloc_sbuf_tensor("silu_s", [128, 1024], F32)
    t2_s = nc.alloc_sbuf_tensor("t2_s", [128, 1024], F32)
    o_sb = silu_s[:]

    psSA = nc.alloc_psum_tensor("psSA", [128, 1024], F32)
    psSB = nc.alloc_psum_tensor("psSB", [128, 1024], F32)
    psY = nc.alloc_psum_tensor("psY", [128, 1024], F32)
    psC = nc.alloc_psum_tensor("psC", [128, 512], F32)
    psT = nc.alloc_psum_tensor("psT", [128, 512], F32)

    p = Prog(nc)
    p.sems("sy", "gp", "pe", "act", "dve")

    with nc.Block() as b:
        boot_clear(b, nc, p)
        def sy0(s):
            p.inc(s.dma_start(k_sb[:, :], kfull[:]), "sy", 16)
            p.inc(s.dma_start(qf_b[:, :], qfd[:]), "sy", 16)
            p.inc(s.dma_start(qkm_t[:], qkmd[:]), "sy", 16)
            p.inc(s.dma_start(eyef[:], eyef_d[:]), "sy", 16)
            p.inc(s.dma_start(s1_t[:], s1d[:]), "sy", 16)
            p.inc(s.dma_start(s2_t[:], s2d[:]), "sy", 16)
            p.inc(s.dma_start(h1s_t[:], h1sd[:]), "sy", 16)
            p.inc(s.dma_start(h2s_t[:], h2sd[:]), "sy", 16)
            p.inc(s.dma_start(s3r[:], s3d[:]), "sy", 16)
        b.sync(sy0)
        def dv0(v):
            p.inc(v.memset(onesf[:], 1.0), "dve")
            p.inc(v.memset(v65[:, :, 64:65], 1.0), "dve")
            p.inc(v.memset(inv32[:], 1.0 / 32.0), "dve")
        b.vector(dv0)
        st_init(p, b, st)
        def sy1(s):
            p.wait(s, "dve")
            p.inc(s.dma_start(v65[:, :, 0:64],
                              vfull[:].rearrange("(c p) d -> p c d", p=128)), "sy", 16)
        b.sync(sy1)
        def dv_c(v):
            p.wait(v, "sy")
            p.inc(v.tensor_reduce(csc[0:1, 0:2],
                                  qkm_t[:].rearrange("o (a c) -> o a c", a=2),
                                  AX.X, ALU.max), "dve")
            p.selfw(v, "dve")
            p.inc(v.tensor_add(csc[0:1, 2:3], csc[0:1, 0:1], csc[0:1, 1:2]), "dve")
            p.selfw(v, "dve")
            p.inc(v.tensor_single_scalar(csc[0:1, 3:4], csc[0:1, 2:3], 1.0 / 64.0,
                                         ALU.mult), "dve")
        b.vector(dv_c)
        def pe_c(t):
            p.wait(t, "dve")
            p.inc(t.matmul(psC[:, 0:1], onesf[0:1, :], csc[0:1, 3:4],
                           start=True, stop=True), "pe")
        b.tensor(pe_c)
        def dv_c2(v):
            p.wait(v, "pe")
            p.inc(v.tensor_copy(cb[:], psC[:, 0:1]), "dve")
            p.selfw(v, "dve")
            p.inc(v.tensor_single_scalar(ncb[:], cb[:], -1.0, ALU.mult), "dve")
        b.vector(dv_c2)

        _sc_e = nc.named_scope("B_attn"); _sc_e.__enter__()
        for th in range(2):
            toff = th * 1024
            act_b = p.c["act"]
            spe, vpe = [], []
            def pe_at(t, toff=toff, act_b=act_b, spe=spe, vpe=vpe):
                p.wait(t, "sy")
                p.wait(t, "dve")
                for kc in range(NK):
                    bank = psSA if kc % 2 == 0 else psSB
                    if kc >= 2:
                        t.wait_ge(p.s["act"], act_b + (kc - 2) + 1)
                    for nn in range(2):
                        mm = t.matmul(bank[:, nn * 512:(nn + 1) * 512],
                                      k_sb[:, kc * 128:(kc + 1) * 128],
                                      qf_b[:, toff + nn * 512:toff + (nn + 1) * 512],
                                      start=True, stop=True)
                    spe.append(p.inc(mm, "pe"))
                    if kc >= 1:
                        pk = kc - 1
                        t.wait_ge(p.s["act"], act_b + pk + 1)
                        for nn in range(2):
                            mm = t.matmul(psY[0:65, nn * 512:(nn + 1) * 512],
                                          v65[:, pk, :],
                                          pt_a[:, pk % 2, nn * 512:(nn + 1) * 512],
                                          start=(pk == 0), stop=False)
                        vpe.append(p.inc(mm, "pe"))
                t.wait_ge(p.s["act"], act_b + NK)
                for nn in range(2):
                    mm = t.matmul(psY[0:65, nn * 512:(nn + 1) * 512], v65[:, NK - 1, :],
                                  pt_a[:, (NK - 1) % 2, nn * 512:(nn + 1) * 512],
                                  start=False, stop=True)
                vpe.append(p.inc(mm, "pe"))
            b.tensor(pe_at)
            def act_at(a, spe=spe, vpe=vpe):
                for kc in range(NK):
                    p.wait(a, "pe", spe[kc])
                    if kc >= 2:
                        p.wait(a, "pe", vpe[kc - 2])
                    bank = psSA if kc % 2 == 0 else psSB
                    p.selfw(a, "act")
                    p.inc(a.activation(pt_a[:, kc % 2, :], bank[:], AF.Exp,
                                       bias=ncb[:], scale=inv32[:]), "act")
            b.scalar(act_at)
            def dv_y(v):
                p.wait(v, "pe")
                p.inc(v.tensor_copy(y_sb[0:65, :], psY[0:65, :]), "dve")
            b.vector(dv_y)

            for tj in range(8):
                tc = th * 8 + tj
                sl = 0
                def sy_ld(s, tc=tc, sl=sl):
                    p.wait(s, "dve")
                    p.inc(s.dma_start(x3t[:, sl, :], x3[tc * 128:(tc + 1) * 128, :]),
                          "sy", 16)
                b.sync(sy_ld)
                def pe_t(t, tj=tj):
                    p.wait(t, "dve")
                    p.inc(t.transpose(psT[:, 0:65],
                                      y_sb[0:65, tj * 128:(tj + 1) * 128],
                                      eyef[0:65, 0:65]), "pe")
                b.tensor(pe_t)
                def dv_t(v, tc=tc, sl=sl):
                    p.wait(v, "pe")
                    p.wait(v, "sy")
                    p.selfw(v, "dve")
                    p.inc(v.tensor_copy(yT_sb[:], psT[:, 0:65]), "dve")
                    p.selfw(v, "dve")
                    p.inc(v.reciprocal(z1[:], yT_sb[:, 64:65]), "dve")
                    p.selfw(v, "dve")
                    p.inc(v.tensor_scalar(ybar[:], yT_sb[:, 0:64], z1[:], 0.0,
                                          ALU.mult, ALU.add), "dve")
                    p.selfw(v, "dve")
                    p.inc(v.tensor_tensor(
                        x4t[:, sl, :].rearrange("p (r d) -> p r d", r=16),
                        x3t[:, sl, :].rearrange("p (r d) -> p r d", r=16),
                        ybar[:].unsqueeze(1).broadcast_to([128, 16, HD]), ALU.add),
                        "dve")
                b.vector(dv_t)
                ln_tile(p, b, x4t[:, sl, :], st, tc % NTC)
                def dv_cp(v, sl=sl):
                    p.wait(v, "sy")
                    p.selfw(v, "dve")
                    p.inc(v.tensor_copy(xtmb[:, sl, :], x4t[:, sl, :]), "dve")
                b.vector(dv_cp)
                def sy_st(s, tc=tc, sl=sl):
                    p.wait(s, "dve")
                    p.inc(s.dma_start(x4_d[tc * 128:(tc + 1) * 128, :], x4t[:, sl, :]),
                          "sy", 16)
                    for hc in range(NHC):
                        p.selfw(s, "sy")
                        p.inc(s.dma_start(x_fmb[:, hc, tc * 128:(tc + 1) * 128],
                                          xtmb[:, sl, hc * 128:(hc + 1) * 128],
                                          transpose=True), "sy", 16)
                b.sync(sy_st)

        _sc_e.__exit__(None, None, None)
        _sc_f = nc.named_scope("B_ffn"); _sc_f.__enter__()
        def pe_s3(t):
            p.wait(t, "dve")
            p.inc(t.matmul(psC[:, :], onesf[0:1, :], s3r[0:1, 0:512],
                           start=True, stop=True), "pe")
            p.inc(t.matmul(psT[:, :], onesf[0:1, :], s3r[0:1, 512:1024],
                           start=True, stop=True), "pe")
        b.tensor(pe_s3)
        def dv_s3(v):
            p.wait(v, "pe")
            p.selfw(v, "dve")
            p.inc(v.tensor_copy(s3_bb[:, 0:512], psC[:]), "dve")
            p.inc(v.tensor_copy(s3_bb[:, 512:1024], psT[:]), "dve")
        b.vector(dv_s3)

        for th in range(2):
            toff = th * 1024

            pe_b = p.c["pe"]
            act_b = p.c["act"]
            dve_b = p.c["dve"]
            gp_b = p.c["gp"]
            def gp_w1(g, pe_b=pe_b):
                for fc in range(32):
                    if fc < 2:
                        if pe_b > 0:
                            g.wait_ge(p.s["pe"], pe_b)
                    else:
                        g.wait_ge(p.s["pe"], pe_b + (fc - 2) + 1)
                    p.inc(g.dma_start(w1_b[:, fc % 2, :, :], w1T[fc]), "gp", 16)
                    p.inc(g.dma_start(w2_b[:, fc % 2, :, :], w2T[fc]), "gp", 16)
            b.gpsimd(gp_w1)
            def pe_s1(t, toff=toff, gp_b=gp_b, act_b=act_b):
                p.wait(t, "sy")
                p.wait(t, "dve")
                for fc in range(32):
                    t.wait_ge(p.s["gp"], gp_b + (fc + 1) * 32)
                    if fc >= 1:
                        t.wait_ge(p.s["act"], act_b + 2 * (fc - 1) + 2)
                    for hc in range(NHC):
                        for nn in range(2):
                            mm = t.matmul(psSA[:, nn * 512:(nn + 1) * 512],
                                          w1_b[:, fc % 2, hc, :],
                                          x_fmb[:, hc, toff + nn * 512:toff + (nn + 1) * 512],
                                          start=(hc == 0), stop=(hc == NHC - 1))
                    for hc in range(NHC):
                        for nn in range(2):
                            mm = t.matmul(psSB[:, nn * 512:(nn + 1) * 512],
                                          w2_b[:, fc % 2, hc, :],
                                          x_fmb[:, hc, toff + nn * 512:toff + (nn + 1) * 512],
                                          start=(hc == 0), stop=(hc == NHC - 1))
                    p.inc(mm, "pe")
            b.tensor(pe_s1)
            s1pe_list = list(range(pe_b + 1, pe_b + 33))
            def act_s1(a, dve_b=dve_b, s1pe_list=s1pe_list):
                for fc in range(32):
                    p.wait(a, "pe", s1pe_list[fc])
                    if fc >= 1:
                        a.wait_ge(p.s["dve"], dve_b + fc)
                    p.selfw(a, "act")
                    p.inc(a.activation(silu_s[:], psSA[:], AF.Silu,
                                       bias=0.0, scale=s1_t[:, fc:fc + 1]), "act")
                    p.inc(a.activation(t2_s[:], psSB[:], AF.Copy,
                                       bias=0.0, scale=s2_t[:, fc:fc + 1]), "act")
            b.scalar(act_s1)
            def dv_s1(v, act_b=act_b):
                for fc in range(32):
                    v.wait_ge(p.s["act"], act_b + 2 * fc + 2)
                    p.selfw(v, "dve")
                    p.inc(v.tensor_mul(h_a[:, fc, :], silu_s[:], t2_s[:]), "dve")
            b.vector(dv_s1)

            for wT, ssc, src, dst in ((h1T, h1s_t, h_a, h_b), (h2T, h2s_t, h_b, h_a)):
                pe_b = p.c["pe"]
                act_b = p.c["act"]
                gp_b = p.c["gp"]
                def gp_wh(g, wT=wT, pe_b=pe_b):
                    for fc in range(32):
                        if fc < 2:
                            g.wait_ge(p.s["pe"], pe_b)
                        else:
                            g.wait_ge(p.s["pe"], pe_b + (fc - 2) + 1)
                        p.inc(g.dma_start(hw_b[:, fc % 2, :, :], wT[fc]), "gp", 16)
                b.gpsimd(gp_wh)
                def pe_h(t, src=src, gp_b=gp_b, act_b=act_b):
                    p.wait(t, "dve")
                    p.wait(t, "act")
                    for fc in range(32):
                        t.wait_ge(p.s["gp"], gp_b + (fc + 1) * 16)
                        if fc >= 1:
                            t.wait_ge(p.s["act"], act_b + fc)
                        for kc in range(32):
                            for nn in range(2):
                                mm = t.matmul(psSA[:, nn * 512:(nn + 1) * 512],
                                              hw_b[:, fc % 2, kc, :],
                                              src[:, kc, nn * 512:(nn + 1) * 512],
                                              start=(kc == 0), stop=(kc == 31))
                        p.inc(mm, "pe")
                b.tensor(pe_h)
                hpe_list = list(range(pe_b + 1, pe_b + 33))
                def act_h(a, ssc=ssc, dst=dst, hpe_list=hpe_list):
                    for fc in range(32):
                        p.wait(a, "pe", hpe_list[fc])
                        p.selfw(a, "act")
                        p.inc(a.activation(dst[:, fc, :], psSA[:], AF.Gelu,
                                           bias=0.0, scale=ssc[:, fc:fc + 1]), "act")
                b.scalar(act_h)

            for tg in range(2):
                banks = [(psSA[:, 0:512], psSA[:, 512:1024]),
                         (psSB[:, 0:512], psSB[:, 512:1024]),
                         (psY[:, 0:512], psY[:, 512:1024]),
                         (psC[:, :], psT[:, :])]
                pe_b = p.c["pe"]
                gp_b = p.c["gp"]
                def gp_w3(g, tg=tg, pe_b=pe_b):
                    for kc in range(32):
                        if kc < 2:
                            g.wait_ge(p.s["pe"], pe_b)
                        else:
                            g.wait_ge(p.s["pe"], pe_b + (kc - 2) + 1)
                        p.inc(g.dma_start(w3_b[:, kc % 2, :],
                                          w3T[kc * 128:(kc + 1) * 128, :]), "gp", 16)
                b.gpsimd(gp_w3)
                def pe_w3(t, tg=tg, gp_b=gp_b, banks=banks):
                    p.wait(t, "act")
                    p.wait(t, "dve")
                    for kc in range(32):
                        t.wait_ge(p.s["gp"], gp_b + (kc + 1) * 16)
                        for tcc in range(4):
                            lh = h_a[:, kc,
                                     tg * 512 + tcc * 128:tg * 512 + (tcc + 1) * 128]
                            for nn in range(2):
                                mm = t.matmul(banks[tcc][nn], lh,
                                              w3_b[:, kc % 2, nn * 512:(nn + 1) * 512],
                                              start=(kc == 0), stop=(kc == 31))
                        p.inc(mm, "pe")
                b.tensor(pe_w3)
                for tcc in range(4):
                    tc = th * 8 + tg * 4 + tcc
                    sl = 0
                    def sy_l4(s, tc=tc, sl=sl):
                        p.wait(s, "dve")
                        p.inc(s.dma_start(x4t[:, sl, :],
                                          x4_d[tc * 128:(tc + 1) * 128, :]), "sy", 16)
                    b.sync(sy_l4)
                    def dv_o(v, tcc=tcc, sl=sl, banks=banks):
                        p.wait(v, "pe")
                        p.wait(v, "sy")
                        p.selfw(v, "dve")
                        p.inc(v.tensor_copy(o_sb[:, 0:512], banks[tcc][0]), "dve")
                        p.inc(v.tensor_copy(o_sb[:, 512:1024], banks[tcc][1]), "dve")
                        p.selfw(v, "dve")
                        p.inc(v.tensor_mul(o_sb[:], o_sb[:], s3_bb[:]), "dve")
                        p.selfw(v, "dve")
                        p.inc(v.tensor_add(x4t[:, sl, :], x4t[:, sl, :], o_sb[:]), "dve")
                    b.vector(dv_o)
                    ln_tile(p, b, x4t[:, sl, :], st, tc % NTC)
                    def sy_o(s, tc=tc, sl=sl):
                        p.wait(s, "dve")
                        p.inc(s.dma_start(out_o[tc * 128:(tc + 1) * 128, :],
                                          x4t[:, sl, :]), "sy", 16)
                    b.sync(sy_o)

        _sc_f.__exit__(None, None, None)
        def sy_fin(s):
            p.wait(s, "sy")
        b.sync(sy_fin)
    return nc



_bf = ml_dtypes.bfloat16


def _tile_w(w, nkc):
    a = np.asarray(w, np.float32).reshape(32, 128, nkc, 128)
    return np.ascontiguousarray(a.transpose(0, 3, 2, 1))


def sel8_host():
    s = np.zeros((NEXP, NEXP * 128), dtype=_bf)
    for e in range(NEXP):
        s[e, e * 128:(e + 1) * 128] = 1.0
    return s


def kernel(L0_attn_x, L0_attn_q_proj, L0_attn_k_proj, L0_ffn_gate_w, L0_ffn_expert_w,
           L1_attn_Wq, L1_attn_bq, L1_attn_Wk, L1_attn_bk, L1_attn_Wv, L1_attn_bv,
           L1_ffn_w1_w, L1_ffn_w1_s, L1_ffn_w2_w, L1_ffn_w2_s,
           L1_ffn_h1_w, L1_ffn_h1_s, L1_ffn_h2_w, L1_ffn_h2_s,
           L1_ffn_w3_w, L1_ffn_w3_s):
    f32 = np.float32
    ct = np.ascontiguousarray
    x = np.asarray(L0_attn_x, f32)
    eyeb = np.eye(128, dtype=_bf)
    eyef = np.eye(128, dtype=f32)

    common_A = {
        "qp": ct(np.asarray(L0_attn_q_proj, f32)),
        "kp": ct(np.asarray(L0_attn_k_proj, f32)),
        "gT": ct(np.asarray(L0_ffn_gate_w, f32).T),
        "ewT": ct(np.asarray(L0_ffn_expert_w, f32).transpose(0, 2, 1)),
        "wqT": ct(np.asarray(L1_attn_Wq, f32).T),
        "wkT": ct(np.asarray(L1_attn_Wk, f32).T),
        "wvT": ct(np.asarray(L1_attn_Wv, f32).T),
        "bq": ct(np.asarray(L1_attn_bq, f32).reshape(HD, 1)),
        "bk": ct(np.asarray(L1_attn_bk, f32).reshape(HD, 1)),
        "bv": ct(np.asarray(L1_attn_bv, f32).reshape(HD, 1)),
        "eyeb": eyeb, "eyef": eyef, "sel8": sel8_host(),
    }
    in_maps_A = []
    for c in range(NCORE):
        m = dict(common_A)
        m["xa"] = ct(x[c])
        m["xaT"] = ct(x[c].T)
        in_maps_A.append(m)

    ncA = build_A()
    resA = run_bass_kernel_spmd(ncA, in_maps_A, core_ids=list(range(NCORE)))
    ra = resA.results

    kfull = ct(np.concatenate([np.asarray(ra[c]["k_o"]) for c in range(NCORE)], axis=1))
    vfull = ct(np.concatenate([np.asarray(ra[c]["v_o"]) for c in range(NCORE)], axis=0))
    qkm = ct(np.concatenate([np.asarray(ra[c]["qkm_o"], f32) for c in range(NCORE)],
                            axis=0).T.reshape(1, 16))

    sc = lambda s: ct(np.asarray(s, f32).reshape(-1, 128).T)
    common_B = {
        "kfull": kfull, "vfull": vfull, "qkm": qkm,
        "w1T": _tile_w(L1_ffn_w1_w, NHC),
        "w2T": _tile_w(L1_ffn_w2_w, NHC),
        "h1T": _tile_w(L1_ffn_h1_w, 32),
        "h2T": _tile_w(L1_ffn_h2_w, 32),
        "w3T": ct(np.asarray(L1_ffn_w3_w, f32).T),
        "s1d": sc(L1_ffn_w1_s), "s2d": sc(L1_ffn_w2_s),
        "h1sd": sc(L1_ffn_h1_s), "h2sd": sc(L1_ffn_h2_s),
        "s3d": ct(np.asarray(L1_ffn_w3_s, f32).reshape(1, D)),
        "eyef": eyef,
    }
    in_maps_B = []
    for c in range(NCORE):
        m = dict(common_B)
        m["x3"] = np.asarray(ra[c]["x3_o"], f32)
        m["qf"] = np.asarray(ra[c]["q_o"])
        in_maps_B.append(m)

    ncB = build_B()
    resB = run_bass_kernel_spmd(ncB, in_maps_B, core_ids=list(range(NCORE)))
    out = np.stack([np.asarray(resB.results[c]["out_o"], f32) for c in range(NCORE)],
                   axis=0)
    return out


# version 10
# speedup vs baseline: 1.3517x; 1.2999x over previous
import numpy as np
import ml_dtypes
import concourse.bass as bass
import concourse.mybir as mybir
from concourse.bass_utils import run_bass_kernel_spmd

F32 = mybir.dt.float32
BF16 = mybir.dt.bfloat16
AF = mybir.ActivationFunctionType
ALU = mybir.AluOpType
AX = mybir.AxisListType

NCORE = 8
T = 2048
D = 1024
E = 256
NEXP = 8
HD = 64
DH = 4096
NTC = T // 128
NHC = D // 128
LN_EPS = 1e-5


class Prog:

    def __init__(self, nc):
        self.nc = nc
        self.s = {}
        self.c = {}

    def sems(self, *names):
        for n in names:
            self.s[n] = self.nc.alloc_semaphore(n)
            self.c[n] = 0

    def inc(self, instr, name, k=1):
        instr.then_inc(self.s[name], k)
        self.c[name] += k
        return self.c[name]

    def wait(self, eng, name, val=None):
        v = self.c[name] if val is None else val
        if v > 0:
            eng.wait_ge(self.s[name], v)

    def selfw(self, eng, name):
        if self.c[name] > 0:
            eng.wait_ge(self.s[name], self.c[name])


import os
_LN_MODE = os.environ.get("LN_MODE", "ts2")
_LN_DELAY = bool(int(os.environ.get("LN_DELAY", "0")))


def ln_stats_batch(p, b, x_tm, st, ntc):
    def act(a):
        p.wait(a, "dve")
        for tc in range(ntc):
            p.selfw(a, "act")
            p.inc(a.activation(st["dump"][:], x_tm[:, tc, :], AF.Square,
                               bias=0.0, scale=1.0,
                               accum_out=st["sq"][:, tc:tc + 1]), "act")
    b.scalar(act)
    def dve(v):
        p.selfw(v, "dve")
        for tc in range(ntc):
            p.inc(v.reduce_sum(st["sum"][:, tc:tc + 1], x_tm[:, tc, :], AX.X), "dve")
        p.wait(v, "act")
        s = slice(0, ntc)
        p.selfw(v, "dve")
        p.inc(v.tensor_scalar(st["m"][:, s], st["sum"][:, s], 1.0 / D, 0.0,
                              ALU.mult, ALU.add), "dve")
        p.inc(v.tensor_scalar(st["ex2"][:, s], st["sq"][:, s], 1.0 / D, 0.0,
                              ALU.mult, ALU.add), "dve")
        p.selfw(v, "dve")
        p.inc(v.tensor_mul(st["msq"][:, s], st["m"][:, s], st["m"][:, s]), "dve")
        p.selfw(v, "dve")
        p.inc(v.tensor_sub(st["var"][:, s], st["ex2"][:, s], st["msq"][:, s]), "dve")
    b.vector(dve)
    def act2(a):
        p.wait(a, "dve")
        s = slice(0, ntc)
        p.inc(a.activation(st["lnv"][:, s], st["var"][:, s], AF.Ln,
                           bias=st["eps"][:], scale=1.0), "act")
        p.selfw(a, "act")
        p.inc(a.activation(st["rstd"][:, s], st["lnv"][:, s], AF.Exp,
                           bias=0.0, scale=st["mhalf"][:]), "act")
    b.scalar(act2)
    def dve2(v):
        p.wait(v, "act")
        s = slice(0, ntc)
        p.inc(v.tensor_mul(st["mh"][:, s], st["m"][:, s], st["rstd"][:, s]), "dve")
        p.selfw(v, "dve")
        p.inc(v.tensor_scalar(st["nmh"][:, s], st["mh"][:, s], -1.0, 0.0,
                              ALU.mult, ALU.add), "dve")
    b.vector(dve2)


def ln_apply_and_fm(p, b, x_tm, x_tmb, x_fmb, st, ntc, PBb, eyeb):
    dve0 = p.c["dve"]
    def dve(v):
        for tc in range(ntc):
            p.selfw(v, "dve")
            p.inc(v.tensor_scalar(x_tm[:, tc, :], x_tm[:, tc, :],
                                  st["rstd"][:, tc:tc + 1], st["nmh"][:, tc:tc + 1],
                                  ALU.mult, ALU.add), "dve")
            p.selfw(v, "dve")
            p.inc(v.tensor_copy(x_tmb[:, tc, :], x_tm[:, tc, :]), "dve")
    b.vector(dve)
    dvc0 = dve0 + 2 * ntc
    trs = []
    def pe_tr(t):
        p.wait(t, "act")
        for g in range(ntc * NHC):
            tc, hc = divmod(g, NHC)
            t.wait_ge(p.s["dve"], dve0 + 2 * (tc + 1))
            if g >= 4:
                t.wait_ge(p.s["dve"], dvc0 + (g - 4) + 1)
            tr = t.transpose(PBb[g % 4][0:128, 0:128],
                             x_tmb[:, tc, hc * 128:(hc + 1) * 128], eyeb[:, :])
            trs.append(p.inc(tr, "pe"))
    b.tensor(pe_tr)
    def dv_tr(v):
        for g in range(ntc * NHC):
            tc, hc = divmod(g, NHC)
            p.wait(v, "pe", trs[g])
            p.inc(v.tensor_copy(x_fmb[:, hc, tc * 128:(tc + 1) * 128],
                                PBb[g % 4][0:128, 0:128]), "dve")
    b.vector(dv_tr)


def ln_tile(p, b, x_ap, st, slot):
    sl = slice(slot, slot + 1)
    def act(a):
        p.wait(a, "dve")
        p.inc(a.activation(st["dump"][:], x_ap, AF.Square, bias=0.0, scale=1.0,
                           accum_out=st["sq"][:, sl]), "act")
    b.scalar(act)
    def dve(v):
        p.selfw(v, "dve")
        p.inc(v.reduce_sum(st["sum"][:, sl], x_ap, AX.X), "dve")
        p.wait(v, "act")
        p.selfw(v, "dve")
        p.inc(v.tensor_scalar(st["m"][:, sl], st["sum"][:, sl], 1.0 / D, 0.0,
                              ALU.mult, ALU.add), "dve")
        p.inc(v.tensor_scalar(st["ex2"][:, sl], st["sq"][:, sl], 1.0 / D, 0.0,
                              ALU.mult, ALU.add), "dve")
        p.selfw(v, "dve")
        p.inc(v.tensor_mul(st["msq"][:, sl], st["m"][:, sl], st["m"][:, sl]), "dve")
        p.selfw(v, "dve")
        p.inc(v.tensor_sub(st["var"][:, sl], st["ex2"][:, sl], st["msq"][:, sl]), "dve")
    b.vector(dve)
    def act2(a):
        p.wait(a, "dve")
        p.inc(a.activation(st["lnv"][:, sl], st["var"][:, sl], AF.Ln,
                           bias=st["eps"][:], scale=1.0), "act")
        p.selfw(a, "act")
        p.inc(a.activation(st["rstd"][:, sl], st["lnv"][:, sl], AF.Exp,
                           bias=0.0, scale=st["mhalf"][:]), "act")
    b.scalar(act2)
    def dve2(v):
        p.wait(v, "act")
        p.inc(v.tensor_mul(st["mh"][:, sl], st["m"][:, sl], st["rstd"][:, sl]), "dve")
        p.selfw(v, "dve")
        p.inc(v.tensor_scalar(st["nmh"][:, sl], st["mh"][:, sl], -1.0, 0.0,
                              ALU.mult, ALU.add), "dve")
        p.selfw(v, "dve")
        p.inc(v.tensor_scalar(x_ap, x_ap, st["rstd"][:, sl], st["nmh"][:, sl],
                              ALU.mult, ALU.add), "dve")
    b.vector(dve2)


_BOOT = bool(int(os.environ.get("KBOOT", "1")))


def boot_clear(b, nc, p):
    if not _BOOT:
        return
    """Clear OUR kernel semaphores at program start (crashed programs leave
    stale values; a stale sem makes every wait_ge pass early). gpsimd clears;
    the other engines sit in a timed nop longer than the clear, then check a
    boot sem that the clear zeroed before it was incremented. Bass's own
    barrier sems are left alone (their protocol is self-normalizing)."""
    bs = nc.alloc_semaphore("bootsem")
    lo = min(h.num for h in list(p.s.values()) + [bs])
    rng_ = range(lo, nc._kernel_sem_range.stop)
    def gp(g):
        g.sem_clear(rng_)
        g.sem_inc(bs, 1)
    b.gpsimd(gp)
    for name in ("tensor", "vector", "scalar", "sync"):
        def w(e):
            e.nop(cycle_cnt=20000)
            e.wait_ge(bs, 1)
        getattr(b, name)(w)


def st_alloc(nc):
    st = {n: nc.alloc_sbuf_tensor("st_" + n, [128, NTC], F32)
          for n in ("sum", "sq", "m", "ex2", "msq", "var", "lnv", "rstd", "mh", "nmh")}
    st["dump"] = nc.alloc_sbuf_tensor("st_dump", [128, D], BF16)
    st["eps"] = nc.alloc_sbuf_tensor("st_eps", [128, 1], F32)
    st["mhalf"] = nc.alloc_sbuf_tensor("st_mhalf", [128, 1], F32)
    return st


def st_init(p, b, st):
    def dv(v):
        p.inc(v.memset(st["eps"][:], LN_EPS), "dve")
        p.inc(v.memset(st["mhalf"][:], -0.5), "dve")
    b.vector(dv)



class _Stop(Exception):
    pass


def _chk(upto, n):
    if upto < n:
        raise _Stop


def build_A(upto=99):
    nc = bass.Bass("TRN2", target_bir_lowering=False)
    xa = nc.dram_tensor("xa", [T, D], F32, kind="ExternalInput")
    xaT = nc.dram_tensor("xaT", [D, T], F32, kind="ExternalInput")
    qp = nc.dram_tensor("qp", [D, E], F32, kind="ExternalInput")
    kp = nc.dram_tensor("kp", [D, E], F32, kind="ExternalInput")
    gTd = nc.dram_tensor("gT", [D, NEXP], F32, kind="ExternalInput")
    ewT = nc.dram_tensor("ewT", [NEXP, D, D], F32, kind="ExternalInput")
    wqT = nc.dram_tensor("wqT", [D, HD], F32, kind="ExternalInput")
    wkT = nc.dram_tensor("wkT", [D, HD], F32, kind="ExternalInput")
    wvT = nc.dram_tensor("wvT", [D, HD], F32, kind="ExternalInput")
    bqd = nc.dram_tensor("bq", [HD, 1], F32, kind="ExternalInput")
    bkd = nc.dram_tensor("bk", [HD, 1], F32, kind="ExternalInput")
    bvd = nc.dram_tensor("bv", [HD, 1], F32, kind="ExternalInput")
    eyeb_d = nc.dram_tensor("eyeb", [128, 128], BF16, kind="ExternalInput")
    eyef_d = nc.dram_tensor("eyef", [128, 128], F32, kind="ExternalInput")
    sel8_d = nc.dram_tensor("sel8", [NEXP, NEXP * 128], BF16, kind="ExternalInput")

    x3_o = nc.dram_tensor("x3_o", [T, D], F32, kind="ExternalOutput")
    q_o = nc.dram_tensor("q_o", [HD, T], BF16, kind="ExternalOutput")
    k_o = nc.dram_tensor("k_o", [HD, T], BF16, kind="ExternalOutput")
    v_o = nc.dram_tensor("v_o", [T, HD], BF16, kind="ExternalOutput")
    qkm_o = nc.dram_tensor("qkm_o", [1, 2], F32, kind="ExternalOutput")

    x_tm = nc.alloc_sbuf_tensor("x_tm", [128, NTC, D], F32)
    x_tmb = nc.alloc_sbuf_tensor("x_tmb", [128, NTC, D], BF16)
    x_fmb = nc.alloc_sbuf_tensor("x_fmb", [128, NHC, T], BF16)
    qpb = nc.alloc_sbuf_tensor("qpb", [128, NHC, E], BF16)
    kpb = nc.alloc_sbuf_tensor("kpb", [128, NHC, E], BF16)
    wq_b = qpb[:, :, 0:HD]
    wk_b = qpb[:, :, HD:2 * HD]
    wv_b = qpb[:, :, 2 * HD:3 * HD]
    pool_a = nc.alloc_sbuf_tensor("pool_a", [128, 4096], BF16)
    qe = pool_a[:].rearrange("p (a t) -> p a t", a=2)
    mb_all = pool_a[:].rearrange("p (e t) -> p e t", e=NEXP)
    pool_b = nc.alloc_sbuf_tensor("pool_b", [128, 8192], BF16)
    ke = pool_b[:, 0:4096].rearrange("p (a e) -> p a e", a=NTC)
    moe_fm2 = pool_b[:].rearrange("p (s f t) -> p s f t", s=2, f=NEXP)
    kv_b = nc.alloc_sbuf_tensor("kv_b", [128, 2, D], BF16)
    zrow = nc.alloc_sbuf_tensor("zrow", [1, T], F32)
    z_tm = nc.alloc_sbuf_tensor("z_tm", [128, NTC], F32)
    ones_b = nc.alloc_sbuf_tensor("ones_b", [128, 1], BF16)
    onesr_b = nc.alloc_sbuf_tensor("onesr_b", [1, 128], BF16)
    eyeb = nc.alloc_sbuf_tensor("eyeb_t", [128, 128], BF16)
    eyef = nc.alloc_sbuf_tensor("eyef_t", [128, 128], F32)
    st = st_alloc(nc)
    gTb = nc.alloc_sbuf_tensor("gTb", [128, NHC, NEXP], BF16)
    scor = nc.alloc_sbuf_tensor("scor", [128, NTC, NEXP], F32)
    gmax = nc.alloc_sbuf_tensor("gmax", [128, NTC], F32)
    pre = nc.alloc_sbuf_tensor("pre", [128, NTC], F32)
    tta = nc.alloc_sbuf_tensor("tta", [128, NTC], F32)
    ttb = nc.alloc_sbuf_tensor("ttb", [128, NTC], F32)
    masks_tm = nc.alloc_sbuf_tensor("masks_tm", [128, NTC, NEXP], BF16)
    shr_a = nc.alloc_sbuf_tensor("shr_a", [HD, T], BF16)
    masks_f = shr_a[0:NEXP, :]
    mtile = nc.alloc_sbuf_tensor("mtile", [128, 512], BF16)
    w_moe = nc.alloc_sbuf_tensor("w_moe", [128, 2, 1024], BF16)
    moe_tm2 = nc.alloc_sbuf_tensor("moe_tm2", [128, 2, D], BF16)
    bq_t = nc.alloc_sbuf_tensor("bq_t", [HD, 1], F32)
    bk_t = nc.alloc_sbuf_tensor("bk_t", [HD, 1], F32)
    bv_t = nc.alloc_sbuf_tensor("bv_t", [HD, 1], F32)
    qf = nc.alloc_sbuf_tensor("qf", [HD, T], BF16)
    kf = nc.alloc_sbuf_tensor("kf", [HD, T], BF16)
    vf = nc.alloc_sbuf_tensor("vf", [HD, T], BF16)
    sqn = shr_a[:, :]
    v_tmb = nc.alloc_sbuf_tensor("v_tmb", [128, NTC, HD], BF16)
    qkm_t = nc.alloc_sbuf_tensor("qkm_t", [1, 2], F32)
    tmp_res = nc.alloc_sbuf_tensor("tmp_res", [128, 512], F32)
    sel8 = tmp_res[:].bitcast(BF16)[0:NEXP, :]

    adr = lambda tt: nc.lookup_mloc(tt).addr
    w_moe2 = nc.alloc_sbuf_tensor_at("w_moe2", [128, 2, 1024], BF16, offset=adr(kf))
    mtile2 = nc.alloc_sbuf_tensor_at("mtile2", [128, 512], BF16, offset=adr(vf))

    def wslot(i):
        return (w_moe if (i % 4) < 2 else w_moe2)[:, i % 2, :]

    def mslot(i):
        return mtile[:] if i % 2 == 0 else mtile2[:]

    PB = [nc.alloc_psum_tensor(f"pb{i}", [128, 512], F32) for i in range(8)]
    PBb = [t[:].bitcast(BF16) for t in PB]

    p = Prog(nc)
    p.sems("sy", "gp", "pe", "act", "dve")

    with nc.Block() as b:
      try:
        boot_clear(b, nc, p)
        def sy0(s):
            p.inc(s.dma_start(x_tm[:], xa[:].rearrange("(c p) h -> p c h", p=128)), "sy", 16)
            p.inc(s.dma_start(eyeb[:], eyeb_d[:]), "sy", 16)
            p.inc(s.dma_start(eyef[:], eyef_d[:]), "sy", 16)
            p.inc(s.dma_start(bq_t[:], bqd[:]), "sy", 16)
            p.inc(s.dma_start(bk_t[:], bkd[:]), "sy", 16)
            p.inc(s.dma_start(bv_t[:], bvd[:]), "sy", 16)
        b.sync(sy0)
        def gp0(g):
            p.inc(g.dma_start(x_tmb[:], xa[:].rearrange("(c p) h -> p c h", p=128)), "gp", 16)
            p.inc(g.dma_start(x_fmb[:], xaT[:].rearrange("(c p) t -> p c t", p=128)), "gp", 16)
            p.inc(g.dma_start(qpb[:], qp[:].rearrange("(c p) e -> p c e", p=128)), "gp", 16)
            p.inc(g.dma_start(kpb[:], kp[:].rearrange("(c p) e -> p c e", p=128)), "gp", 16)
            p.inc(g.dma_start(gTb[:], gTd[:].rearrange("(c p) e -> p c e", p=128)), "gp", 16)
        b.gpsimd(gp0)
        def dv0(v):
            p.inc(v.memset(ones_b[:], 1.0), "dve")
            p.inc(v.memset(onesr_b[:], 1.0), "dve")
        b.vector(dv0)
        st_init(p, b, st)
        gp_ld = p.c["gp"]

        _chk(upto, 2)
        _sc_a = nc.named_scope("A_l0attn"); _sc_a.__enter__()
        act0 = p.c["act"]
        qpe, kpe = [], []
        def pe_qk(t):
            p.wait(t, "gp", gp_ld)
            p.wait(t, "sy")
            for ec in range(2):
                for tn in range(4):
                    g = ec * 4 + tn
                    if g >= 2:
                        t.wait_ge(p.s["act"], act0 + (g - 2) + 1)
                    bank = PB[g % 2]
                    for hc in range(NHC):
                        mm = t.matmul(bank[:], qpb[:, hc, ec * 128:(ec + 1) * 128],
                                      x_fmb[:, hc, tn * 512:(tn + 1) * 512],
                                      start=(hc == 0), stop=(hc == NHC - 1))
                    qpe.append(p.inc(mm, "pe"))
            for tc in range(NTC):
                if tc >= 2:
                    t.wait_ge(p.s["act"], act0 + 8 + (tc - 2) + 1)
                bank = PB[2 + tc % 2]
                for hc in range(NHC):
                    mm = t.matmul(bank[:, 0:E], x_fmb[:, hc, tc * 128:(tc + 1) * 128],
                                  kpb[:, hc, :], start=(hc == 0), stop=(hc == NHC - 1))
                kpe.append(p.inc(mm, "pe"))
        b.tensor(pe_qk)
        def act_qk(a):
            for ec in range(2):
                for tn in range(4):
                    g = ec * 4 + tn
                    p.wait(a, "pe", qpe[g])
                    p.selfw(a, "act")
                    p.inc(a.activation(qe[:, ec, tn * 512:(tn + 1) * 512],
                                       PB[g % 2][:], AF.Exp, bias=0.0, scale=1.0), "act")
            for tc in range(NTC):
                p.wait(a, "pe", kpe[tc])
                p.selfw(a, "act")
                p.inc(a.activation(ke[:, tc, :], PB[2 + tc % 2][:, 0:E],
                                   AF.Exp, bias=0.0, scale=1.0), "act")
        b.scalar(act_qk)

        _chk(upto, 3)
        dve0 = p.c["dve"]
        zpe = []
        def pe_z(t):
            p.wait(t, "act")
            t.wait_ge(p.s["dve"], 2)
            for tn in range(4):
                if tn >= 1:
                    t.wait_ge(p.s["dve"], dve0 + tn)
                for ec in range(2):
                    mm = t.matmul(PB[4][0:1, :], ones_b[:, 0:1],
                                  qe[:, ec, tn * 512:(tn + 1) * 512],
                                  start=(ec == 0), stop=(ec == 1))
                zpe.append(p.inc(mm, "pe"))
        b.tensor(pe_z)
        def dv_z(v):
            for tn in range(4):
                p.wait(v, "pe", zpe[tn])
                p.inc(v.tensor_copy(zrow[0:1, tn * 512:(tn + 1) * 512], PB[4][0:1, :]),
                      "dve")
            p.selfw(v, "dve")
            p.inc(v.tensor_single_scalar(zrow[:], zrow[:], 1e-8, ALU.add), "dve")
            p.selfw(v, "dve")
            p.inc(v.reciprocal(zrow[:], zrow[:]), "dve")
        b.vector(dv_z)
        dve1 = p.c["dve"]
        ztr = []
        def pe_zt(t):
            p.wait(t, "dve", dve1)
            for tc in range(NTC):
                if tc >= 2:
                    t.wait_ge(p.s["dve"], dve1 + (tc - 2) + 1)
                tr = t.transpose(PB[4 + tc % 2][:, 0:1],
                                 zrow[0:1, tc * 128:(tc + 1) * 128], eyef[0:1, 0:1])
                ztr.append(p.inc(tr, "pe"))
        b.tensor(pe_zt)
        def dv_zt(v):
            for tc in range(NTC):
                p.wait(v, "pe", ztr[tc])
                p.inc(v.tensor_copy(z_tm[:, tc:tc + 1], PB[4 + tc % 2][:, 0:1]), "dve")
        b.vector(dv_zt)

        _chk(upto, 4)
        dve2 = p.c["dve"]
        kvpe = []
        def pe_kv(t):
            for g in range(4):
                ec, hn = divmod(g, 2)
                if g >= 2:
                    t.wait_ge(p.s["dve"], dve2 + (g - 2) + 1)
                bank = PB[6 + g % 2]
                for tc in range(NTC):
                    mm = t.matmul(bank[:], ke[:, tc, ec * 128:(ec + 1) * 128],
                                  x_tmb[:, tc, hn * 512:(hn + 1) * 512],
                                  start=(tc == 0), stop=(tc == NTC - 1))
                kvpe.append(p.inc(mm, "pe"))
        b.tensor(pe_kv)
        def dv_kv(v):
            for g in range(4):
                ec, hn = divmod(g, 2)
                p.wait(v, "pe", kvpe[g])
                p.inc(v.tensor_copy(kv_b[:, ec, hn * 512:(hn + 1) * 512],
                                    PB[6 + g % 2][:]), "dve")
        b.vector(dv_kv)

        _chk(upto, 5)
        act1 = p.c["act"]
        numpe = []
        def pe_num(t):
            p.wait(t, "dve")
            for g in range(NTC * 2):
                tc, hn = divmod(g, 2)
                if g >= 2:
                    t.wait_ge(p.s["act"], act1 + (g - 2) + 1)
                bank = PB[g % 2]
                for ec in range(2):
                    mm = t.matmul(bank[:], qe[:, ec, tc * 128:(tc + 1) * 128],
                                  kv_b[:, ec, hn * 512:(hn + 1) * 512],
                                  start=(ec == 0), stop=(ec == 1))
                numpe.append(p.inc(mm, "pe"))
        b.tensor(pe_num)
        dve3 = p.c["dve"]
        numact = []
        def act_num(a):
            for g in range(NTC * 2):
                tc, hn = divmod(g, 2)
                p.wait(a, "pe", numpe[g])
                if g >= 1:
                    a.wait_ge(p.s["dve"], dve3 + g)
                p.selfw(a, "act")
                numact.append(p.inc(a.activation(tmp_res[:], PB[g % 2][:],
                                                 AF.Copy, bias=0.0,
                                                 scale=z_tm[:, tc:tc + 1]), "act"))
        b.scalar(act_num)
        def dv_num(v):
            for g in range(NTC * 2):
                tc, hn = divmod(g, 2)
                p.wait(v, "act", numact[g])
                p.selfw(v, "dve")
                p.inc(v.tensor_add(x_tm[:, tc, hn * 512:(hn + 1) * 512],
                                   x_tm[:, tc, hn * 512:(hn + 1) * 512],
                                   tmp_res[:]), "dve")
        b.vector(dv_num)

        _sc_a.__exit__(None, None, None)
        _chk(upto, 6)
        _sc_b = nc.named_scope("A_ln1"); _sc_b.__enter__()
        ln_stats_batch(p, b, x_tm, st, NTC)
        ln_apply_and_fm(p, b, x_tm, x_tmb, x_fmb, st, NTC, PBb, eyeb)

        _sc_b.__exit__(None, None, None)
        _chk(upto, 7)
        _sc_c = nc.named_scope("A_moe"); _sc_c.__enter__()
        scpe = []
        dve4 = p.c["dve"]
        def pe_sc(t):
            p.wait(t, "sy")
            p.wait(t, "dve")
            for tc in range(NTC):
                if tc >= 2:
                    t.wait_ge(p.s["dve"], dve4 + (tc - 2) + 1)
                bank = PB[2 + tc % 2]
                for hc in range(NHC):
                    mm = t.matmul(bank[:, 0:NEXP], x_fmb[:, hc, tc * 128:(tc + 1) * 128],
                                  gTb[:, hc, :], start=(hc == 0), stop=(hc == NHC - 1))
                scpe.append(p.inc(mm, "pe"))
        b.tensor(pe_sc)
        def dv_sc(v):
            for tc in range(NTC):
                p.wait(v, "pe", scpe[tc])
                p.inc(v.tensor_copy(scor[:, tc, :], PB[2 + tc % 2][:, 0:NEXP]), "dve")
            p.selfw(v, "dve")
            p.inc(v.reduce_max(gmax[:], scor[:], AX.X), "dve")
            for e in range(NEXP):
                se = scor[:, :, e]
                p.selfw(v, "dve")
                p.inc(v.tensor_tensor(tta[:], se, gmax[:], ALU.is_ge), "dve")
                if e == 0:
                    p.selfw(v, "dve")
                    p.inc(v.tensor_copy(masks_tm[:, :, 0], tta[:]), "dve")
                    p.inc(v.tensor_copy(pre[:], se), "dve")
                else:
                    p.selfw(v, "dve")
                    p.inc(v.tensor_tensor(ttb[:], pre[:], gmax[:], ALU.is_lt), "dve")
                    p.selfw(v, "dve")
                    p.inc(v.tensor_mul(tta[:], tta[:], ttb[:]), "dve")
                    p.selfw(v, "dve")
                    p.inc(v.tensor_copy(masks_tm[:, :, e], tta[:]), "dve")
                    if e < NEXP - 1:
                        p.selfw(v, "dve")
                        p.inc(v.tensor_max(pre[:], pre[:], se), "dve")
        b.vector(dv_sc)
        _chk(upto, 8)
        dve5 = p.c["dve"]
        mtr = []
        def pe_mt(t):
            p.wait(t, "dve", dve5)
            for tc in range(NTC):
                if tc >= 2:
                    t.wait_ge(p.s["dve"], dve5 + (tc - 2) + 1)
                tr = t.transpose(PBb[4 + tc % 2][0:NEXP, 0:128], masks_tm[:, tc, :],
                                 eyeb[:, :])
                mtr.append(p.inc(tr, "pe"))
        b.tensor(pe_mt)
        def dv_mt(v):
            for tc in range(NTC):
                p.wait(v, "pe", mtr[tc])
                p.inc(v.tensor_copy(masks_f[0:NEXP, tc * 128:(tc + 1) * 128],
                                    PBb[4 + tc % 2][0:NEXP, 0:128]), "dve")
        b.vector(dv_mt)

        _chk(upto, 9)
        def sy_sel(s):
            p.wait(s, "dve")
            p.inc(s.dma_start(sel8, sel8_d[:]), "sy", 16)
        b.sync(sy_sel)

        for tn in range(4):
            for e in range(NEXP):
                def pe_mb(t, e=e, tn=tn):
                    p.wait(t, "dve")
                    p.wait(t, "sy")
                    p.inc(t.matmul(PB[6][:], sel8[:, e * 128:(e + 1) * 128],
                                   masks_f[0:NEXP, tn * 512:(tn + 1) * 512],
                                   start=True, stop=True), "pe")
                b.tensor(pe_mb)
                def dv_mb(v, e=e):
                    p.wait(v, "pe")
                    p.selfw(v, "dve")
                    p.inc(v.tensor_copy(mb_all[:, e, :], PB[6][:]), "dve")
                b.vector(dv_mb)
            pe_moe0 = p.c["pe"]
            gp_moe0 = p.c["gp"]
            dve_moe0 = p.c["dve"]
            def gp_w(g, tn=tn):
                for j in range(NEXP * NHC // 2):
                    e, hc = divmod(2 * j, NHC)
                    if j >= 2:
                        g.wait_ge(p.s["pe"], pe_moe0 + 2 * j - 2)
                    elif pe_moe0 > 0:
                        g.wait_ge(p.s["pe"], pe_moe0)
                    dst = w_moe[:] if j % 2 == 0 else w_moe2[:]
                    p.inc(g.dma_start(
                        dst, ewT[e, hc * 128:(hc + 2) * 128, :].rearrange(
                            "(s p) h -> p s h", p=128)), "gp", 16)
            b.gpsimd(gp_w)
            def dv_mm(v, tn=tn):
                for i in range(NEXP * NHC):
                    e, hc = divmod(i, NHC)
                    if i >= 1:
                        v.wait_ge(p.s["pe"], pe_moe0 + i)
                    p.selfw(v, "dve")
                    p.inc(v.tensor_tensor(mtile[:], x_fmb[:, hc, tn * 512:(tn + 1) * 512],
                                          mb_all[:, e, :], ALU.mult), "dve")
            b.vector(dv_mm)
            def pe_moe(t, tn=tn):
                for i in range(NEXP * NHC):
                    e, hc = divmod(i, NHC)
                    t.wait_ge(p.s["gp"], gp_moe0 + (i // 2 + 1) * 16)
                    t.wait_ge(p.s["dve"], dve_moe0 + i + 1)
                    wsl = (w_moe if (i // 2) % 2 == 0 else w_moe2)[:, i % 2, :]
                    for fc in range(8):
                        mm = t.matmul(PB[fc][:], wsl[:, fc * 128:(fc + 1) * 128],
                                      mtile[:], start=(i == 0),
                                      stop=(i == NEXP * NHC - 1))
                    p.inc(mm, "pe")
            b.tensor(pe_moe)
            slot = tn % 2
            def dv_ev(v, tn=tn, slot=slot):
                p.wait(v, "pe")
                p.wait(v, "sy")
                p.selfw(v, "dve")
                for fc in range(8):
                    p.inc(v.tensor_copy(moe_fm2[:, slot, fc, :], PB[fc][:]), "dve")
            b.vector(dv_ev)
            for tj in range(4):
                tc = tn * 4 + tj
                def sy_tr(s, tn=tn, slot=slot, tj=tj, tc=tc):
                    p.wait(s, "dve")
                    for hc in range(NHC):
                        p.selfw(s, "sy")
                        p.inc(s.dma_start(
                            moe_tm2[:, tc % 2, hc * 128:(hc + 1) * 128],
                            moe_fm2[:, slot, hc, tj * 128:(tj + 1) * 128],
                            transpose=True), "sy", 16)
                b.sync(sy_tr)
                def dv_res(v, tc=tc):
                    p.wait(v, "sy")
                    p.selfw(v, "dve")
                    p.inc(v.tensor_scalar(x_tm[:, tc, :], x_tm[:, tc, :], 2.0, 0.0,
                                          ALU.mult, ALU.add), "dve")
                    p.selfw(v, "dve")
                    p.inc(v.tensor_add(x_tm[:, tc, :], x_tm[:, tc, :],
                                       moe_tm2[:, tc % 2, :]), "dve")
                b.vector(dv_res)


        _sc_c.__exit__(None, None, None)
        _chk(upto, 10)
        _sc_d = nc.named_scope("A_ln2_out"); _sc_d.__enter__()
        ln_stats_batch(p, b, x_tm, st, NTC)
        ln_apply_and_fm(p, b, x_tm, x_tmb, x_fmb, st, NTC, PBb, eyeb)

        def sy_x3(s):
            p.wait(s, "dve")
            p.inc(s.dma_start(x3_o[:].rearrange("(c p) h -> p c h", p=128), x_tm[:]),
                  "sy", 16)
        b.sync(sy_x3)

        _chk(upto, 11)
        def gp_w2(g):
            p.wait(g, "pe")
            p.inc(g.dma_start(wq_b, wqT[:].rearrange("(c p) e -> p c e", p=128)), "gp", 16)
            p.inc(g.dma_start(wk_b, wkT[:].rearrange("(c p) e -> p c e", p=128)), "gp", 16)
            p.inc(g.dma_start(wv_b, wvT[:].rearrange("(c p) e -> p c e", p=128)), "gp", 16)
        b.gpsimd(gp_w2)
        gp_ld2 = p.c["gp"]
        act2c = p.c["act"]
        prpe = []
        def pe_pr(t):
            p.wait(t, "sy")
            p.wait(t, "dve")
            p.wait(t, "gp", gp_ld2)
            for j, wt in enumerate((wq_b, wk_b, wv_b)):
                for tn in range(4):
                    g = j * 4 + tn
                    if g >= 2:
                        t.wait_ge(p.s["act"], act2c + (g - 2) + 1)
                    bank = PB[g % 2]
                    for hc in range(NHC):
                        mm = t.matmul(bank[0:HD, :], wt[:, hc, :],
                                      x_fmb[:, hc, tn * 512:(tn + 1) * 512],
                                      start=(hc == 0), stop=(hc == NHC - 1))
                    prpe.append(p.inc(mm, "pe"))
        b.tensor(pe_pr)
        def act_pr(a):
            outs = (qf, kf, vf)
            biases = (bq_t, bk_t, bv_t)
            for j in range(3):
                for tn in range(4):
                    g = j * 4 + tn
                    p.wait(a, "pe", prpe[g])
                    p.selfw(a, "act")
                    p.inc(a.activation(outs[j][:, tn * 512:(tn + 1) * 512],
                                       PB[g % 2][0:HD, :], AF.Identity,
                                       bias=biases[j][:], scale=1.0), "act")
        b.scalar(act_pr)

        _chk(upto, 12)
        dve6 = p.c["dve"]
        vtr = []
        def pe_vt(t):
            p.wait(t, "act")
            for tc in range(NTC):
                if tc >= 2:
                    t.wait_ge(p.s["dve"], dve6 + (tc - 2) + 1)
                tr = t.transpose(PBb[2 + tc % 2][0:128, 0:HD],
                                 vf[0:HD, tc * 128:(tc + 1) * 128], eyeb[0:HD, 0:HD])
                vtr.append(p.inc(tr, "pe"))
        b.tensor(pe_vt)
        def dv_vt(v):
            for tc in range(NTC):
                p.wait(v, "pe", vtr[tc])
                p.inc(v.tensor_copy(v_tmb[:, tc, :], PBb[2 + tc % 2][0:128, 0:HD]),
                      "dve")
        b.vector(dv_vt)

        _chk(upto, 13)
        dve7 = p.c["dve"]
        nrmpe = []
        def dv_n1(v):
            p.wait(v, "act")
            p.selfw(v, "dve")
            p.inc(v.tensor_mul(sqn[:], qf[:], qf[:]), "dve")
        b.vector(dv_n1)
        def pe_nrm(t):
            for j in range(2):
                for tn in range(4):
                    g = j * 4 + tn
                    if g == 0:
                        t.wait_ge(p.s["dve"], dve7 + 1)
                    elif j == 0:
                        t.wait_ge(p.s["dve"], dve7 + 1 + g)
                    elif tn == 0:
                        t.wait_ge(p.s["dve"], dve7 + 7)
                    else:
                        t.wait_ge(p.s["dve"], dve7 + 7 + tn)
                    mm = t.matmul(PB[4][0:1, :], ones_b[0:HD, 0:1],
                                  sqn[:, tn * 512:(tn + 1) * 512], start=True, stop=True)
                    nrmpe.append(p.inc(mm, "pe"))
        b.tensor(pe_nrm)
        def dv_n2(v):
            for tn in range(4):
                p.wait(v, "pe", nrmpe[tn])
                p.selfw(v, "dve")
                p.inc(v.tensor_copy(zrow[0:1, tn * 512:(tn + 1) * 512], PB[4][0:1, :]),
                      "dve")
            p.selfw(v, "dve")
            p.inc(v.tensor_reduce(qkm_t[0:1, 0:1], zrow[:], AX.X, ALU.max), "dve")
            p.selfw(v, "dve")
            p.inc(v.tensor_mul(sqn[:], kf[:], kf[:]), "dve")
            for tn in range(4):
                p.wait(v, "pe", nrmpe[4 + tn])
                p.selfw(v, "dve")
                p.inc(v.tensor_copy(zrow[0:1, tn * 512:(tn + 1) * 512], PB[4][0:1, :]),
                      "dve")
            p.selfw(v, "dve")
            p.inc(v.tensor_reduce(qkm_t[0:1, 1:2], zrow[:], AX.X, ALU.max), "dve")
        b.vector(dv_n2)

        _chk(upto, 14)
        def sy_out(s):
            p.wait(s, "dve")
            p.wait(s, "act")
            p.inc(s.dma_start(q_o[:], qf[:]), "sy", 16)
            p.inc(s.dma_start(k_o[:], kf[:]), "sy", 16)
            p.inc(s.dma_start(v_o[:].rearrange("(c p) d -> p c d", p=128), v_tmb[:]),
                  "sy", 16)
            p.inc(s.dma_start(qkm_o[:], qkm_t[:]), "sy", 16)
            p.wait(s, "sy")
        b.sync(sy_out)
        _sc_d.__exit__(None, None, None)
      except _Stop:
        def sy_tail(s):
            p.wait(s, "dve")
            p.wait(s, "act")
            p.wait(s, "pe")
            p.inc(s.dma_start(x3_o[:].rearrange("(c p) h -> p c h", p=128),
                              x_tm[:]), "sy", 16)
            p.inc(s.dma_start(q_o[:], qf[:]), "sy", 16)
            p.wait(s, "sy")
        b.sync(sy_tail)
    return nc



def build_B():
    nc = bass.Bass("TRN2", target_bir_lowering=False)
    x3 = nc.dram_tensor("x3", [T, D], F32, kind="ExternalInput")
    qfd = nc.dram_tensor("qf", [HD, T], BF16, kind="ExternalInput")
    kfull = nc.dram_tensor("kfull", [HD, NCORE * T], BF16, kind="ExternalInput")
    vfull = nc.dram_tensor("vfull", [NCORE * T, HD], BF16, kind="ExternalInput")
    qkmd = nc.dram_tensor("qkm", [1, 16], F32, kind="ExternalInput")
    w1T = nc.dram_tensor("w1T", [32, 128, NHC, 128], BF16, kind="ExternalInput")
    w2T = nc.dram_tensor("w2T", [32, 128, NHC, 128], BF16, kind="ExternalInput")
    h1T = nc.dram_tensor("h1T", [32, 128, 32, 128], BF16, kind="ExternalInput")
    h2T = nc.dram_tensor("h2T", [32, 128, 32, 128], BF16, kind="ExternalInput")
    w3T = nc.dram_tensor("w3T", [DH, D], BF16, kind="ExternalInput")
    s1d = nc.dram_tensor("s1d", [128, 32], F32, kind="ExternalInput")
    s2d = nc.dram_tensor("s2d", [128, 32], F32, kind="ExternalInput")
    h1sd = nc.dram_tensor("h1sd", [128, 32], F32, kind="ExternalInput")
    h2sd = nc.dram_tensor("h2sd", [128, 32], F32, kind="ExternalInput")
    s3d = nc.dram_tensor("s3d", [1, D], F32, kind="ExternalInput")
    eyef_d = nc.dram_tensor("eyef", [128, 128], F32, kind="ExternalInput")
    out_o = nc.dram_tensor("out_o", [T, D], F32, kind="ExternalOutput")
    x4_d = nc.dram_tensor("x4_d", [T, D], F32, kind="Internal")

    NK = (NCORE * T) // 128

    arena = nc.alloc_sbuf_tensor("arena", [128, 65536], BF16)
    cur = [0]
    def carve(n):
        cur[0] = (cur[0] + 15) // 16 * 16
        off = cur[0]
        cur[0] += n
        return off
    o_k = carve(16384)
    k_sb = arena[:, o_k:o_k + 16384][0:HD, :]
    o_v = carve(NK * 65 + 7)
    v65 = arena[:, o_v:o_v + NK * 65].rearrange("p (c d) -> p c d", d=65)
    o_pt = carve(2048)
    pt_a = arena[:, o_pt:o_pt + 2048].rearrange("p (s t) -> p s t", s=2)
    o_y = carve(2048)
    y_sb = arena[:, o_y:o_y + 2048].bitcast(F32)
    o_q = carve(2048)
    qf_b = arena[:, o_q:o_q + 2048][0:HD, :]
    o_x3 = carve(4096)
    x3t = arena[:, o_x3:o_x3 + 4096].bitcast(F32).rearrange(
        "p (s h) -> p s h", s=2)
    h_a = arena[:, 0:32768].rearrange("p (c t) -> p c t", c=32)
    h_b = arena[:, 32768:65536].rearrange("p (c t) -> p c t", c=32)

    x_fmb = nc.alloc_sbuf_tensor("x_fmb", [128, NHC, T], BF16)
    x4t = nc.alloc_sbuf_tensor("x4t", [128, 1, D], F32)
    xtmb = nc.alloc_sbuf_tensor("xtmb", [128, 1, D], BF16)
    yT_sb = nc.alloc_sbuf_tensor("yT_sb", [128, 65], F32)
    z1 = nc.alloc_sbuf_tensor("z1", [128, 1], F32)
    ybar = nc.alloc_sbuf_tensor("ybar", [128, HD], F32)
    st = st_alloc(nc)
    qkm_t = nc.alloc_sbuf_tensor("qkm_t", [1, 16], F32)
    csc = nc.alloc_sbuf_tensor("cs", [1, 4], F32)
    cb = nc.alloc_sbuf_tensor("cb", [128, 1], F32)
    ncb = nc.alloc_sbuf_tensor("ncb", [128, 1], F32)
    inv32 = nc.alloc_sbuf_tensor("inv32", [128, 1], F32)
    onesf = nc.alloc_sbuf_tensor("onesf", [1, 128], F32)
    eyef = nc.alloc_sbuf_tensor("eyef_t", [128, 128], F32)
    s1_t = nc.alloc_sbuf_tensor("s1_t", [128, 32], F32)
    s2_t = nc.alloc_sbuf_tensor("s2_t", [128, 32], F32)
    h1s_t = nc.alloc_sbuf_tensor("h1s_t", [128, 32], F32)
    h2s_t = nc.alloc_sbuf_tensor("h2s_t", [128, 32], F32)
    s3r = nc.alloc_sbuf_tensor("s3r", [1, D], F32)
    s3_bb = nc.alloc_sbuf_tensor("s3_bb", [128, D], F32)
    hw_b = nc.alloc_sbuf_tensor("hw_b", [128, 2, 32, 128], BF16)
    w1_b = hw_b[:, :, 0:NHC, :]
    w2_b = hw_b[:, :, NHC:2 * NHC, :]
    w3_b = hw_b[:].rearrange("p s c f -> p s (c f)")[:, :, 0:1024]
    silu_s = nc.alloc_sbuf_tensor("silu_s", [128, 1024], F32)
    t2_s = nc.alloc_sbuf_tensor("t2_s", [128, 1024], F32)
    o_sb = silu_s[:]

    psSA = nc.alloc_psum_tensor("psSA", [128, 1024], F32)
    psSB = nc.alloc_psum_tensor("psSB", [128, 1024], F32)
    psY = nc.alloc_psum_tensor("psY", [128, 1024], F32)
    psC = nc.alloc_psum_tensor("psC", [128, 512], F32)
    psT = nc.alloc_psum_tensor("psT", [128, 512], F32)

    p = Prog(nc)
    p.sems("sy", "gp", "pe", "act", "dve")

    with nc.Block() as b:
        boot_clear(b, nc, p)
        def sy0(s):
            p.inc(s.dma_start(k_sb[:, :], kfull[:]), "sy", 16)
            p.inc(s.dma_start(qf_b[:, :], qfd[:]), "sy", 16)
            p.inc(s.dma_start(qkm_t[:], qkmd[:]), "sy", 16)
            p.inc(s.dma_start(eyef[:], eyef_d[:]), "sy", 16)
            p.inc(s.dma_start(s1_t[:], s1d[:]), "sy", 16)
            p.inc(s.dma_start(s2_t[:], s2d[:]), "sy", 16)
            p.inc(s.dma_start(h1s_t[:], h1sd[:]), "sy", 16)
            p.inc(s.dma_start(h2s_t[:], h2sd[:]), "sy", 16)
            p.inc(s.dma_start(s3r[:], s3d[:]), "sy", 16)
        b.sync(sy0)
        def dv0(v):
            p.inc(v.memset(onesf[:], 1.0), "dve")
            p.inc(v.memset(v65[:, :, 64:65], 1.0), "dve")
            p.inc(v.memset(inv32[:], 1.0 / 32.0), "dve")
        b.vector(dv0)
        st_init(p, b, st)
        def sy1(s):
            p.wait(s, "dve")
            p.inc(s.dma_start(v65[:, :, 0:64],
                              vfull[:].rearrange("(c p) d -> p c d", p=128)), "sy", 16)
        b.sync(sy1)
        def dv_c(v):
            p.wait(v, "sy")
            p.inc(v.tensor_reduce(csc[0:1, 0:2],
                                  qkm_t[:].rearrange("o (a c) -> o a c", a=2),
                                  AX.X, ALU.max), "dve")
            p.selfw(v, "dve")
            p.inc(v.tensor_add(csc[0:1, 2:3], csc[0:1, 0:1], csc[0:1, 1:2]), "dve")
            p.selfw(v, "dve")
            p.inc(v.tensor_single_scalar(csc[0:1, 3:4], csc[0:1, 2:3], 1.0 / 64.0,
                                         ALU.mult), "dve")
        b.vector(dv_c)
        def pe_c(t):
            p.wait(t, "dve")
            p.inc(t.matmul(psC[:, 0:1], onesf[0:1, :], csc[0:1, 3:4],
                           start=True, stop=True), "pe")
        b.tensor(pe_c)
        def dv_c2(v):
            p.wait(v, "pe")
            p.inc(v.tensor_copy(cb[:], psC[:, 0:1]), "dve")
            p.selfw(v, "dve")
            p.inc(v.tensor_single_scalar(ncb[:], cb[:], -1.0, ALU.mult), "dve")
        b.vector(dv_c2)

        _sc_e = nc.named_scope("B_attn"); _sc_e.__enter__()
        for th in range(2):
            toff = th * 1024
            act_b = p.c["act"]
            spe, vpe = [], []
            def pe_at(t, toff=toff, act_b=act_b, spe=spe, vpe=vpe):
                p.wait(t, "sy")
                p.wait(t, "dve")
                for kc in range(NK):
                    bank = psSA if kc % 2 == 0 else psSB
                    if kc >= 2:
                        t.wait_ge(p.s["act"], act_b + (kc - 2) + 1)
                    for nn in range(2):
                        mm = t.matmul(bank[:, nn * 512:(nn + 1) * 512],
                                      k_sb[:, kc * 128:(kc + 1) * 128],
                                      qf_b[:, toff + nn * 512:toff + (nn + 1) * 512],
                                      start=True, stop=True)
                    spe.append(p.inc(mm, "pe"))
                    if kc >= 1:
                        pk = kc - 1
                        t.wait_ge(p.s["act"], act_b + pk + 1)
                        for nn in range(2):
                            mm = t.matmul(psY[0:65, nn * 512:(nn + 1) * 512],
                                          v65[:, pk, :],
                                          pt_a[:, pk % 2, nn * 512:(nn + 1) * 512],
                                          start=(pk == 0), stop=False)
                        vpe.append(p.inc(mm, "pe"))
                t.wait_ge(p.s["act"], act_b + NK)
                for nn in range(2):
                    mm = t.matmul(psY[0:65, nn * 512:(nn + 1) * 512], v65[:, NK - 1, :],
                                  pt_a[:, (NK - 1) % 2, nn * 512:(nn + 1) * 512],
                                  start=False, stop=True)
                vpe.append(p.inc(mm, "pe"))
            b.tensor(pe_at)
            def act_at(a, spe=spe, vpe=vpe):
                for kc in range(NK):
                    p.wait(a, "pe", spe[kc])
                    if kc >= 2:
                        p.wait(a, "pe", vpe[kc - 2])
                    bank = psSA if kc % 2 == 0 else psSB
                    p.selfw(a, "act")
                    p.inc(a.activation(pt_a[:, kc % 2, :], bank[:], AF.Exp,
                                       bias=ncb[:], scale=inv32[:]), "act")
            b.scalar(act_at)
            def dv_y(v):
                p.wait(v, "pe")
                p.inc(v.tensor_copy(y_sb[0:65, :], psY[0:65, :]), "dve")
            b.vector(dv_y)

            for tj in range(8):
                tc = th * 8 + tj
                sl = 0
                def sy_ld(s, tc=tc, sl=sl):
                    p.wait(s, "dve")
                    p.inc(s.dma_start(x3t[:, sl, :], x3[tc * 128:(tc + 1) * 128, :]),
                          "sy", 16)
                b.sync(sy_ld)
                def pe_t(t, tj=tj):
                    p.wait(t, "dve")
                    p.inc(t.transpose(psT[:, 0:65],
                                      y_sb[0:65, tj * 128:(tj + 1) * 128],
                                      eyef[0:65, 0:65]), "pe")
                b.tensor(pe_t)
                def dv_t(v, tc=tc, sl=sl):
                    p.wait(v, "pe")
                    p.wait(v, "sy")
                    p.selfw(v, "dve")
                    p.inc(v.tensor_copy(yT_sb[:], psT[:, 0:65]), "dve")
                    p.selfw(v, "dve")
                    p.inc(v.reciprocal(z1[:], yT_sb[:, 64:65]), "dve")
                    p.selfw(v, "dve")
                    p.inc(v.tensor_scalar(ybar[:], yT_sb[:, 0:64], z1[:], 0.0,
                                          ALU.mult, ALU.add), "dve")
                    p.selfw(v, "dve")
                    p.inc(v.tensor_tensor(
                        x4t[:, sl, :].rearrange("p (r d) -> p r d", r=16),
                        x3t[:, sl, :].rearrange("p (r d) -> p r d", r=16),
                        ybar[:].unsqueeze(1).broadcast_to([128, 16, HD]), ALU.add),
                        "dve")
                b.vector(dv_t)
                ln_tile(p, b, x4t[:, sl, :], st, tc % NTC)
                def dv_cp(v, sl=sl):
                    p.wait(v, "sy")
                    p.selfw(v, "dve")
                    p.inc(v.tensor_copy(xtmb[:, sl, :], x4t[:, sl, :]), "dve")
                b.vector(dv_cp)
                def sy_st(s, tc=tc, sl=sl):
                    p.wait(s, "dve")
                    p.inc(s.dma_start(x4_d[tc * 128:(tc + 1) * 128, :], x4t[:, sl, :]),
                          "sy", 16)
                    for hc in range(NHC):
                        p.selfw(s, "sy")
                        p.inc(s.dma_start(x_fmb[:, hc, tc * 128:(tc + 1) * 128],
                                          xtmb[:, sl, hc * 128:(hc + 1) * 128],
                                          transpose=True), "sy", 16)
                b.sync(sy_st)

        _sc_e.__exit__(None, None, None)
        _sc_f = nc.named_scope("B_ffn"); _sc_f.__enter__()
        def pe_s3(t):
            p.wait(t, "dve")
            p.inc(t.matmul(psC[:, :], onesf[0:1, :], s3r[0:1, 0:512],
                           start=True, stop=True), "pe")
            p.inc(t.matmul(psT[:, :], onesf[0:1, :], s3r[0:1, 512:1024],
                           start=True, stop=True), "pe")
        b.tensor(pe_s3)
        def dv_s3(v):
            p.wait(v, "pe")
            p.selfw(v, "dve")
            p.inc(v.tensor_copy(s3_bb[:, 0:512], psC[:]), "dve")
            p.inc(v.tensor_copy(s3_bb[:, 512:1024], psT[:]), "dve")
        b.vector(dv_s3)

        for th in range(2):
            toff = th * 1024

            pe_b = p.c["pe"]
            act_b = p.c["act"]
            dve_b = p.c["dve"]
            gp_b = p.c["gp"]
            def gp_w1(g, pe_b=pe_b):
                for fc in range(32):
                    if fc < 2:
                        if pe_b > 0:
                            g.wait_ge(p.s["pe"], pe_b)
                    else:
                        g.wait_ge(p.s["pe"], pe_b + (fc - 2) + 1)
                    p.inc(g.dma_start(w1_b[:, fc % 2, :, :], w1T[fc]), "gp", 16)
                    p.inc(g.dma_start(w2_b[:, fc % 2, :, :], w2T[fc]), "gp", 16)
            b.gpsimd(gp_w1)
            def pe_s1(t, toff=toff, gp_b=gp_b, act_b=act_b):
                p.wait(t, "sy")
                p.wait(t, "dve")
                for fc in range(32):
                    t.wait_ge(p.s["gp"], gp_b + (fc + 1) * 32)
                    if fc >= 1:
                        t.wait_ge(p.s["act"], act_b + 2 * (fc - 1) + 2)
                    for hc in range(NHC):
                        for nn in range(2):
                            mm = t.matmul(psSA[:, nn * 512:(nn + 1) * 512],
                                          w1_b[:, fc % 2, hc, :],
                                          x_fmb[:, hc, toff + nn * 512:toff + (nn + 1) * 512],
                                          start=(hc == 0), stop=(hc == NHC - 1))
                    for hc in range(NHC):
                        for nn in range(2):
                            mm = t.matmul(psSB[:, nn * 512:(nn + 1) * 512],
                                          w2_b[:, fc % 2, hc, :],
                                          x_fmb[:, hc, toff + nn * 512:toff + (nn + 1) * 512],
                                          start=(hc == 0), stop=(hc == NHC - 1))
                    p.inc(mm, "pe")
            b.tensor(pe_s1)
            s1pe_list = list(range(pe_b + 1, pe_b + 33))
            def act_s1(a, dve_b=dve_b, s1pe_list=s1pe_list):
                for fc in range(32):
                    p.wait(a, "pe", s1pe_list[fc])
                    if fc >= 1:
                        a.wait_ge(p.s["dve"], dve_b + fc)
                    p.selfw(a, "act")
                    p.inc(a.activation(silu_s[:], psSA[:], AF.Silu,
                                       bias=0.0, scale=s1_t[:, fc:fc + 1]), "act")
                    p.inc(a.activation(t2_s[:], psSB[:], AF.Copy,
                                       bias=0.0, scale=s2_t[:, fc:fc + 1]), "act")
            b.scalar(act_s1)
            def dv_s1(v, act_b=act_b):
                for fc in range(32):
                    v.wait_ge(p.s["act"], act_b + 2 * fc + 2)
                    p.selfw(v, "dve")
                    p.inc(v.tensor_mul(h_a[:, fc, :], silu_s[:], t2_s[:]), "dve")
            b.vector(dv_s1)

            for wT, ssc, src, dst in ((h1T, h1s_t, h_a, h_b), (h2T, h2s_t, h_b, h_a)):
                pe_b = p.c["pe"]
                act_b = p.c["act"]
                gp_b = p.c["gp"]
                def gp_wh(g, wT=wT, pe_b=pe_b):
                    for fc in range(32):
                        if fc < 2:
                            g.wait_ge(p.s["pe"], pe_b)
                        else:
                            g.wait_ge(p.s["pe"], pe_b + (fc - 2) + 1)
                        p.inc(g.dma_start(hw_b[:, fc % 2, :, :], wT[fc]), "gp", 16)
                b.gpsimd(gp_wh)
                def pe_h(t, src=src, gp_b=gp_b, act_b=act_b):
                    p.wait(t, "dve")
                    p.wait(t, "act")
                    for fc in range(32):
                        t.wait_ge(p.s["gp"], gp_b + (fc + 1) * 16)
                        if fc >= 1:
                            t.wait_ge(p.s["act"], act_b + fc)
                        for kc in range(32):
                            for nn in range(2):
                                mm = t.matmul(psSA[:, nn * 512:(nn + 1) * 512],
                                              hw_b[:, fc % 2, kc, :],
                                              src[:, kc, nn * 512:(nn + 1) * 512],
                                              start=(kc == 0), stop=(kc == 31))
                        p.inc(mm, "pe")
                b.tensor(pe_h)
                hpe_list = list(range(pe_b + 1, pe_b + 33))
                def act_h(a, ssc=ssc, dst=dst, hpe_list=hpe_list):
                    for fc in range(32):
                        p.wait(a, "pe", hpe_list[fc])
                        p.selfw(a, "act")
                        p.inc(a.activation(dst[:, fc, :], psSA[:], AF.Gelu,
                                           bias=0.0, scale=ssc[:, fc:fc + 1]), "act")
                b.scalar(act_h)

            for tg in range(2):
                banks = [(psSA[:, 0:512], psSA[:, 512:1024]),
                         (psSB[:, 0:512], psSB[:, 512:1024]),
                         (psY[:, 0:512], psY[:, 512:1024]),
                         (psC[:, :], psT[:, :])]
                pe_b = p.c["pe"]
                gp_b = p.c["gp"]
                def gp_w3(g, tg=tg, pe_b=pe_b):
                    for kc in range(32):
                        if kc < 2:
                            g.wait_ge(p.s["pe"], pe_b)
                        else:
                            g.wait_ge(p.s["pe"], pe_b + (kc - 2) + 1)
                        p.inc(g.dma_start(w3_b[:, kc % 2, :],
                                          w3T[kc * 128:(kc + 1) * 128, :]), "gp", 16)
                b.gpsimd(gp_w3)
                def pe_w3(t, tg=tg, gp_b=gp_b, banks=banks):
                    p.wait(t, "act")
                    p.wait(t, "dve")
                    for kc in range(32):
                        t.wait_ge(p.s["gp"], gp_b + (kc + 1) * 16)
                        for tcc in range(4):
                            lh = h_a[:, kc,
                                     tg * 512 + tcc * 128:tg * 512 + (tcc + 1) * 128]
                            for nn in range(2):
                                mm = t.matmul(banks[tcc][nn], lh,
                                              w3_b[:, kc % 2, nn * 512:(nn + 1) * 512],
                                              start=(kc == 0), stop=(kc == 31))
                        p.inc(mm, "pe")
                b.tensor(pe_w3)
                for tcc in range(4):
                    tc = th * 8 + tg * 4 + tcc
                    sl = 0
                    def sy_l4(s, tc=tc, sl=sl):
                        p.wait(s, "dve")
                        p.inc(s.dma_start(x4t[:, sl, :],
                                          x4_d[tc * 128:(tc + 1) * 128, :]), "sy", 16)
                    b.sync(sy_l4)
                    def dv_o(v, tcc=tcc, sl=sl, banks=banks):
                        p.wait(v, "pe")
                        p.wait(v, "sy")
                        p.selfw(v, "dve")
                        p.inc(v.tensor_copy(o_sb[:, 0:512], banks[tcc][0]), "dve")
                        p.inc(v.tensor_copy(o_sb[:, 512:1024], banks[tcc][1]), "dve")
                        p.selfw(v, "dve")
                        p.inc(v.tensor_mul(o_sb[:], o_sb[:], s3_bb[:]), "dve")
                        p.selfw(v, "dve")
                        p.inc(v.tensor_add(x4t[:, sl, :], x4t[:, sl, :], o_sb[:]), "dve")
                    b.vector(dv_o)
                    ln_tile(p, b, x4t[:, sl, :], st, tc % NTC)
                    def sy_o(s, tc=tc, sl=sl):
                        p.wait(s, "dve")
                        p.inc(s.dma_start(out_o[tc * 128:(tc + 1) * 128, :],
                                          x4t[:, sl, :]), "sy", 16)
                    b.sync(sy_o)

        _sc_f.__exit__(None, None, None)
        def sy_fin(s):
            p.wait(s, "sy")
        b.sync(sy_fin)
    return nc



_bf = ml_dtypes.bfloat16


def _tile_w(w, nkc):
    a = np.asarray(w, np.float32).reshape(32, 128, nkc, 128)
    return np.ascontiguousarray(a.transpose(0, 3, 2, 1)).astype(_bf)


def sel8_host():
    s = np.zeros((NEXP, NEXP * 128), dtype=_bf)
    for e in range(NEXP):
        s[e, e * 128:(e + 1) * 128] = 1.0
    return s


def kernel(L0_attn_x, L0_attn_q_proj, L0_attn_k_proj, L0_ffn_gate_w, L0_ffn_expert_w,
           L1_attn_Wq, L1_attn_bq, L1_attn_Wk, L1_attn_bk, L1_attn_Wv, L1_attn_bv,
           L1_ffn_w1_w, L1_ffn_w1_s, L1_ffn_w2_w, L1_ffn_w2_s,
           L1_ffn_h1_w, L1_ffn_h1_s, L1_ffn_h2_w, L1_ffn_h2_s,
           L1_ffn_w3_w, L1_ffn_w3_s):
    f32 = np.float32
    ct = np.ascontiguousarray
    x = np.asarray(L0_attn_x, f32)
    eyeb = np.eye(128, dtype=_bf)
    eyef = np.eye(128, dtype=f32)

    common_A = {
        "qp": ct(np.asarray(L0_attn_q_proj, f32)),
        "kp": ct(np.asarray(L0_attn_k_proj, f32)),
        "gT": ct(np.asarray(L0_ffn_gate_w, f32).T),
        "ewT": ct(np.asarray(L0_ffn_expert_w, f32).transpose(0, 2, 1)),
        "wqT": ct(np.asarray(L1_attn_Wq, f32).T),
        "wkT": ct(np.asarray(L1_attn_Wk, f32).T),
        "wvT": ct(np.asarray(L1_attn_Wv, f32).T),
        "bq": ct(np.asarray(L1_attn_bq, f32).reshape(HD, 1)),
        "bk": ct(np.asarray(L1_attn_bk, f32).reshape(HD, 1)),
        "bv": ct(np.asarray(L1_attn_bv, f32).reshape(HD, 1)),
        "eyeb": eyeb, "eyef": eyef, "sel8": sel8_host(),
    }
    in_maps_A = []
    for c in range(NCORE):
        m = dict(common_A)
        m["xa"] = ct(x[c])
        m["xaT"] = ct(x[c].T)
        in_maps_A.append(m)

    ncA = build_A()
    resA = run_bass_kernel_spmd(ncA, in_maps_A, core_ids=list(range(NCORE)))
    ra = resA.results

    kfull = ct(np.concatenate([np.asarray(ra[c]["k_o"]) for c in range(NCORE)], axis=1))
    vfull = ct(np.concatenate([np.asarray(ra[c]["v_o"]) for c in range(NCORE)], axis=0))
    qkm = ct(np.concatenate([np.asarray(ra[c]["qkm_o"], f32) for c in range(NCORE)],
                            axis=0).T.reshape(1, 16))

    sc = lambda s: ct(np.asarray(s, f32).reshape(-1, 128).T)
    common_B = {
        "kfull": kfull, "vfull": vfull, "qkm": qkm,
        "w1T": _tile_w(L1_ffn_w1_w, NHC),
        "w2T": _tile_w(L1_ffn_w2_w, NHC),
        "h1T": _tile_w(L1_ffn_h1_w, 32),
        "h2T": _tile_w(L1_ffn_h2_w, 32),
        "w3T": ct(np.asarray(L1_ffn_w3_w, f32).T.astype(_bf)),
        "s1d": sc(L1_ffn_w1_s), "s2d": sc(L1_ffn_w2_s),
        "h1sd": sc(L1_ffn_h1_s), "h2sd": sc(L1_ffn_h2_s),
        "s3d": ct(np.asarray(L1_ffn_w3_s, f32).reshape(1, D)),
        "eyef": eyef,
    }
    in_maps_B = []
    for c in range(NCORE):
        m = dict(common_B)
        m["x3"] = np.asarray(ra[c]["x3_o"], f32)
        m["qf"] = np.asarray(ra[c]["q_o"])
        in_maps_B.append(m)

    ncB = build_B()
    resB = run_bass_kernel_spmd(ncB, in_maps_B, core_ids=list(range(NCORE)))
    out = np.stack([np.asarray(resB.results[c]["out_o"], f32) for c in range(NCORE)],
                   axis=0)
    return out


# version 12
# speedup vs baseline: 1.4214x; 1.0516x over previous
import numpy as np
import ml_dtypes
import concourse.bass as bass
import concourse.mybir as mybir
from concourse.bass_utils import run_bass_kernel_spmd

F32 = mybir.dt.float32
BF16 = mybir.dt.bfloat16
AF = mybir.ActivationFunctionType
ALU = mybir.AluOpType
AX = mybir.AxisListType

NCORE = 8
T = 2048
D = 1024
E = 256
NEXP = 8
HD = 64
DH = 4096
NTC = T // 128
NHC = D // 128
LN_EPS = 1e-5


class Prog:

    def __init__(self, nc):
        self.nc = nc
        self.s = {}
        self.c = {}

    def sems(self, *names):
        for n in names:
            self.s[n] = self.nc.alloc_semaphore(n)
            self.c[n] = 0

    def inc(self, instr, name, k=1):
        instr.then_inc(self.s[name], k)
        self.c[name] += k
        return self.c[name]

    def wait(self, eng, name, val=None):
        v = self.c[name] if val is None else val
        if v > 0:
            eng.wait_ge(self.s[name], v)

    def selfw(self, eng, name):
        if self.c[name] > 0:
            eng.wait_ge(self.s[name], self.c[name])


import os
_LN_MODE = os.environ.get("LN_MODE", "ts2")
_LN_DELAY = bool(int(os.environ.get("LN_DELAY", "0")))


def ln_stats_batch(p, b, x_tm, st, ntc):
    def act(a):
        p.wait(a, "dve")
        for tc in range(ntc):
            p.selfw(a, "act")
            p.inc(a.activation(st["dump"][:], x_tm[:, tc, :], AF.Square,
                               bias=0.0, scale=1.0,
                               accum_out=st["sq"][:, tc:tc + 1]), "act")
    b.scalar(act)
    def dve(v):
        p.selfw(v, "dve")
        for tc in range(ntc):
            p.inc(v.reduce_sum(st["sum"][:, tc:tc + 1], x_tm[:, tc, :], AX.X), "dve")
        p.wait(v, "act")
        s = slice(0, ntc)
        p.selfw(v, "dve")
        p.inc(v.tensor_scalar(st["m"][:, s], st["sum"][:, s], 1.0 / D, 0.0,
                              ALU.mult, ALU.add), "dve")
        p.inc(v.tensor_scalar(st["ex2"][:, s], st["sq"][:, s], 1.0 / D, 0.0,
                              ALU.mult, ALU.add), "dve")
        p.selfw(v, "dve")
        p.inc(v.tensor_mul(st["msq"][:, s], st["m"][:, s], st["m"][:, s]), "dve")
        p.selfw(v, "dve")
        p.inc(v.tensor_sub(st["var"][:, s], st["ex2"][:, s], st["msq"][:, s]), "dve")
    b.vector(dve)
    def act2(a):
        p.wait(a, "dve")
        s = slice(0, ntc)
        p.inc(a.activation(st["lnv"][:, s], st["var"][:, s], AF.Ln,
                           bias=st["eps"][:], scale=1.0), "act")
        p.selfw(a, "act")
        p.inc(a.activation(st["rstd"][:, s], st["lnv"][:, s], AF.Exp,
                           bias=0.0, scale=st["mhalf"][:]), "act")
    b.scalar(act2)
    def dve2(v):
        p.wait(v, "act")
        s = slice(0, ntc)
        p.inc(v.tensor_mul(st["mh"][:, s], st["m"][:, s], st["rstd"][:, s]), "dve")
        p.selfw(v, "dve")
        p.inc(v.tensor_scalar(st["nmh"][:, s], st["mh"][:, s], -1.0, 0.0,
                              ALU.mult, ALU.add), "dve")
    b.vector(dve2)


def ln_apply_and_fm(p, b, x_tm, x_tmb, x_fmb, st, ntc, PBb, eyeb):
    dve0 = p.c["dve"]
    def dve(v):
        for tc in range(ntc):
            p.selfw(v, "dve")
            p.inc(v.tensor_scalar(x_tm[:, tc, :], x_tm[:, tc, :],
                                  st["rstd"][:, tc:tc + 1], st["nmh"][:, tc:tc + 1],
                                  ALU.mult, ALU.add), "dve")
            p.selfw(v, "dve")
            p.inc(v.tensor_copy(x_tmb[:, tc, :], x_tm[:, tc, :]), "dve")
    b.vector(dve)
    dvc0 = dve0 + 2 * ntc
    trs = []
    def pe_tr(t):
        p.wait(t, "act")
        for g in range(ntc * NHC):
            tc, hc = divmod(g, NHC)
            t.wait_ge(p.s["dve"], dve0 + 2 * (tc + 1))
            if g >= 4:
                t.wait_ge(p.s["dve"], dvc0 + (g - 4) + 1)
            tr = t.transpose(PBb[g % 4][0:128, 0:128],
                             x_tmb[:, tc, hc * 128:(hc + 1) * 128], eyeb[:, :])
            trs.append(p.inc(tr, "pe"))
    b.tensor(pe_tr)
    def dv_tr(v):
        for g in range(ntc * NHC):
            tc, hc = divmod(g, NHC)
            p.wait(v, "pe", trs[g])
            p.inc(v.tensor_copy(x_fmb[:, hc, tc * 128:(tc + 1) * 128],
                                PBb[g % 4][0:128, 0:128]), "dve")
    b.vector(dv_tr)


def ln_tile(p, b, x_ap, st, slot):
    sl = slice(slot, slot + 1)
    def act(a):
        p.wait(a, "dve")
        p.inc(a.activation(st["dump"][:], x_ap, AF.Square, bias=0.0, scale=1.0,
                           accum_out=st["sq"][:, sl]), "act")
    b.scalar(act)
    def dve(v):
        p.selfw(v, "dve")
        p.inc(v.reduce_sum(st["sum"][:, sl], x_ap, AX.X), "dve")
        p.wait(v, "act")
        p.selfw(v, "dve")
        p.inc(v.tensor_scalar(st["m"][:, sl], st["sum"][:, sl], 1.0 / D, 0.0,
                              ALU.mult, ALU.add), "dve")
        p.inc(v.tensor_scalar(st["ex2"][:, sl], st["sq"][:, sl], 1.0 / D, 0.0,
                              ALU.mult, ALU.add), "dve")
        p.selfw(v, "dve")
        p.inc(v.tensor_mul(st["msq"][:, sl], st["m"][:, sl], st["m"][:, sl]), "dve")
        p.selfw(v, "dve")
        p.inc(v.tensor_sub(st["var"][:, sl], st["ex2"][:, sl], st["msq"][:, sl]), "dve")
    b.vector(dve)
    def act2(a):
        p.wait(a, "dve")
        p.inc(a.activation(st["lnv"][:, sl], st["var"][:, sl], AF.Ln,
                           bias=st["eps"][:], scale=1.0), "act")
        p.selfw(a, "act")
        p.inc(a.activation(st["rstd"][:, sl], st["lnv"][:, sl], AF.Exp,
                           bias=0.0, scale=st["mhalf"][:]), "act")
    b.scalar(act2)
    def dve2(v):
        p.wait(v, "act")
        p.inc(v.tensor_mul(st["mh"][:, sl], st["m"][:, sl], st["rstd"][:, sl]), "dve")
        p.selfw(v, "dve")
        p.inc(v.tensor_scalar(st["nmh"][:, sl], st["mh"][:, sl], -1.0, 0.0,
                              ALU.mult, ALU.add), "dve")
        p.selfw(v, "dve")
        p.inc(v.tensor_scalar(x_ap, x_ap, st["rstd"][:, sl], st["nmh"][:, sl],
                              ALU.mult, ALU.add), "dve")
    b.vector(dve2)


_BOOT = bool(int(os.environ.get("KBOOT", "1")))


def boot_clear(b, nc, p):
    if not _BOOT:
        return
    """Clear OUR kernel semaphores at program start (crashed programs leave
    stale values; a stale sem makes every wait_ge pass early). gpsimd clears;
    the other engines sit in a timed nop longer than the clear, then check a
    boot sem that the clear zeroed before it was incremented. Bass's own
    barrier sems are left alone (their protocol is self-normalizing)."""
    bs = nc.alloc_semaphore("bootsem")
    lo = min(h.num for h in list(p.s.values()) + [bs])
    rng_ = range(lo, nc._kernel_sem_range.stop)
    def gp(g):
        g.sem_clear(rng_)
        g.sem_inc(bs, 1)
    b.gpsimd(gp)
    for name in ("tensor", "vector", "scalar", "sync"):
        def w(e):
            e.nop(cycle_cnt=20000)
            e.wait_ge(bs, 1)
        getattr(b, name)(w)


def st_alloc(nc):
    st = {n: nc.alloc_sbuf_tensor("st_" + n, [128, NTC], F32)
          for n in ("sum", "sq", "m", "ex2", "msq", "var", "lnv", "rstd", "mh", "nmh")}
    st["dump"] = nc.alloc_sbuf_tensor("st_dump", [128, D], BF16)
    st["eps"] = nc.alloc_sbuf_tensor("st_eps", [128, 1], F32)
    st["mhalf"] = nc.alloc_sbuf_tensor("st_mhalf", [128, 1], F32)
    return st


def st_init(p, b, st):
    def dv(v):
        p.inc(v.memset(st["eps"][:], LN_EPS), "dve")
        p.inc(v.memset(st["mhalf"][:], -0.5), "dve")
    b.vector(dv)



class _Stop(Exception):
    pass


def _chk(upto, n):
    if upto < n:
        raise _Stop


def build_A(upto=99):
    nc = bass.Bass("TRN2", target_bir_lowering=False)
    xa = nc.dram_tensor("xa", [T, D], F32, kind="ExternalInput")
    xaT = nc.dram_tensor("xaT", [D, T], F32, kind="ExternalInput")
    qp = nc.dram_tensor("qp", [D, E], F32, kind="ExternalInput")
    kp = nc.dram_tensor("kp", [D, E], F32, kind="ExternalInput")
    gTd = nc.dram_tensor("gT", [D, NEXP], F32, kind="ExternalInput")
    ewT = nc.dram_tensor("ewT", [NEXP, D, D], F32, kind="ExternalInput")
    wqT = nc.dram_tensor("wqT", [D, HD], F32, kind="ExternalInput")
    wkT = nc.dram_tensor("wkT", [D, HD], F32, kind="ExternalInput")
    wvT = nc.dram_tensor("wvT", [D, HD], F32, kind="ExternalInput")
    bqd = nc.dram_tensor("bq", [HD, 1], F32, kind="ExternalInput")
    bkd = nc.dram_tensor("bk", [HD, 1], F32, kind="ExternalInput")
    bvd = nc.dram_tensor("bv", [HD, 1], F32, kind="ExternalInput")
    eyeb_d = nc.dram_tensor("eyeb", [128, 128], BF16, kind="ExternalInput")
    eyef_d = nc.dram_tensor("eyef", [128, 128], F32, kind="ExternalInput")
    sel8_d = nc.dram_tensor("sel8", [NEXP, NEXP * 128], BF16, kind="ExternalInput")

    x3_o = nc.dram_tensor("x3_o", [T, D], F32, kind="ExternalOutput")
    q_o = nc.dram_tensor("q_o", [HD, T], BF16, kind="ExternalOutput")
    k_o = nc.dram_tensor("k_o", [HD, T], BF16, kind="ExternalOutput")
    v_o = nc.dram_tensor("v_o", [T, HD], BF16, kind="ExternalOutput")
    qkm_o = nc.dram_tensor("qkm_o", [1, 2], F32, kind="ExternalOutput")

    x_tm = nc.alloc_sbuf_tensor("x_tm", [128, NTC, D], F32)
    x_tmb = nc.alloc_sbuf_tensor("x_tmb", [128, NTC, D], BF16)
    x_fmb = nc.alloc_sbuf_tensor("x_fmb", [128, NHC, T], BF16)
    qpb = nc.alloc_sbuf_tensor("qpb", [128, NHC, E], BF16)
    kpb = nc.alloc_sbuf_tensor("kpb", [128, NHC, E], BF16)
    wq_b = qpb[:, :, 0:HD]
    wk_b = qpb[:, :, HD:2 * HD]
    wv_b = qpb[:, :, 2 * HD:3 * HD]
    pool_a = nc.alloc_sbuf_tensor("pool_a", [128, 4096], BF16)
    qe = pool_a[:].rearrange("p (a t) -> p a t", a=2)
    mb_all = pool_a[:].rearrange("p (e t) -> p e t", e=NEXP)
    pool_b = nc.alloc_sbuf_tensor("pool_b", [128, 8192], BF16)
    ke = pool_b[:, 0:4096].rearrange("p (a e) -> p a e", a=NTC)
    moe_fm2 = pool_b[:].rearrange("p (s f t) -> p s f t", s=2, f=NEXP)
    kv_b = nc.alloc_sbuf_tensor("kv_b", [128, 2, D], BF16)
    zrow = nc.alloc_sbuf_tensor("zrow", [1, T], F32)
    z_tm = nc.alloc_sbuf_tensor("z_tm", [128, NTC], F32)
    ones_b = nc.alloc_sbuf_tensor("ones_b", [128, 1], BF16)
    onesr_b = nc.alloc_sbuf_tensor("onesr_b", [1, 128], BF16)
    eyeb = nc.alloc_sbuf_tensor("eyeb_t", [128, 128], BF16)
    eyef = nc.alloc_sbuf_tensor("eyef_t", [128, 128], F32)
    st = st_alloc(nc)
    gTb = nc.alloc_sbuf_tensor("gTb", [128, NHC, NEXP], BF16)
    scor = nc.alloc_sbuf_tensor("scor", [128, NTC, NEXP], F32)
    gmax = nc.alloc_sbuf_tensor("gmax", [128, NTC], F32)
    pre = nc.alloc_sbuf_tensor("pre", [128, NTC], F32)
    tta = nc.alloc_sbuf_tensor("tta", [128, NTC], F32)
    ttb = nc.alloc_sbuf_tensor("ttb", [128, NTC], F32)
    masks_tm = nc.alloc_sbuf_tensor("masks_tm", [128, NTC, NEXP], BF16)
    shr_a = nc.alloc_sbuf_tensor("shr_a", [HD, T], BF16)
    masks_f = shr_a[0:NEXP, :]
    mtile = nc.alloc_sbuf_tensor("mtile", [128, 512], BF16)
    w_moe = nc.alloc_sbuf_tensor("w_moe", [128, 2, 1024], BF16)
    moe_tm2 = nc.alloc_sbuf_tensor("moe_tm2", [128, 2, D], BF16)
    bq_t = nc.alloc_sbuf_tensor("bq_t", [HD, 1], F32)
    bk_t = nc.alloc_sbuf_tensor("bk_t", [HD, 1], F32)
    bv_t = nc.alloc_sbuf_tensor("bv_t", [HD, 1], F32)
    qf = nc.alloc_sbuf_tensor("qf", [HD, T], BF16)
    kf = nc.alloc_sbuf_tensor("kf", [HD, T], BF16)
    vf = nc.alloc_sbuf_tensor("vf", [HD, T], BF16)
    sqn = shr_a[:, :]
    v_tmb = nc.alloc_sbuf_tensor("v_tmb", [128, NTC, HD], BF16)
    qkm_t = nc.alloc_sbuf_tensor("qkm_t", [1, 2], F32)
    tmp_res = nc.alloc_sbuf_tensor("tmp_res", [128, 512], F32)
    sel8 = tmp_res[:].bitcast(BF16)[0:NEXP, :]

    adr = lambda tt: nc.lookup_mloc(tt).addr
    w_moe2 = nc.alloc_sbuf_tensor_at("w_moe2", [128, 2, 1024], BF16, offset=adr(kf))
    mtile2 = nc.alloc_sbuf_tensor_at("mtile2", [128, 512], BF16, offset=adr(vf))

    def wslot(i):
        return (w_moe if (i % 4) < 2 else w_moe2)[:, i % 2, :]

    def mslot(i):
        return mtile[:] if i % 2 == 0 else mtile2[:]

    PB = [nc.alloc_psum_tensor(f"pb{i}", [128, 512], F32) for i in range(8)]
    PBb = [t[:].bitcast(BF16) for t in PB]

    p = Prog(nc)
    p.sems("sy", "gp", "pe", "act", "dve")

    with nc.Block() as b:
      try:
        boot_clear(b, nc, p)
        def sy0(s):
            p.inc(s.dma_start(x_tm[:], xa[:].rearrange("(c p) h -> p c h", p=128)), "sy", 16)
            p.inc(s.dma_start(eyeb[:], eyeb_d[:]), "sy", 16)
            p.inc(s.dma_start(eyef[:], eyef_d[:]), "sy", 16)
            p.inc(s.dma_start(bq_t[:], bqd[:]), "sy", 16)
            p.inc(s.dma_start(bk_t[:], bkd[:]), "sy", 16)
            p.inc(s.dma_start(bv_t[:], bvd[:]), "sy", 16)
        b.sync(sy0)
        def gp0(g):
            p.inc(g.dma_start(x_tmb[:], xa[:].rearrange("(c p) h -> p c h", p=128)), "gp", 16)
            p.inc(g.dma_start(x_fmb[:], xaT[:].rearrange("(c p) t -> p c t", p=128)), "gp", 16)
            p.inc(g.dma_start(qpb[:], qp[:].rearrange("(c p) e -> p c e", p=128)), "gp", 16)
            p.inc(g.dma_start(kpb[:], kp[:].rearrange("(c p) e -> p c e", p=128)), "gp", 16)
            p.inc(g.dma_start(gTb[:], gTd[:].rearrange("(c p) e -> p c e", p=128)), "gp", 16)
        b.gpsimd(gp0)
        def dv0(v):
            p.inc(v.memset(ones_b[:], 1.0), "dve")
            p.inc(v.memset(onesr_b[:], 1.0), "dve")
        b.vector(dv0)
        st_init(p, b, st)
        gp_ld = p.c["gp"]

        _chk(upto, 2)
        _sc_a = nc.named_scope("A_l0attn"); _sc_a.__enter__()
        act0 = p.c["act"]
        qpe, kpe = [], []
        def pe_qk(t):
            p.wait(t, "gp", gp_ld)
            p.wait(t, "sy")
            for ec in range(2):
                for tn in range(4):
                    g = ec * 4 + tn
                    if g >= 2:
                        t.wait_ge(p.s["act"], act0 + (g - 2) + 1)
                    bank = PB[g % 2]
                    for hc in range(NHC):
                        mm = t.matmul(bank[:], qpb[:, hc, ec * 128:(ec + 1) * 128],
                                      x_fmb[:, hc, tn * 512:(tn + 1) * 512],
                                      start=(hc == 0), stop=(hc == NHC - 1))
                    qpe.append(p.inc(mm, "pe"))
            for tc in range(NTC):
                if tc >= 2:
                    t.wait_ge(p.s["act"], act0 + 8 + (tc - 2) + 1)
                bank = PB[2 + tc % 2]
                for hc in range(NHC):
                    mm = t.matmul(bank[:, 0:E], x_fmb[:, hc, tc * 128:(tc + 1) * 128],
                                  kpb[:, hc, :], start=(hc == 0), stop=(hc == NHC - 1))
                kpe.append(p.inc(mm, "pe"))
        b.tensor(pe_qk)
        def act_qk(a):
            for ec in range(2):
                for tn in range(4):
                    g = ec * 4 + tn
                    p.wait(a, "pe", qpe[g])
                    p.selfw(a, "act")
                    p.inc(a.activation(qe[:, ec, tn * 512:(tn + 1) * 512],
                                       PB[g % 2][:], AF.Exp, bias=0.0, scale=1.0), "act")
            for tc in range(NTC):
                p.wait(a, "pe", kpe[tc])
                p.selfw(a, "act")
                p.inc(a.activation(ke[:, tc, :], PB[2 + tc % 2][:, 0:E],
                                   AF.Exp, bias=0.0, scale=1.0), "act")
        b.scalar(act_qk)

        _chk(upto, 3)
        dve0 = p.c["dve"]
        zpe = []
        def pe_z(t):
            p.wait(t, "act")
            t.wait_ge(p.s["dve"], 2)
            for tn in range(4):
                if tn >= 1:
                    t.wait_ge(p.s["dve"], dve0 + tn)
                for ec in range(2):
                    mm = t.matmul(PB[4][0:1, :], ones_b[:, 0:1],
                                  qe[:, ec, tn * 512:(tn + 1) * 512],
                                  start=(ec == 0), stop=(ec == 1))
                zpe.append(p.inc(mm, "pe"))
        b.tensor(pe_z)
        def dv_z(v):
            for tn in range(4):
                p.wait(v, "pe", zpe[tn])
                p.inc(v.tensor_copy(zrow[0:1, tn * 512:(tn + 1) * 512], PB[4][0:1, :]),
                      "dve")
            p.selfw(v, "dve")
            p.inc(v.tensor_single_scalar(zrow[:], zrow[:], 1e-8, ALU.add), "dve")
            p.selfw(v, "dve")
            p.inc(v.reciprocal(zrow[:], zrow[:]), "dve")
        b.vector(dv_z)
        dve1 = p.c["dve"]
        ztr = []
        def pe_zt(t):
            p.wait(t, "dve", dve1)
            for tc in range(NTC):
                if tc >= 2:
                    t.wait_ge(p.s["dve"], dve1 + (tc - 2) + 1)
                tr = t.transpose(PB[4 + tc % 2][:, 0:1],
                                 zrow[0:1, tc * 128:(tc + 1) * 128], eyef[0:1, 0:1])
                ztr.append(p.inc(tr, "pe"))
        b.tensor(pe_zt)
        def dv_zt(v):
            for tc in range(NTC):
                p.wait(v, "pe", ztr[tc])
                p.inc(v.tensor_copy(z_tm[:, tc:tc + 1], PB[4 + tc % 2][:, 0:1]), "dve")
        b.vector(dv_zt)

        _chk(upto, 4)
        dve2 = p.c["dve"]
        kvpe = []
        def pe_kv(t):
            for g in range(4):
                ec, hn = divmod(g, 2)
                if g >= 2:
                    t.wait_ge(p.s["dve"], dve2 + (g - 2) + 1)
                bank = PB[6 + g % 2]
                for tc in range(NTC):
                    mm = t.matmul(bank[:], ke[:, tc, ec * 128:(ec + 1) * 128],
                                  x_tmb[:, tc, hn * 512:(hn + 1) * 512],
                                  start=(tc == 0), stop=(tc == NTC - 1))
                kvpe.append(p.inc(mm, "pe"))
        b.tensor(pe_kv)
        def dv_kv(v):
            for g in range(4):
                ec, hn = divmod(g, 2)
                p.wait(v, "pe", kvpe[g])
                p.inc(v.tensor_copy(kv_b[:, ec, hn * 512:(hn + 1) * 512],
                                    PB[6 + g % 2][:]), "dve")
        b.vector(dv_kv)

        _chk(upto, 5)
        act1 = p.c["act"]
        numpe = []
        def pe_num(t):
            p.wait(t, "dve")
            for g in range(NTC * 2):
                tc, hn = divmod(g, 2)
                if g >= 2:
                    t.wait_ge(p.s["act"], act1 + (g - 2) + 1)
                bank = PB[g % 2]
                for ec in range(2):
                    mm = t.matmul(bank[:], qe[:, ec, tc * 128:(tc + 1) * 128],
                                  kv_b[:, ec, hn * 512:(hn + 1) * 512],
                                  start=(ec == 0), stop=(ec == 1))
                numpe.append(p.inc(mm, "pe"))
        b.tensor(pe_num)
        dve3 = p.c["dve"]
        numact = []
        def act_num(a):
            for g in range(NTC * 2):
                tc, hn = divmod(g, 2)
                p.wait(a, "pe", numpe[g])
                if g >= 1:
                    a.wait_ge(p.s["dve"], dve3 + g)
                p.selfw(a, "act")
                numact.append(p.inc(a.activation(tmp_res[:], PB[g % 2][:],
                                                 AF.Copy, bias=0.0,
                                                 scale=z_tm[:, tc:tc + 1]), "act"))
        b.scalar(act_num)
        def dv_num(v):
            for g in range(NTC * 2):
                tc, hn = divmod(g, 2)
                p.wait(v, "act", numact[g])
                p.selfw(v, "dve")
                p.inc(v.tensor_add(x_tm[:, tc, hn * 512:(hn + 1) * 512],
                                   x_tm[:, tc, hn * 512:(hn + 1) * 512],
                                   tmp_res[:]), "dve")
        b.vector(dv_num)

        _sc_a.__exit__(None, None, None)
        _chk(upto, 6)
        _sc_b = nc.named_scope("A_ln1"); _sc_b.__enter__()
        ln_stats_batch(p, b, x_tm, st, NTC)
        ln_apply_and_fm(p, b, x_tm, x_tmb, x_fmb, st, NTC, PBb, eyeb)

        _sc_b.__exit__(None, None, None)
        _chk(upto, 7)
        _sc_c = nc.named_scope("A_moe"); _sc_c.__enter__()
        scpe = []
        dve4 = p.c["dve"]
        def pe_sc(t):
            p.wait(t, "sy")
            p.wait(t, "dve")
            for tc in range(NTC):
                if tc >= 2:
                    t.wait_ge(p.s["dve"], dve4 + (tc - 2) + 1)
                bank = PB[2 + tc % 2]
                for hc in range(NHC):
                    mm = t.matmul(bank[:, 0:NEXP], x_fmb[:, hc, tc * 128:(tc + 1) * 128],
                                  gTb[:, hc, :], start=(hc == 0), stop=(hc == NHC - 1))
                scpe.append(p.inc(mm, "pe"))
        b.tensor(pe_sc)
        def dv_sc(v):
            for tc in range(NTC):
                p.wait(v, "pe", scpe[tc])
                p.inc(v.tensor_copy(scor[:, tc, :], PB[2 + tc % 2][:, 0:NEXP]), "dve")
            p.selfw(v, "dve")
            p.inc(v.reduce_max(gmax[:], scor[:], AX.X), "dve")
            for e in range(NEXP):
                se = scor[:, :, e]
                p.selfw(v, "dve")
                p.inc(v.tensor_tensor(tta[:], se, gmax[:], ALU.is_ge), "dve")
                if e == 0:
                    p.selfw(v, "dve")
                    p.inc(v.tensor_copy(masks_tm[:, :, 0], tta[:]), "dve")
                    p.inc(v.tensor_copy(pre[:], se), "dve")
                else:
                    p.selfw(v, "dve")
                    p.inc(v.tensor_tensor(ttb[:], pre[:], gmax[:], ALU.is_lt), "dve")
                    p.selfw(v, "dve")
                    p.inc(v.tensor_mul(tta[:], tta[:], ttb[:]), "dve")
                    p.selfw(v, "dve")
                    p.inc(v.tensor_copy(masks_tm[:, :, e], tta[:]), "dve")
                    if e < NEXP - 1:
                        p.selfw(v, "dve")
                        p.inc(v.tensor_max(pre[:], pre[:], se), "dve")
        b.vector(dv_sc)
        _chk(upto, 8)
        dve5 = p.c["dve"]
        mtr = []
        def pe_mt(t):
            p.wait(t, "dve", dve5)
            for tc in range(NTC):
                if tc >= 2:
                    t.wait_ge(p.s["dve"], dve5 + (tc - 2) + 1)
                tr = t.transpose(PBb[4 + tc % 2][0:NEXP, 0:128], masks_tm[:, tc, :],
                                 eyeb[:, :])
                mtr.append(p.inc(tr, "pe"))
        b.tensor(pe_mt)
        def dv_mt(v):
            for tc in range(NTC):
                p.wait(v, "pe", mtr[tc])
                p.inc(v.tensor_copy(masks_f[0:NEXP, tc * 128:(tc + 1) * 128],
                                    PBb[4 + tc % 2][0:NEXP, 0:128]), "dve")
        b.vector(dv_mt)

        _chk(upto, 9)
        def sy_sel(s):
            p.wait(s, "dve")
            p.inc(s.dma_start(sel8, sel8_d[:]), "sy", 16)
        b.sync(sy_sel)

        for tn in range(4):
            mbdve = []
            for e in range(NEXP):
                def pe_mb(t, e=e, tn=tn):
                    if e < 2:
                        p.wait(t, "dve")
                        p.wait(t, "sy")
                    else:
                        t.wait_ge(p.s["dve"], mbdve[e - 2])
                    p.inc(t.matmul(PB[6 + e % 2][:], sel8[:, e * 128:(e + 1) * 128],
                                   masks_f[0:NEXP, tn * 512:(tn + 1) * 512],
                                   start=True, stop=True), "pe")
                b.tensor(pe_mb)
                def dv_mb(v, e=e):
                    p.wait(v, "pe")
                    p.selfw(v, "dve")
                    p.inc(v.tensor_copy(mb_all[:, e, :], PB[6 + e % 2][:]), "dve")
                    mbdve.append(p.c["dve"])
                b.vector(dv_mb)
            pe_moe0 = p.c["pe"]
            gp_moe0 = p.c["gp"]
            dve_moe0 = p.c["dve"]
            def gp_w(g, tn=tn):
                for j in range(NEXP * NHC // 2):
                    e, hc = divmod(2 * j, NHC)
                    if j >= 2:
                        g.wait_ge(p.s["pe"], pe_moe0 + 2 * j - 2)
                    elif pe_moe0 > 0:
                        g.wait_ge(p.s["pe"], pe_moe0)
                    dst = w_moe[:] if j % 2 == 0 else w_moe2[:]
                    p.inc(g.dma_start(
                        dst, ewT[e, hc * 128:(hc + 2) * 128, :].rearrange(
                            "(s p) h -> p s h", p=128)), "gp", 16)
            b.gpsimd(gp_w)
            def dv_mm(v, tn=tn):
                for i in range(NEXP * NHC):
                    e, hc = divmod(i, NHC)
                    if i >= 1:
                        v.wait_ge(p.s["pe"], pe_moe0 + i)
                    p.selfw(v, "dve")
                    p.inc(v.tensor_tensor(mtile[:], x_fmb[:, hc, tn * 512:(tn + 1) * 512],
                                          mb_all[:, e, :], ALU.mult), "dve")
            b.vector(dv_mm)
            def pe_moe(t, tn=tn):
                for i in range(NEXP * NHC):
                    e, hc = divmod(i, NHC)
                    t.wait_ge(p.s["gp"], gp_moe0 + (i // 2 + 1) * 16)
                    t.wait_ge(p.s["dve"], dve_moe0 + i + 1)
                    wsl = (w_moe if (i // 2) % 2 == 0 else w_moe2)[:, i % 2, :]
                    for fc in range(8):
                        mm = t.matmul(PB[fc][:], wsl[:, fc * 128:(fc + 1) * 128],
                                      mtile[:], start=(i == 0),
                                      stop=(i == NEXP * NHC - 1))
                    p.inc(mm, "pe")
            b.tensor(pe_moe)
            slot = tn % 2
            def dv_ev(v, tn=tn, slot=slot):
                p.wait(v, "pe")
                p.wait(v, "sy")
                p.selfw(v, "dve")
                for fc in range(8):
                    p.inc(v.tensor_copy(moe_fm2[:, slot, fc, :], PB[fc][:]), "dve")
            b.vector(dv_ev)
            for tj in range(4):
                tc = tn * 4 + tj
                def sy_tr(s, tn=tn, slot=slot, tj=tj, tc=tc):
                    p.wait(s, "dve")
                    for hc in range(NHC):
                        p.selfw(s, "sy")
                        p.inc(s.dma_start(
                            moe_tm2[:, tc % 2, hc * 128:(hc + 1) * 128],
                            moe_fm2[:, slot, hc, tj * 128:(tj + 1) * 128],
                            transpose=True), "sy", 16)
                b.sync(sy_tr)
                def dv_res(v, tc=tc):
                    p.wait(v, "sy")
                    p.selfw(v, "dve")
                    p.inc(v.tensor_scalar(x_tm[:, tc, :], x_tm[:, tc, :], 2.0, 0.0,
                                          ALU.mult, ALU.add), "dve")
                    p.selfw(v, "dve")
                    p.inc(v.tensor_add(x_tm[:, tc, :], x_tm[:, tc, :],
                                       moe_tm2[:, tc % 2, :]), "dve")
                b.vector(dv_res)


        _sc_c.__exit__(None, None, None)
        _chk(upto, 10)
        _sc_d = nc.named_scope("A_ln2_out"); _sc_d.__enter__()
        ln_stats_batch(p, b, x_tm, st, NTC)
        ln_apply_and_fm(p, b, x_tm, x_tmb, x_fmb, st, NTC, PBb, eyeb)

        def sy_x3(s):
            p.wait(s, "dve")
            p.inc(s.dma_start(x3_o[:].rearrange("(c p) h -> p c h", p=128), x_tm[:]),
                  "sy", 16)
        b.sync(sy_x3)

        _chk(upto, 11)
        def gp_w2(g):
            p.wait(g, "pe")
            p.inc(g.dma_start(wq_b, wqT[:].rearrange("(c p) e -> p c e", p=128)), "gp", 16)
            p.inc(g.dma_start(wk_b, wkT[:].rearrange("(c p) e -> p c e", p=128)), "gp", 16)
            p.inc(g.dma_start(wv_b, wvT[:].rearrange("(c p) e -> p c e", p=128)), "gp", 16)
        b.gpsimd(gp_w2)
        gp_ld2 = p.c["gp"]
        act2c = p.c["act"]
        prpe = []
        def pe_pr(t):
            p.wait(t, "sy")
            p.wait(t, "dve")
            p.wait(t, "gp", gp_ld2)
            for j, wt in enumerate((wq_b, wk_b, wv_b)):
                for tn in range(4):
                    g = j * 4 + tn
                    if g >= 2:
                        t.wait_ge(p.s["act"], act2c + (g - 2) + 1)
                    bank = PB[g % 2]
                    for hc in range(NHC):
                        mm = t.matmul(bank[0:HD, :], wt[:, hc, :],
                                      x_fmb[:, hc, tn * 512:(tn + 1) * 512],
                                      start=(hc == 0), stop=(hc == NHC - 1))
                    prpe.append(p.inc(mm, "pe"))
        b.tensor(pe_pr)
        def act_pr(a):
            outs = (qf, kf, vf)
            biases = (bq_t, bk_t, bv_t)
            for j in range(3):
                for tn in range(4):
                    g = j * 4 + tn
                    p.wait(a, "pe", prpe[g])
                    p.selfw(a, "act")
                    p.inc(a.activation(outs[j][:, tn * 512:(tn + 1) * 512],
                                       PB[g % 2][0:HD, :], AF.Identity,
                                       bias=biases[j][:], scale=1.0), "act")
        b.scalar(act_pr)

        _chk(upto, 12)
        dve6 = p.c["dve"]
        vtr = []
        def pe_vt(t):
            p.wait(t, "act")
            for tc in range(NTC):
                if tc >= 2:
                    t.wait_ge(p.s["dve"], dve6 + (tc - 2) + 1)
                tr = t.transpose(PBb[2 + tc % 2][0:128, 0:HD],
                                 vf[0:HD, tc * 128:(tc + 1) * 128], eyeb[0:HD, 0:HD])
                vtr.append(p.inc(tr, "pe"))
        b.tensor(pe_vt)
        def dv_vt(v):
            for tc in range(NTC):
                p.wait(v, "pe", vtr[tc])
                p.inc(v.tensor_copy(v_tmb[:, tc, :], PBb[2 + tc % 2][0:128, 0:HD]),
                      "dve")
        b.vector(dv_vt)

        _chk(upto, 13)
        dve7 = p.c["dve"]
        nrmpe = []
        def dv_n1(v):
            p.wait(v, "act")
            p.selfw(v, "dve")
            p.inc(v.tensor_mul(sqn[:], qf[:], qf[:]), "dve")
        b.vector(dv_n1)
        def pe_nrm(t):
            for j in range(2):
                for tn in range(4):
                    g = j * 4 + tn
                    if g == 0:
                        t.wait_ge(p.s["dve"], dve7 + 1)
                    elif j == 0:
                        t.wait_ge(p.s["dve"], dve7 + 1 + g)
                    elif tn == 0:
                        t.wait_ge(p.s["dve"], dve7 + 7)
                    else:
                        t.wait_ge(p.s["dve"], dve7 + 7 + tn)
                    mm = t.matmul(PB[4][0:1, :], ones_b[0:HD, 0:1],
                                  sqn[:, tn * 512:(tn + 1) * 512], start=True, stop=True)
                    nrmpe.append(p.inc(mm, "pe"))
        b.tensor(pe_nrm)
        def dv_n2(v):
            for tn in range(4):
                p.wait(v, "pe", nrmpe[tn])
                p.selfw(v, "dve")
                p.inc(v.tensor_copy(zrow[0:1, tn * 512:(tn + 1) * 512], PB[4][0:1, :]),
                      "dve")
            p.selfw(v, "dve")
            p.inc(v.tensor_reduce(qkm_t[0:1, 0:1], zrow[:], AX.X, ALU.max), "dve")
            p.selfw(v, "dve")
            p.inc(v.tensor_mul(sqn[:], kf[:], kf[:]), "dve")
            for tn in range(4):
                p.wait(v, "pe", nrmpe[4 + tn])
                p.selfw(v, "dve")
                p.inc(v.tensor_copy(zrow[0:1, tn * 512:(tn + 1) * 512], PB[4][0:1, :]),
                      "dve")
            p.selfw(v, "dve")
            p.inc(v.tensor_reduce(qkm_t[0:1, 1:2], zrow[:], AX.X, ALU.max), "dve")
        b.vector(dv_n2)

        _chk(upto, 14)
        def sy_out(s):
            p.wait(s, "dve")
            p.wait(s, "act")
            p.inc(s.dma_start(q_o[:], qf[:]), "sy", 16)
            p.inc(s.dma_start(k_o[:], kf[:]), "sy", 16)
            p.inc(s.dma_start(v_o[:].rearrange("(c p) d -> p c d", p=128), v_tmb[:]),
                  "sy", 16)
            p.inc(s.dma_start(qkm_o[:], qkm_t[:]), "sy", 16)
            p.wait(s, "sy")
        b.sync(sy_out)
        _sc_d.__exit__(None, None, None)
      except _Stop:
        def sy_tail(s):
            p.wait(s, "dve")
            p.wait(s, "act")
            p.wait(s, "pe")
            p.inc(s.dma_start(x3_o[:].rearrange("(c p) h -> p c h", p=128),
                              x_tm[:]), "sy", 16)
            p.inc(s.dma_start(q_o[:], qf[:]), "sy", 16)
            p.wait(s, "sy")
        b.sync(sy_tail)
    return nc



def build_B():
    nc = bass.Bass("TRN2", target_bir_lowering=False)
    x3 = nc.dram_tensor("x3", [T, D], F32, kind="ExternalInput")
    qfd = nc.dram_tensor("qf", [HD, T], BF16, kind="ExternalInput")
    kfull = nc.dram_tensor("kfull", [HD, NCORE * T], BF16, kind="ExternalInput")
    vfull = nc.dram_tensor("vfull", [NCORE * T, HD], BF16, kind="ExternalInput")
    qkmd = nc.dram_tensor("qkm", [1, 16], F32, kind="ExternalInput")
    w1T = nc.dram_tensor("w1T", [32, 128, NHC, 128], BF16, kind="ExternalInput")
    w2T = nc.dram_tensor("w2T", [32, 128, NHC, 128], BF16, kind="ExternalInput")
    h1T = nc.dram_tensor("h1T", [32, 128, 32, 128], BF16, kind="ExternalInput")
    h2T = nc.dram_tensor("h2T", [32, 128, 32, 128], BF16, kind="ExternalInput")
    w3T = nc.dram_tensor("w3T", [DH, D], BF16, kind="ExternalInput")
    s1d = nc.dram_tensor("s1d", [128, 32], F32, kind="ExternalInput")
    s2d = nc.dram_tensor("s2d", [128, 32], F32, kind="ExternalInput")
    h1sd = nc.dram_tensor("h1sd", [128, 32], F32, kind="ExternalInput")
    h2sd = nc.dram_tensor("h2sd", [128, 32], F32, kind="ExternalInput")
    s3d = nc.dram_tensor("s3d", [1, D], F32, kind="ExternalInput")
    eyef_d = nc.dram_tensor("eyef", [128, 128], F32, kind="ExternalInput")
    out_o = nc.dram_tensor("out_o", [T, D], F32, kind="ExternalOutput")
    x4_d = nc.dram_tensor("x4_d", [T, D], F32, kind="Internal")

    NK = (NCORE * T) // 128

    arena = nc.alloc_sbuf_tensor("arena", [128, 65536], BF16)
    cur = [0]
    def carve(n):
        cur[0] = (cur[0] + 15) // 16 * 16
        off = cur[0]
        cur[0] += n
        return off
    o_k = carve(16384)
    k_sb = arena[:, o_k:o_k + 16384][0:HD, :]
    o_v = carve(NK * 65 + 7)
    v65 = arena[:, o_v:o_v + NK * 65].rearrange("p (c d) -> p c d", d=65)
    o_pt = carve(2048)
    pt_a = arena[:, o_pt:o_pt + 2048].rearrange("p (s t) -> p s t", s=2)
    o_y = carve(2048)
    y_sb = arena[:, o_y:o_y + 2048].bitcast(F32)
    o_q = carve(2048)
    qf_b = arena[:, o_q:o_q + 2048][0:HD, :]
    o_x3 = carve(4096)
    x3t = arena[:, o_x3:o_x3 + 4096].bitcast(F32).rearrange(
        "p (s h) -> p s h", s=2)
    h_a = arena[:, 0:32768].rearrange("p (c t) -> p c t", c=32)
    h_b = arena[:, 32768:65536].rearrange("p (c t) -> p c t", c=32)

    x_fmb = nc.alloc_sbuf_tensor("x_fmb", [128, NHC, T], BF16)
    x4t = nc.alloc_sbuf_tensor("x4t", [128, 1, D], F32)
    xtmb = nc.alloc_sbuf_tensor("xtmb", [128, 1, D], BF16)
    yT_sb = nc.alloc_sbuf_tensor("yT_sb", [128, 65], F32)
    z1 = nc.alloc_sbuf_tensor("z1", [128, 1], F32)
    ybar = nc.alloc_sbuf_tensor("ybar", [128, HD], F32)
    st = st_alloc(nc)
    qkm_t = nc.alloc_sbuf_tensor("qkm_t", [1, 16], F32)
    csc = nc.alloc_sbuf_tensor("cs", [1, 4], F32)
    cb = nc.alloc_sbuf_tensor("cb", [128, 1], F32)
    ncb = nc.alloc_sbuf_tensor("ncb", [128, 1], F32)
    inv32 = nc.alloc_sbuf_tensor("inv32", [128, 1], F32)
    onesf = nc.alloc_sbuf_tensor("onesf", [1, 128], F32)
    eyef = nc.alloc_sbuf_tensor("eyef_t", [128, 128], F32)
    s1_t = nc.alloc_sbuf_tensor("s1_t", [128, 32], F32)
    s2_t = nc.alloc_sbuf_tensor("s2_t", [128, 32], F32)
    h1s_t = nc.alloc_sbuf_tensor("h1s_t", [128, 32], F32)
    h2s_t = nc.alloc_sbuf_tensor("h2s_t", [128, 32], F32)
    s3r = nc.alloc_sbuf_tensor("s3r", [1, D], F32)
    s3_bb = nc.alloc_sbuf_tensor("s3_bb", [128, D], F32)
    hw_b = nc.alloc_sbuf_tensor("hw_b", [128, 2, 32, 128], BF16)
    w1_b = hw_b[:, :, 0:NHC, :]
    w2_b = hw_b[:, :, NHC:2 * NHC, :]
    w3_b = hw_b[:].rearrange("p s c f -> p s (c f)")[:, :, 0:1024]
    silu_s = nc.alloc_sbuf_tensor("silu_s", [128, 1024], F32)
    t2_s = nc.alloc_sbuf_tensor("t2_s", [128, 1024], F32)
    o_sb = silu_s[:]

    psSA = nc.alloc_psum_tensor("psSA", [128, 1024], F32)
    psSB = nc.alloc_psum_tensor("psSB", [128, 1024], F32)
    psY = nc.alloc_psum_tensor("psY", [128, 1024], F32)
    psC = nc.alloc_psum_tensor("psC", [128, 512], F32)
    psT = nc.alloc_psum_tensor("psT", [128, 512], F32)

    p = Prog(nc)
    p.sems("sy", "gp", "pe", "act", "dve")

    with nc.Block() as b:
        boot_clear(b, nc, p)
        def sy0(s):
            p.inc(s.dma_start(k_sb[:, :], kfull[:]), "sy", 16)
            p.inc(s.dma_start(qf_b[:, :], qfd[:]), "sy", 16)
            p.inc(s.dma_start(qkm_t[:], qkmd[:]), "sy", 16)
            p.inc(s.dma_start(eyef[:], eyef_d[:]), "sy", 16)
            p.inc(s.dma_start(s1_t[:], s1d[:]), "sy", 16)
            p.inc(s.dma_start(s2_t[:], s2d[:]), "sy", 16)
            p.inc(s.dma_start(h1s_t[:], h1sd[:]), "sy", 16)
            p.inc(s.dma_start(h2s_t[:], h2sd[:]), "sy", 16)
            p.inc(s.dma_start(s3r[:], s3d[:]), "sy", 16)
        b.sync(sy0)
        def dv0(v):
            p.inc(v.memset(onesf[:], 1.0), "dve")
            p.inc(v.memset(v65[:, :, 64:65], 1.0), "dve")
            p.inc(v.memset(inv32[:], 1.0 / 32.0), "dve")
        b.vector(dv0)
        st_init(p, b, st)
        def sy1(s):
            p.wait(s, "dve")
            p.inc(s.dma_start(v65[:, :, 0:64],
                              vfull[:].rearrange("(c p) d -> p c d", p=128)), "sy", 16)
        b.sync(sy1)
        def dv_c(v):
            p.wait(v, "sy")
            p.inc(v.tensor_reduce(csc[0:1, 0:2],
                                  qkm_t[:].rearrange("o (a c) -> o a c", a=2),
                                  AX.X, ALU.max), "dve")
            p.selfw(v, "dve")
            p.inc(v.tensor_add(csc[0:1, 2:3], csc[0:1, 0:1], csc[0:1, 1:2]), "dve")
            p.selfw(v, "dve")
            p.inc(v.tensor_single_scalar(csc[0:1, 3:4], csc[0:1, 2:3], 1.0 / 64.0,
                                         ALU.mult), "dve")
        b.vector(dv_c)
        def pe_c(t):
            p.wait(t, "dve")
            p.inc(t.matmul(psC[:, 0:1], onesf[0:1, :], csc[0:1, 3:4],
                           start=True, stop=True), "pe")
        b.tensor(pe_c)
        def dv_c2(v):
            p.wait(v, "pe")
            p.inc(v.tensor_copy(cb[:], psC[:, 0:1]), "dve")
            p.selfw(v, "dve")
            p.inc(v.tensor_single_scalar(ncb[:], cb[:], -1.0, ALU.mult), "dve")
        b.vector(dv_c2)

        _sc_e = nc.named_scope("B_attn"); _sc_e.__enter__()
        for th in range(2):
            toff = th * 1024
            act_b = p.c["act"]
            spe, vpe = [], []
            def pe_at(t, toff=toff, act_b=act_b, spe=spe, vpe=vpe):
                p.wait(t, "sy")
                p.wait(t, "dve")
                for kc in range(NK):
                    bank = psSA if kc % 2 == 0 else psSB
                    if kc >= 2:
                        t.wait_ge(p.s["act"], act_b + (kc - 2) + 1)
                    for nn in range(2):
                        mm = t.matmul(bank[:, nn * 512:(nn + 1) * 512],
                                      k_sb[:, kc * 128:(kc + 1) * 128],
                                      qf_b[:, toff + nn * 512:toff + (nn + 1) * 512],
                                      start=True, stop=True)
                    spe.append(p.inc(mm, "pe"))
                    if kc >= 1:
                        pk = kc - 1
                        t.wait_ge(p.s["act"], act_b + pk + 1)
                        for nn in range(2):
                            mm = t.matmul(psY[0:65, nn * 512:(nn + 1) * 512],
                                          v65[:, pk, :],
                                          pt_a[:, pk % 2, nn * 512:(nn + 1) * 512],
                                          start=(pk == 0), stop=False)
                        vpe.append(p.inc(mm, "pe"))
                t.wait_ge(p.s["act"], act_b + NK)
                for nn in range(2):
                    mm = t.matmul(psY[0:65, nn * 512:(nn + 1) * 512], v65[:, NK - 1, :],
                                  pt_a[:, (NK - 1) % 2, nn * 512:(nn + 1) * 512],
                                  start=False, stop=True)
                vpe.append(p.inc(mm, "pe"))
            b.tensor(pe_at)
            def act_at(a, spe=spe, vpe=vpe):
                for kc in range(NK):
                    p.wait(a, "pe", spe[kc])
                    if kc >= 2:
                        p.wait(a, "pe", vpe[kc - 2])
                    bank = psSA if kc % 2 == 0 else psSB
                    p.selfw(a, "act")
                    p.inc(a.activation(pt_a[:, kc % 2, :], bank[:], AF.Exp,
                                       bias=ncb[:], scale=inv32[:]), "act")
            b.scalar(act_at)
            def dv_y(v):
                p.wait(v, "pe")
                p.inc(v.tensor_copy(y_sb[0:65, :], psY[0:65, :]), "dve")
            b.vector(dv_y)

            for tj in range(8):
                tc = th * 8 + tj
                sl = 0
                def sy_ld(s, tc=tc, sl=sl):
                    p.wait(s, "dve")
                    p.inc(s.dma_start(x3t[:, sl, :], x3[tc * 128:(tc + 1) * 128, :]),
                          "sy", 16)
                b.sync(sy_ld)
                def pe_t(t, tj=tj):
                    p.wait(t, "dve")
                    p.inc(t.transpose(psT[:, 0:65],
                                      y_sb[0:65, tj * 128:(tj + 1) * 128],
                                      eyef[0:65, 0:65]), "pe")
                b.tensor(pe_t)
                def dv_t(v, tc=tc, sl=sl):
                    p.wait(v, "pe")
                    p.wait(v, "sy")
                    p.selfw(v, "dve")
                    p.inc(v.tensor_copy(yT_sb[:], psT[:, 0:65]), "dve")
                    p.selfw(v, "dve")
                    p.inc(v.reciprocal(z1[:], yT_sb[:, 64:65]), "dve")
                    p.selfw(v, "dve")
                    p.inc(v.tensor_scalar(ybar[:], yT_sb[:, 0:64], z1[:], 0.0,
                                          ALU.mult, ALU.add), "dve")
                    p.selfw(v, "dve")
                    p.inc(v.tensor_tensor(
                        x4t[:, sl, :].rearrange("p (r d) -> p r d", r=16),
                        x3t[:, sl, :].rearrange("p (r d) -> p r d", r=16),
                        ybar[:].unsqueeze(1).broadcast_to([128, 16, HD]), ALU.add),
                        "dve")
                b.vector(dv_t)
                ln_tile(p, b, x4t[:, sl, :], st, tc % NTC)
                def dv_cp(v, sl=sl):
                    p.wait(v, "sy")
                    p.selfw(v, "dve")
                    p.inc(v.tensor_copy(xtmb[:, sl, :], x4t[:, sl, :]), "dve")
                b.vector(dv_cp)
                def sy_st(s, tc=tc, sl=sl):
                    p.wait(s, "dve")
                    p.inc(s.dma_start(x4_d[tc * 128:(tc + 1) * 128, :], x4t[:, sl, :]),
                          "sy", 16)
                    for hc in range(NHC):
                        p.selfw(s, "sy")
                        p.inc(s.dma_start(x_fmb[:, hc, tc * 128:(tc + 1) * 128],
                                          xtmb[:, sl, hc * 128:(hc + 1) * 128],
                                          transpose=True), "sy", 16)
                b.sync(sy_st)

        _sc_e.__exit__(None, None, None)
        _sc_f = nc.named_scope("B_ffn"); _sc_f.__enter__()
        def pe_s3(t):
            p.wait(t, "dve")
            p.inc(t.matmul(psC[:, :], onesf[0:1, :], s3r[0:1, 0:512],
                           start=True, stop=True), "pe")
            p.inc(t.matmul(psT[:, :], onesf[0:1, :], s3r[0:1, 512:1024],
                           start=True, stop=True), "pe")
        b.tensor(pe_s3)
        def dv_s3(v):
            p.wait(v, "pe")
            p.selfw(v, "dve")
            p.inc(v.tensor_copy(s3_bb[:, 0:512], psC[:]), "dve")
            p.inc(v.tensor_copy(s3_bb[:, 512:1024], psT[:]), "dve")
        b.vector(dv_s3)

        for th in range(2):
            toff = th * 1024

            pe_b = p.c["pe"]
            act_b = p.c["act"]
            dve_b = p.c["dve"]
            gp_b = p.c["gp"]
            def gp_w1(g, pe_b=pe_b):
                for fc in range(32):
                    if fc < 2:
                        if pe_b > 0:
                            g.wait_ge(p.s["pe"], pe_b)
                    else:
                        g.wait_ge(p.s["pe"], pe_b + (fc - 2) + 1)
                    p.inc(g.dma_start(w1_b[:, fc % 2, :, :], w1T[fc]), "gp", 16)
                    p.inc(g.dma_start(w2_b[:, fc % 2, :, :], w2T[fc]), "gp", 16)
            b.gpsimd(gp_w1)
            def pe_s1(t, toff=toff, gp_b=gp_b, act_b=act_b):
                p.wait(t, "sy")
                p.wait(t, "dve")
                for fc in range(32):
                    t.wait_ge(p.s["gp"], gp_b + (fc + 1) * 32)
                    if fc >= 1:
                        t.wait_ge(p.s["act"], act_b + 2 * (fc - 1) + 2)
                    for hc in range(NHC):
                        for nn in range(2):
                            mm = t.matmul(psSA[:, nn * 512:(nn + 1) * 512],
                                          w1_b[:, fc % 2, hc, :],
                                          x_fmb[:, hc, toff + nn * 512:toff + (nn + 1) * 512],
                                          start=(hc == 0), stop=(hc == NHC - 1))
                    for hc in range(NHC):
                        for nn in range(2):
                            mm = t.matmul(psSB[:, nn * 512:(nn + 1) * 512],
                                          w2_b[:, fc % 2, hc, :],
                                          x_fmb[:, hc, toff + nn * 512:toff + (nn + 1) * 512],
                                          start=(hc == 0), stop=(hc == NHC - 1))
                    p.inc(mm, "pe")
            b.tensor(pe_s1)
            s1pe_list = list(range(pe_b + 1, pe_b + 33))
            def act_s1(a, dve_b=dve_b, s1pe_list=s1pe_list):
                for fc in range(32):
                    p.wait(a, "pe", s1pe_list[fc])
                    if fc >= 1:
                        a.wait_ge(p.s["dve"], dve_b + fc)
                    p.selfw(a, "act")
                    p.inc(a.activation(silu_s[:], psSA[:], AF.Silu,
                                       bias=0.0, scale=s1_t[:, fc:fc + 1]), "act")
                    p.inc(a.activation(t2_s[:], psSB[:], AF.Copy,
                                       bias=0.0, scale=s2_t[:, fc:fc + 1]), "act")
            b.scalar(act_s1)
            def dv_s1(v, act_b=act_b):
                for fc in range(32):
                    v.wait_ge(p.s["act"], act_b + 2 * fc + 2)
                    p.selfw(v, "dve")
                    p.inc(v.tensor_mul(h_a[:, fc, :], silu_s[:], t2_s[:]), "dve")
            b.vector(dv_s1)

            for wT, ssc, src, dst in ((h1T, h1s_t, h_a, h_b), (h2T, h2s_t, h_b, h_a)):
                pe_b = p.c["pe"]
                act_b = p.c["act"]
                gp_b = p.c["gp"]
                def gp_wh(g, wT=wT, pe_b=pe_b):
                    for fc in range(32):
                        if fc < 2:
                            g.wait_ge(p.s["pe"], pe_b)
                        else:
                            g.wait_ge(p.s["pe"], pe_b + (fc - 2) + 1)
                        p.inc(g.dma_start(hw_b[:, fc % 2, :, :], wT[fc]), "gp", 16)
                b.gpsimd(gp_wh)
                def pe_h(t, src=src, gp_b=gp_b, act_b=act_b):
                    p.wait(t, "dve")
                    p.wait(t, "act")
                    for fc in range(32):
                        ps = psSA if fc % 2 == 0 else psSB
                        t.wait_ge(p.s["gp"], gp_b + (fc + 1) * 16)
                        if fc >= 2:
                            t.wait_ge(p.s["act"], act_b + fc - 1)
                        for kc in range(32):
                            for nn in range(2):
                                mm = t.matmul(ps[:, nn * 512:(nn + 1) * 512],
                                              hw_b[:, fc % 2, kc, :],
                                              src[:, kc, nn * 512:(nn + 1) * 512],
                                              start=(kc == 0), stop=(kc == 31))
                        p.inc(mm, "pe")
                b.tensor(pe_h)
                hpe_list = list(range(pe_b + 1, pe_b + 33))
                def act_h(a, ssc=ssc, dst=dst, hpe_list=hpe_list):
                    for fc in range(32):
                        ps = psSA if fc % 2 == 0 else psSB
                        p.wait(a, "pe", hpe_list[fc])
                        p.selfw(a, "act")
                        p.inc(a.activation(dst[:, fc, :], ps[:], AF.Gelu,
                                           bias=0.0, scale=ssc[:, fc:fc + 1]), "act")
                b.scalar(act_h)

            for tg in range(2):
                banks = [(psSA[:, 0:512], psSA[:, 512:1024]),
                         (psSB[:, 0:512], psSB[:, 512:1024]),
                         (psY[:, 0:512], psY[:, 512:1024]),
                         (psC[:, :], psT[:, :])]
                pe_b = p.c["pe"]
                gp_b = p.c["gp"]
                def gp_w3(g, tg=tg, pe_b=pe_b):
                    for kc in range(32):
                        if kc < 2:
                            g.wait_ge(p.s["pe"], pe_b)
                        else:
                            g.wait_ge(p.s["pe"], pe_b + (kc - 2) + 1)
                        p.inc(g.dma_start(w3_b[:, kc % 2, :],
                                          w3T[kc * 128:(kc + 1) * 128, :]), "gp", 16)
                b.gpsimd(gp_w3)
                def pe_w3(t, tg=tg, gp_b=gp_b, banks=banks):
                    p.wait(t, "act")
                    p.wait(t, "dve")
                    for kc in range(32):
                        t.wait_ge(p.s["gp"], gp_b + (kc + 1) * 16)
                        for tcc in range(4):
                            lh = h_a[:, kc,
                                     tg * 512 + tcc * 128:tg * 512 + (tcc + 1) * 128]
                            for nn in range(2):
                                mm = t.matmul(banks[tcc][nn], lh,
                                              w3_b[:, kc % 2, nn * 512:(nn + 1) * 512],
                                              start=(kc == 0), stop=(kc == 31))
                        p.inc(mm, "pe")
                b.tensor(pe_w3)
                for tcc in range(4):
                    tc = th * 8 + tg * 4 + tcc
                    sl = 0
                    def sy_l4(s, tc=tc, sl=sl):
                        p.wait(s, "dve")
                        p.inc(s.dma_start(x4t[:, sl, :],
                                          x4_d[tc * 128:(tc + 1) * 128, :]), "sy", 16)
                    b.sync(sy_l4)
                    def dv_o(v, tcc=tcc, sl=sl, banks=banks):
                        p.wait(v, "pe")
                        p.wait(v, "sy")
                        p.selfw(v, "dve")
                        p.inc(v.tensor_copy(o_sb[:, 0:512], banks[tcc][0]), "dve")
                        p.inc(v.tensor_copy(o_sb[:, 512:1024], banks[tcc][1]), "dve")
                        p.selfw(v, "dve")
                        p.inc(v.tensor_mul(o_sb[:], o_sb[:], s3_bb[:]), "dve")
                        p.selfw(v, "dve")
                        p.inc(v.tensor_add(x4t[:, sl, :], x4t[:, sl, :], o_sb[:]), "dve")
                    b.vector(dv_o)
                    ln_tile(p, b, x4t[:, sl, :], st, tc % NTC)
                    def sy_o(s, tc=tc, sl=sl):
                        p.wait(s, "dve")
                        p.inc(s.dma_start(out_o[tc * 128:(tc + 1) * 128, :],
                                          x4t[:, sl, :]), "sy", 16)
                    b.sync(sy_o)

        _sc_f.__exit__(None, None, None)
        def sy_fin(s):
            p.wait(s, "sy")
        b.sync(sy_fin)
    return nc



_bf = ml_dtypes.bfloat16


def _tile_w(w, nkc):
    a = np.asarray(w, np.float32).reshape(32, 128, nkc, 128)
    return np.ascontiguousarray(a.transpose(0, 3, 2, 1)).astype(_bf)


def sel8_host():
    s = np.zeros((NEXP, NEXP * 128), dtype=_bf)
    for e in range(NEXP):
        s[e, e * 128:(e + 1) * 128] = 1.0
    return s


def kernel(L0_attn_x, L0_attn_q_proj, L0_attn_k_proj, L0_ffn_gate_w, L0_ffn_expert_w,
           L1_attn_Wq, L1_attn_bq, L1_attn_Wk, L1_attn_bk, L1_attn_Wv, L1_attn_bv,
           L1_ffn_w1_w, L1_ffn_w1_s, L1_ffn_w2_w, L1_ffn_w2_s,
           L1_ffn_h1_w, L1_ffn_h1_s, L1_ffn_h2_w, L1_ffn_h2_s,
           L1_ffn_w3_w, L1_ffn_w3_s):
    f32 = np.float32
    ct = np.ascontiguousarray
    x = np.asarray(L0_attn_x, f32)
    eyeb = np.eye(128, dtype=_bf)
    eyef = np.eye(128, dtype=f32)

    common_A = {
        "qp": ct(np.asarray(L0_attn_q_proj, f32)),
        "kp": ct(np.asarray(L0_attn_k_proj, f32)),
        "gT": ct(np.asarray(L0_ffn_gate_w, f32).T),
        "ewT": ct(np.asarray(L0_ffn_expert_w, f32).transpose(0, 2, 1)),
        "wqT": ct(np.asarray(L1_attn_Wq, f32).T),
        "wkT": ct(np.asarray(L1_attn_Wk, f32).T),
        "wvT": ct(np.asarray(L1_attn_Wv, f32).T),
        "bq": ct(np.asarray(L1_attn_bq, f32).reshape(HD, 1)),
        "bk": ct(np.asarray(L1_attn_bk, f32).reshape(HD, 1)),
        "bv": ct(np.asarray(L1_attn_bv, f32).reshape(HD, 1)),
        "eyeb": eyeb, "eyef": eyef, "sel8": sel8_host(),
    }
    in_maps_A = []
    for c in range(NCORE):
        m = dict(common_A)
        m["xa"] = ct(x[c])
        m["xaT"] = ct(x[c].T)
        in_maps_A.append(m)

    ncA = build_A()
    resA = run_bass_kernel_spmd(ncA, in_maps_A, core_ids=list(range(NCORE)))
    ra = resA.results

    kfull = ct(np.concatenate([np.asarray(ra[c]["k_o"]) for c in range(NCORE)], axis=1))
    vfull = ct(np.concatenate([np.asarray(ra[c]["v_o"]) for c in range(NCORE)], axis=0))
    qkm = ct(np.concatenate([np.asarray(ra[c]["qkm_o"], f32) for c in range(NCORE)],
                            axis=0).T.reshape(1, 16))

    sc = lambda s: ct(np.asarray(s, f32).reshape(-1, 128).T)
    common_B = {
        "kfull": kfull, "vfull": vfull, "qkm": qkm,
        "w1T": _tile_w(L1_ffn_w1_w, NHC),
        "w2T": _tile_w(L1_ffn_w2_w, NHC),
        "h1T": _tile_w(L1_ffn_h1_w, 32),
        "h2T": _tile_w(L1_ffn_h2_w, 32),
        "w3T": ct(np.asarray(L1_ffn_w3_w, f32).T.astype(_bf)),
        "s1d": sc(L1_ffn_w1_s), "s2d": sc(L1_ffn_w2_s),
        "h1sd": sc(L1_ffn_h1_s), "h2sd": sc(L1_ffn_h2_s),
        "s3d": ct(np.asarray(L1_ffn_w3_s, f32).reshape(1, D)),
        "eyef": eyef,
    }
    in_maps_B = []
    for c in range(NCORE):
        m = dict(common_B)
        m["x3"] = np.asarray(ra[c]["x3_o"], f32)
        m["qf"] = np.asarray(ra[c]["q_o"])
        in_maps_B.append(m)

    ncB = build_B()
    resB = run_bass_kernel_spmd(ncB, in_maps_B, core_ids=list(range(NCORE)))
    out = np.stack([np.asarray(resB.results[c]["out_o"], f32) for c in range(NCORE)],
                   axis=0)
    return out


# version 13
# speedup vs baseline: 1.4291x; 1.0054x over previous
import numpy as np
import ml_dtypes
import concourse.bass as bass
import concourse.mybir as mybir
from concourse.bass_utils import run_bass_kernel_spmd

F32 = mybir.dt.float32
BF16 = mybir.dt.bfloat16
AF = mybir.ActivationFunctionType
ALU = mybir.AluOpType
AX = mybir.AxisListType

NCORE = 8
T = 2048
D = 1024
E = 256
NEXP = 8
HD = 64
DH = 4096
NTC = T // 128
NHC = D // 128
LN_EPS = 1e-5


class Prog:

    def __init__(self, nc):
        self.nc = nc
        self.s = {}
        self.c = {}

    def sems(self, *names):
        for n in names:
            self.s[n] = self.nc.alloc_semaphore(n)
            self.c[n] = 0

    def inc(self, instr, name, k=1):
        instr.then_inc(self.s[name], k)
        self.c[name] += k
        return self.c[name]

    def wait(self, eng, name, val=None):
        v = self.c[name] if val is None else val
        if v > 0:
            eng.wait_ge(self.s[name], v)

    def selfw(self, eng, name):
        if self.c[name] > 0:
            eng.wait_ge(self.s[name], self.c[name])


import os
_LN_MODE = os.environ.get("LN_MODE", "ts2")
_LN_DELAY = bool(int(os.environ.get("LN_DELAY", "0")))


def ln_stats_batch(p, b, x_tm, st, ntc):
    def act(a):
        p.wait(a, "dve")
        for tc in range(ntc):
            p.selfw(a, "act")
            p.inc(a.activation(st["dump"][:], x_tm[:, tc, :], AF.Square,
                               bias=0.0, scale=1.0,
                               accum_out=st["sq"][:, tc:tc + 1]), "act")
    b.scalar(act)
    def dve(v):
        p.selfw(v, "dve")
        for tc in range(ntc):
            p.inc(v.reduce_sum(st["sum"][:, tc:tc + 1], x_tm[:, tc, :], AX.X), "dve")
        p.wait(v, "act")
        s = slice(0, ntc)
        p.selfw(v, "dve")
        p.inc(v.tensor_scalar(st["m"][:, s], st["sum"][:, s], 1.0 / D, 0.0,
                              ALU.mult, ALU.add), "dve")
        p.inc(v.tensor_scalar(st["ex2"][:, s], st["sq"][:, s], 1.0 / D, 0.0,
                              ALU.mult, ALU.add), "dve")
        p.selfw(v, "dve")
        p.inc(v.tensor_mul(st["msq"][:, s], st["m"][:, s], st["m"][:, s]), "dve")
        p.selfw(v, "dve")
        p.inc(v.tensor_sub(st["var"][:, s], st["ex2"][:, s], st["msq"][:, s]), "dve")
    b.vector(dve)
    def act2(a):
        p.wait(a, "dve")
        s = slice(0, ntc)
        p.inc(a.activation(st["lnv"][:, s], st["var"][:, s], AF.Ln,
                           bias=st["eps"][:], scale=1.0), "act")
        p.selfw(a, "act")
        p.inc(a.activation(st["rstd"][:, s], st["lnv"][:, s], AF.Exp,
                           bias=0.0, scale=st["mhalf"][:]), "act")
    b.scalar(act2)
    def dve2(v):
        p.wait(v, "act")
        s = slice(0, ntc)
        p.inc(v.tensor_mul(st["mh"][:, s], st["m"][:, s], st["rstd"][:, s]), "dve")
        p.selfw(v, "dve")
        p.inc(v.tensor_scalar(st["nmh"][:, s], st["mh"][:, s], -1.0, 0.0,
                              ALU.mult, ALU.add), "dve")
    b.vector(dve2)


def ln_apply_and_fm(p, b, x_tm, x_tmb, x_fmb, st, ntc, PBb, eyeb):
    dve0 = p.c["dve"]
    def dve(v):
        for tc in range(ntc):
            p.selfw(v, "dve")
            p.inc(v.tensor_scalar(x_tm[:, tc, :], x_tm[:, tc, :],
                                  st["rstd"][:, tc:tc + 1], st["nmh"][:, tc:tc + 1],
                                  ALU.mult, ALU.add), "dve")
            p.selfw(v, "dve")
            p.inc(v.tensor_copy(x_tmb[:, tc, :], x_tm[:, tc, :]), "dve")
    b.vector(dve)
    dvc0 = dve0 + 2 * ntc
    trs = []
    def pe_tr(t):
        p.wait(t, "act")
        for g in range(ntc * NHC):
            tc, hc = divmod(g, NHC)
            t.wait_ge(p.s["dve"], dve0 + 2 * (tc + 1))
            if g >= 4:
                t.wait_ge(p.s["dve"], dvc0 + (g - 4) + 1)
            tr = t.transpose(PBb[g % 4][0:128, 0:128],
                             x_tmb[:, tc, hc * 128:(hc + 1) * 128], eyeb[:, :])
            trs.append(p.inc(tr, "pe"))
    b.tensor(pe_tr)
    def dv_tr(v):
        for g in range(ntc * NHC):
            tc, hc = divmod(g, NHC)
            p.wait(v, "pe", trs[g])
            p.inc(v.tensor_copy(x_fmb[:, hc, tc * 128:(tc + 1) * 128],
                                PBb[g % 4][0:128, 0:128]), "dve")
    b.vector(dv_tr)


def ln_tile(p, b, x_ap, st, slot):
    sl = slice(slot, slot + 1)
    def act(a):
        p.wait(a, "dve")
        p.inc(a.activation(st["dump"][:], x_ap, AF.Square, bias=0.0, scale=1.0,
                           accum_out=st["sq"][:, sl]), "act")
    b.scalar(act)
    def dve(v):
        p.selfw(v, "dve")
        p.inc(v.reduce_sum(st["sum"][:, sl], x_ap, AX.X), "dve")
        p.wait(v, "act")
        p.selfw(v, "dve")
        p.inc(v.tensor_scalar(st["m"][:, sl], st["sum"][:, sl], 1.0 / D, 0.0,
                              ALU.mult, ALU.add), "dve")
        p.inc(v.tensor_scalar(st["ex2"][:, sl], st["sq"][:, sl], 1.0 / D, 0.0,
                              ALU.mult, ALU.add), "dve")
        p.selfw(v, "dve")
        p.inc(v.tensor_mul(st["msq"][:, sl], st["m"][:, sl], st["m"][:, sl]), "dve")
        p.selfw(v, "dve")
        p.inc(v.tensor_sub(st["var"][:, sl], st["ex2"][:, sl], st["msq"][:, sl]), "dve")
    b.vector(dve)
    def act2(a):
        p.wait(a, "dve")
        p.inc(a.activation(st["lnv"][:, sl], st["var"][:, sl], AF.Ln,
                           bias=st["eps"][:], scale=1.0), "act")
        p.selfw(a, "act")
        p.inc(a.activation(st["rstd"][:, sl], st["lnv"][:, sl], AF.Exp,
                           bias=0.0, scale=st["mhalf"][:]), "act")
    b.scalar(act2)
    def dve2(v):
        p.wait(v, "act")
        p.inc(v.tensor_mul(st["mh"][:, sl], st["m"][:, sl], st["rstd"][:, sl]), "dve")
        p.selfw(v, "dve")
        p.inc(v.tensor_scalar(st["nmh"][:, sl], st["mh"][:, sl], -1.0, 0.0,
                              ALU.mult, ALU.add), "dve")
        p.selfw(v, "dve")
        p.inc(v.tensor_scalar(x_ap, x_ap, st["rstd"][:, sl], st["nmh"][:, sl],
                              ALU.mult, ALU.add), "dve")
    b.vector(dve2)


_BOOT = bool(int(os.environ.get("KBOOT", "1")))


def boot_clear(b, nc, p):
    if not _BOOT:
        return
    """Clear OUR kernel semaphores at program start (crashed programs leave
    stale values; a stale sem makes every wait_ge pass early). gpsimd clears;
    the other engines sit in a timed nop longer than the clear, then check a
    boot sem that the clear zeroed before it was incremented. Bass's own
    barrier sems are left alone (their protocol is self-normalizing)."""
    bs = nc.alloc_semaphore("bootsem")
    lo = min(h.num for h in list(p.s.values()) + [bs])
    rng_ = range(lo, nc._kernel_sem_range.stop)
    def gp(g):
        g.sem_clear(rng_)
        g.sem_inc(bs, 1)
    b.gpsimd(gp)
    for name in ("tensor", "vector", "scalar", "sync"):
        def w(e):
            e.nop(cycle_cnt=20000)
            e.wait_ge(bs, 1)
        getattr(b, name)(w)


def st_alloc(nc):
    st = {n: nc.alloc_sbuf_tensor("st_" + n, [128, NTC], F32)
          for n in ("sum", "sq", "m", "ex2", "msq", "var", "lnv", "rstd", "mh", "nmh")}
    st["dump"] = nc.alloc_sbuf_tensor("st_dump", [128, D], BF16)
    st["eps"] = nc.alloc_sbuf_tensor("st_eps", [128, 1], F32)
    st["mhalf"] = nc.alloc_sbuf_tensor("st_mhalf", [128, 1], F32)
    return st


def st_init(p, b, st):
    def dv(v):
        p.inc(v.memset(st["eps"][:], LN_EPS), "dve")
        p.inc(v.memset(st["mhalf"][:], -0.5), "dve")
    b.vector(dv)



class _Stop(Exception):
    pass


def _chk(upto, n):
    if upto < n:
        raise _Stop


def build_A(upto=99):
    nc = bass.Bass("TRN2", target_bir_lowering=False)
    xa = nc.dram_tensor("xa", [T, D], F32, kind="ExternalInput")
    xaT = nc.dram_tensor("xaT", [D, T], F32, kind="ExternalInput")
    qp = nc.dram_tensor("qp", [D, E], F32, kind="ExternalInput")
    kp = nc.dram_tensor("kp", [D, E], F32, kind="ExternalInput")
    gTd = nc.dram_tensor("gT", [D, NEXP], F32, kind="ExternalInput")
    ewT = nc.dram_tensor("ewT", [NEXP, D, D], F32, kind="ExternalInput")
    wqT = nc.dram_tensor("wqT", [D, HD], F32, kind="ExternalInput")
    wkT = nc.dram_tensor("wkT", [D, HD], F32, kind="ExternalInput")
    wvT = nc.dram_tensor("wvT", [D, HD], F32, kind="ExternalInput")
    bqd = nc.dram_tensor("bq", [HD, 1], F32, kind="ExternalInput")
    bkd = nc.dram_tensor("bk", [HD, 1], F32, kind="ExternalInput")
    bvd = nc.dram_tensor("bv", [HD, 1], F32, kind="ExternalInput")
    eyeb_d = nc.dram_tensor("eyeb", [128, 128], BF16, kind="ExternalInput")
    eyef_d = nc.dram_tensor("eyef", [128, 128], F32, kind="ExternalInput")
    sel8_d = nc.dram_tensor("sel8", [NEXP, NEXP * 128], BF16, kind="ExternalInput")

    x3_o = nc.dram_tensor("x3_o", [T, D], F32, kind="ExternalOutput")
    q_o = nc.dram_tensor("q_o", [HD, T], BF16, kind="ExternalOutput")
    k_o = nc.dram_tensor("k_o", [HD, T], BF16, kind="ExternalOutput")
    v_o = nc.dram_tensor("v_o", [T, HD], BF16, kind="ExternalOutput")
    qkm_o = nc.dram_tensor("qkm_o", [1, 2], F32, kind="ExternalOutput")

    x_tm = nc.alloc_sbuf_tensor("x_tm", [128, NTC, D], F32)
    x_tmb = nc.alloc_sbuf_tensor("x_tmb", [128, NTC, D], BF16)
    x_fmb = nc.alloc_sbuf_tensor("x_fmb", [128, NHC, T], BF16)
    qpb = nc.alloc_sbuf_tensor("qpb", [128, NHC, E], BF16)
    kpb = nc.alloc_sbuf_tensor("kpb", [128, NHC, E], BF16)
    wq_b = qpb[:, :, 0:HD]
    wk_b = qpb[:, :, HD:2 * HD]
    wv_b = qpb[:, :, 2 * HD:3 * HD]
    pool_a = nc.alloc_sbuf_tensor("pool_a", [128, 4096], BF16)
    qe = pool_a[:].rearrange("p (a t) -> p a t", a=2)
    mb_all = pool_a[:].rearrange("p (e t) -> p e t", e=NEXP)
    pool_b = nc.alloc_sbuf_tensor("pool_b", [128, 8192], BF16)
    ke = pool_b[:, 0:4096].rearrange("p (a e) -> p a e", a=NTC)
    moe_fm2 = pool_b[:].rearrange("p (s f t) -> p s f t", s=2, f=NEXP)
    kv_b = nc.alloc_sbuf_tensor("kv_b", [128, 2, D], BF16)
    zrow = nc.alloc_sbuf_tensor("zrow", [1, T], F32)
    z_tm = nc.alloc_sbuf_tensor("z_tm", [128, NTC], F32)
    ones_b = nc.alloc_sbuf_tensor("ones_b", [128, 1], BF16)
    onesr_b = nc.alloc_sbuf_tensor("onesr_b", [1, 128], BF16)
    eyeb = nc.alloc_sbuf_tensor("eyeb_t", [128, 128], BF16)
    eyef = nc.alloc_sbuf_tensor("eyef_t", [128, 128], F32)
    st = st_alloc(nc)
    gTb = nc.alloc_sbuf_tensor("gTb", [128, NHC, NEXP], BF16)
    scor = nc.alloc_sbuf_tensor("scor", [128, NTC, NEXP], F32)
    gmax = nc.alloc_sbuf_tensor("gmax", [128, NTC], F32)
    pre = nc.alloc_sbuf_tensor("pre", [128, NTC], F32)
    tta = nc.alloc_sbuf_tensor("tta", [128, NTC], F32)
    ttb = nc.alloc_sbuf_tensor("ttb", [128, NTC], F32)
    masks_tm = nc.alloc_sbuf_tensor("masks_tm", [128, NTC, NEXP], BF16)
    shr_a = nc.alloc_sbuf_tensor("shr_a", [HD, T], BF16)
    masks_f = shr_a[0:NEXP, :]
    mtile = nc.alloc_sbuf_tensor("mtile", [128, 512], BF16)
    w_moe = nc.alloc_sbuf_tensor("w_moe", [128, 2, 1024], BF16)
    moe_tm2 = nc.alloc_sbuf_tensor("moe_tm2", [128, 2, D], BF16)
    bq_t = nc.alloc_sbuf_tensor("bq_t", [HD, 1], F32)
    bk_t = nc.alloc_sbuf_tensor("bk_t", [HD, 1], F32)
    bv_t = nc.alloc_sbuf_tensor("bv_t", [HD, 1], F32)
    qf = nc.alloc_sbuf_tensor("qf", [HD, T], BF16)
    kf = nc.alloc_sbuf_tensor("kf", [HD, T], BF16)
    vf = nc.alloc_sbuf_tensor("vf", [HD, T], BF16)
    sqn = shr_a[:, :]
    v_tmb = nc.alloc_sbuf_tensor("v_tmb", [128, NTC, HD], BF16)
    qkm_t = nc.alloc_sbuf_tensor("qkm_t", [1, 2], F32)
    tmp_res = nc.alloc_sbuf_tensor("tmp_res", [128, 512], F32)
    sel8 = tmp_res[:].bitcast(BF16)[0:NEXP, :]

    adr = lambda tt: nc.lookup_mloc(tt).addr
    w_moe2 = nc.alloc_sbuf_tensor_at("w_moe2", [128, 2, 1024], BF16, offset=adr(kf))
    mtile2 = nc.alloc_sbuf_tensor_at("mtile2", [128, 512], BF16, offset=adr(vf))

    def wslot(i):
        return (w_moe if (i % 4) < 2 else w_moe2)[:, i % 2, :]

    def mslot(i):
        return mtile[:] if i % 2 == 0 else mtile2[:]

    PB = [nc.alloc_psum_tensor(f"pb{i}", [128, 512], F32) for i in range(8)]
    PBb = [t[:].bitcast(BF16) for t in PB]

    p = Prog(nc)
    p.sems("sy", "gp", "pe", "act", "dve")

    with nc.Block() as b:
      try:
        boot_clear(b, nc, p)
        def sy0(s):
            p.inc(s.dma_start(x_tm[:], xa[:].rearrange("(c p) h -> p c h", p=128)), "sy", 16)
            p.inc(s.dma_start(eyeb[:], eyeb_d[:]), "sy", 16)
            p.inc(s.dma_start(eyef[:], eyef_d[:]), "sy", 16)
            p.inc(s.dma_start(bq_t[:], bqd[:]), "sy", 16)
            p.inc(s.dma_start(bk_t[:], bkd[:]), "sy", 16)
            p.inc(s.dma_start(bv_t[:], bvd[:]), "sy", 16)
        b.sync(sy0)
        def gp0(g):
            p.inc(g.dma_start(x_tmb[:], xa[:].rearrange("(c p) h -> p c h", p=128)), "gp", 16)
            p.inc(g.dma_start(x_fmb[:], xaT[:].rearrange("(c p) t -> p c t", p=128)), "gp", 16)
            p.inc(g.dma_start(qpb[:], qp[:].rearrange("(c p) e -> p c e", p=128)), "gp", 16)
            p.inc(g.dma_start(kpb[:], kp[:].rearrange("(c p) e -> p c e", p=128)), "gp", 16)
            p.inc(g.dma_start(gTb[:], gTd[:].rearrange("(c p) e -> p c e", p=128)), "gp", 16)
        b.gpsimd(gp0)
        def dv0(v):
            p.inc(v.memset(ones_b[:], 1.0), "dve")
            p.inc(v.memset(onesr_b[:], 1.0), "dve")
        b.vector(dv0)
        st_init(p, b, st)
        gp_ld = p.c["gp"]

        _chk(upto, 2)
        _sc_a = nc.named_scope("A_l0attn"); _sc_a.__enter__()
        act0 = p.c["act"]
        qpe, kpe = [], []
        def pe_qk(t):
            p.wait(t, "gp", gp_ld)
            p.wait(t, "sy")
            for ec in range(2):
                for tn in range(4):
                    g = ec * 4 + tn
                    if g >= 2:
                        t.wait_ge(p.s["act"], act0 + (g - 2) + 1)
                    bank = PB[g % 2]
                    for hc in range(NHC):
                        mm = t.matmul(bank[:], qpb[:, hc, ec * 128:(ec + 1) * 128],
                                      x_fmb[:, hc, tn * 512:(tn + 1) * 512],
                                      start=(hc == 0), stop=(hc == NHC - 1))
                    qpe.append(p.inc(mm, "pe"))
            for tc in range(NTC):
                if tc >= 2:
                    t.wait_ge(p.s["act"], act0 + 8 + (tc - 2) + 1)
                bank = PB[2 + tc % 2]
                for hc in range(NHC):
                    mm = t.matmul(bank[:, 0:E], x_fmb[:, hc, tc * 128:(tc + 1) * 128],
                                  kpb[:, hc, :], start=(hc == 0), stop=(hc == NHC - 1))
                kpe.append(p.inc(mm, "pe"))
        b.tensor(pe_qk)
        def act_qk(a):
            for ec in range(2):
                for tn in range(4):
                    g = ec * 4 + tn
                    p.wait(a, "pe", qpe[g])
                    p.selfw(a, "act")
                    p.inc(a.activation(qe[:, ec, tn * 512:(tn + 1) * 512],
                                       PB[g % 2][:], AF.Exp, bias=0.0, scale=1.0), "act")
            for tc in range(NTC):
                p.wait(a, "pe", kpe[tc])
                p.selfw(a, "act")
                p.inc(a.activation(ke[:, tc, :], PB[2 + tc % 2][:, 0:E],
                                   AF.Exp, bias=0.0, scale=1.0), "act")
        b.scalar(act_qk)

        _chk(upto, 3)
        dve0 = p.c["dve"]
        zpe = []
        def pe_z(t):
            p.wait(t, "act")
            t.wait_ge(p.s["dve"], 2)
            for tn in range(4):
                if tn >= 1:
                    t.wait_ge(p.s["dve"], dve0 + tn)
                for ec in range(2):
                    mm = t.matmul(PB[4][0:1, :], ones_b[:, 0:1],
                                  qe[:, ec, tn * 512:(tn + 1) * 512],
                                  start=(ec == 0), stop=(ec == 1))
                zpe.append(p.inc(mm, "pe"))
        b.tensor(pe_z)
        def dv_z(v):
            for tn in range(4):
                p.wait(v, "pe", zpe[tn])
                p.inc(v.tensor_copy(zrow[0:1, tn * 512:(tn + 1) * 512], PB[4][0:1, :]),
                      "dve")
            p.selfw(v, "dve")
            p.inc(v.tensor_single_scalar(zrow[:], zrow[:], 1e-8, ALU.add), "dve")
            p.selfw(v, "dve")
            p.inc(v.reciprocal(zrow[:], zrow[:]), "dve")
        b.vector(dv_z)
        dve1 = p.c["dve"]
        ztr = []
        def pe_zt(t):
            p.wait(t, "dve", dve1)
            for tc in range(NTC):
                if tc >= 2:
                    t.wait_ge(p.s["dve"], dve1 + (tc - 2) + 1)
                tr = t.transpose(PB[4 + tc % 2][:, 0:1],
                                 zrow[0:1, tc * 128:(tc + 1) * 128], eyef[0:1, 0:1])
                ztr.append(p.inc(tr, "pe"))
        b.tensor(pe_zt)
        def dv_zt(v):
            for tc in range(NTC):
                p.wait(v, "pe", ztr[tc])
                p.inc(v.tensor_copy(z_tm[:, tc:tc + 1], PB[4 + tc % 2][:, 0:1]), "dve")
        b.vector(dv_zt)

        _chk(upto, 4)
        dve2 = p.c["dve"]
        kvpe = []
        def pe_kv(t):
            for g in range(4):
                ec, hn = divmod(g, 2)
                if g >= 2:
                    t.wait_ge(p.s["dve"], dve2 + (g - 2) + 1)
                bank = PB[6 + g % 2]
                for tc in range(NTC):
                    mm = t.matmul(bank[:], ke[:, tc, ec * 128:(ec + 1) * 128],
                                  x_tmb[:, tc, hn * 512:(hn + 1) * 512],
                                  start=(tc == 0), stop=(tc == NTC - 1))
                kvpe.append(p.inc(mm, "pe"))
        b.tensor(pe_kv)
        def dv_kv(v):
            for g in range(4):
                ec, hn = divmod(g, 2)
                p.wait(v, "pe", kvpe[g])
                p.inc(v.tensor_copy(kv_b[:, ec, hn * 512:(hn + 1) * 512],
                                    PB[6 + g % 2][:]), "dve")
        b.vector(dv_kv)

        _chk(upto, 5)
        act1 = p.c["act"]
        numpe = []
        def pe_num(t):
            p.wait(t, "dve")
            for g in range(NTC * 2):
                tc, hn = divmod(g, 2)
                if g >= 2:
                    t.wait_ge(p.s["act"], act1 + (g - 2) + 1)
                bank = PB[g % 2]
                for ec in range(2):
                    mm = t.matmul(bank[:], qe[:, ec, tc * 128:(tc + 1) * 128],
                                  kv_b[:, ec, hn * 512:(hn + 1) * 512],
                                  start=(ec == 0), stop=(ec == 1))
                numpe.append(p.inc(mm, "pe"))
        b.tensor(pe_num)
        dve3 = p.c["dve"]
        numact = []
        def act_num(a):
            for g in range(NTC * 2):
                tc, hn = divmod(g, 2)
                p.wait(a, "pe", numpe[g])
                if g >= 1:
                    a.wait_ge(p.s["dve"], dve3 + g)
                p.selfw(a, "act")
                numact.append(p.inc(a.activation(tmp_res[:], PB[g % 2][:],
                                                 AF.Copy, bias=0.0,
                                                 scale=z_tm[:, tc:tc + 1]), "act"))
        b.scalar(act_num)
        def dv_num(v):
            for g in range(NTC * 2):
                tc, hn = divmod(g, 2)
                p.wait(v, "act", numact[g])
                p.selfw(v, "dve")
                p.inc(v.tensor_add(x_tm[:, tc, hn * 512:(hn + 1) * 512],
                                   x_tm[:, tc, hn * 512:(hn + 1) * 512],
                                   tmp_res[:]), "dve")
        b.vector(dv_num)

        _sc_a.__exit__(None, None, None)
        _chk(upto, 6)
        _sc_b = nc.named_scope("A_ln1"); _sc_b.__enter__()
        ln_stats_batch(p, b, x_tm, st, NTC)
        ln_apply_and_fm(p, b, x_tm, x_tmb, x_fmb, st, NTC, PBb, eyeb)

        _sc_b.__exit__(None, None, None)
        _chk(upto, 7)
        _sc_c = nc.named_scope("A_moe"); _sc_c.__enter__()
        scpe = []
        dve4 = p.c["dve"]
        def pe_sc(t):
            p.wait(t, "sy")
            p.wait(t, "dve")
            for tc in range(NTC):
                if tc >= 2:
                    t.wait_ge(p.s["dve"], dve4 + (tc - 2) + 1)
                bank = PB[2 + tc % 2]
                for hc in range(NHC):
                    mm = t.matmul(bank[:, 0:NEXP], x_fmb[:, hc, tc * 128:(tc + 1) * 128],
                                  gTb[:, hc, :], start=(hc == 0), stop=(hc == NHC - 1))
                scpe.append(p.inc(mm, "pe"))
        b.tensor(pe_sc)
        def dv_sc(v):
            for tc in range(NTC):
                p.wait(v, "pe", scpe[tc])
                p.inc(v.tensor_copy(scor[:, tc, :], PB[2 + tc % 2][:, 0:NEXP]), "dve")
            p.selfw(v, "dve")
            p.inc(v.reduce_max(gmax[:], scor[:], AX.X), "dve")
            for e in range(NEXP):
                se = scor[:, :, e]
                p.selfw(v, "dve")
                p.inc(v.tensor_tensor(tta[:], se, gmax[:], ALU.is_ge), "dve")
                if e == 0:
                    p.selfw(v, "dve")
                    p.inc(v.tensor_copy(masks_tm[:, :, 0], tta[:]), "dve")
                    p.inc(v.tensor_copy(pre[:], se), "dve")
                else:
                    p.selfw(v, "dve")
                    p.inc(v.tensor_tensor(ttb[:], pre[:], gmax[:], ALU.is_lt), "dve")
                    p.selfw(v, "dve")
                    p.inc(v.tensor_mul(tta[:], tta[:], ttb[:]), "dve")
                    p.selfw(v, "dve")
                    p.inc(v.tensor_copy(masks_tm[:, :, e], tta[:]), "dve")
                    if e < NEXP - 1:
                        p.selfw(v, "dve")
                        p.inc(v.tensor_max(pre[:], pre[:], se), "dve")
        b.vector(dv_sc)
        _chk(upto, 8)
        dve5 = p.c["dve"]
        mtr = []
        def pe_mt(t):
            p.wait(t, "dve", dve5)
            for tc in range(NTC):
                if tc >= 2:
                    t.wait_ge(p.s["dve"], dve5 + (tc - 2) + 1)
                tr = t.transpose(PBb[4 + tc % 2][0:NEXP, 0:128], masks_tm[:, tc, :],
                                 eyeb[:, :])
                mtr.append(p.inc(tr, "pe"))
        b.tensor(pe_mt)
        def dv_mt(v):
            for tc in range(NTC):
                p.wait(v, "pe", mtr[tc])
                p.inc(v.tensor_copy(masks_f[0:NEXP, tc * 128:(tc + 1) * 128],
                                    PBb[4 + tc % 2][0:NEXP, 0:128]), "dve")
        b.vector(dv_mt)

        _chk(upto, 9)
        def sy_sel(s):
            p.wait(s, "dve")
            p.inc(s.dma_start(sel8, sel8_d[:]), "sy", 16)
        b.sync(sy_sel)

        for tn in range(4):
            mbdve = []
            for e in range(NEXP):
                def pe_mb(t, e=e, tn=tn):
                    if e < 2:
                        p.wait(t, "dve")
                        p.wait(t, "sy")
                    else:
                        t.wait_ge(p.s["dve"], mbdve[e - 2])
                    p.inc(t.matmul(PB[6 + e % 2][:], sel8[:, e * 128:(e + 1) * 128],
                                   masks_f[0:NEXP, tn * 512:(tn + 1) * 512],
                                   start=True, stop=True), "pe")
                b.tensor(pe_mb)
                def dv_mb(v, e=e):
                    p.wait(v, "pe")
                    p.selfw(v, "dve")
                    p.inc(v.tensor_copy(mb_all[:, e, :], PB[6 + e % 2][:]), "dve")
                    mbdve.append(p.c["dve"])
                b.vector(dv_mb)
            pe_moe0 = p.c["pe"]
            gp_moe0 = p.c["gp"]
            dve_moe0 = p.c["dve"]
            def gp_w(g, tn=tn):
                for j in range(NEXP * NHC // 2):
                    e, hc = divmod(2 * j, NHC)
                    if j >= 2:
                        g.wait_ge(p.s["pe"], pe_moe0 + 2 * j - 2)
                    elif pe_moe0 > 0:
                        g.wait_ge(p.s["pe"], pe_moe0)
                    dst = w_moe[:] if j % 2 == 0 else w_moe2[:]
                    p.inc(g.dma_start(
                        dst, ewT[e, hc * 128:(hc + 2) * 128, :].rearrange(
                            "(s p) h -> p s h", p=128)), "gp", 16)
            b.gpsimd(gp_w)
            def dv_mm(v, tn=tn):
                for i in range(NEXP * NHC):
                    e, hc = divmod(i, NHC)
                    if i >= 2:
                        v.wait_ge(p.s["pe"], pe_moe0 + i - 1)
                    elif pe_moe0 > 0:
                        v.wait_ge(p.s["pe"], pe_moe0)
                    p.selfw(v, "dve")
                    p.inc(v.tensor_tensor(mslot(i), x_fmb[:, hc, tn * 512:(tn + 1) * 512],
                                          mb_all[:, e, :], ALU.mult), "dve")
            b.vector(dv_mm)
            def pe_moe(t, tn=tn):
                for i in range(NEXP * NHC):
                    e, hc = divmod(i, NHC)
                    t.wait_ge(p.s["gp"], gp_moe0 + (i // 2 + 1) * 16)
                    t.wait_ge(p.s["dve"], dve_moe0 + i + 1)
                    wsl = (w_moe if (i // 2) % 2 == 0 else w_moe2)[:, i % 2, :]
                    for fc in range(8):
                        mm = t.matmul(PB[fc][:], wsl[:, fc * 128:(fc + 1) * 128],
                                      mslot(i), start=(i == 0),
                                      stop=(i == NEXP * NHC - 1))
                    p.inc(mm, "pe")
            b.tensor(pe_moe)
            slot = tn % 2
            def dv_ev(v, tn=tn, slot=slot):
                p.wait(v, "pe")
                p.wait(v, "sy")
                p.selfw(v, "dve")
                for fc in range(8):
                    p.inc(v.tensor_copy(moe_fm2[:, slot, fc, :], PB[fc][:]), "dve")
            b.vector(dv_ev)
            for tj in range(4):
                tc = tn * 4 + tj
                def sy_tr(s, tn=tn, slot=slot, tj=tj, tc=tc):
                    p.wait(s, "dve")
                    for hc in range(NHC):
                        p.selfw(s, "sy")
                        p.inc(s.dma_start(
                            moe_tm2[:, tc % 2, hc * 128:(hc + 1) * 128],
                            moe_fm2[:, slot, hc, tj * 128:(tj + 1) * 128],
                            transpose=True), "sy", 16)
                b.sync(sy_tr)
                def dv_res(v, tc=tc):
                    p.wait(v, "sy")
                    p.selfw(v, "dve")
                    p.inc(v.tensor_scalar(x_tm[:, tc, :], x_tm[:, tc, :], 2.0, 0.0,
                                          ALU.mult, ALU.add), "dve")
                    p.selfw(v, "dve")
                    p.inc(v.tensor_add(x_tm[:, tc, :], x_tm[:, tc, :],
                                       moe_tm2[:, tc % 2, :]), "dve")
                b.vector(dv_res)


        _sc_c.__exit__(None, None, None)
        _chk(upto, 10)
        _sc_d = nc.named_scope("A_ln2_out"); _sc_d.__enter__()
        ln_stats_batch(p, b, x_tm, st, NTC)
        ln_apply_and_fm(p, b, x_tm, x_tmb, x_fmb, st, NTC, PBb, eyeb)

        def sy_x3(s):
            p.wait(s, "dve")
            p.inc(s.dma_start(x3_o[:].rearrange("(c p) h -> p c h", p=128), x_tm[:]),
                  "sy", 16)
        b.sync(sy_x3)

        _chk(upto, 11)
        def gp_w2(g):
            p.wait(g, "pe")
            p.inc(g.dma_start(wq_b, wqT[:].rearrange("(c p) e -> p c e", p=128)), "gp", 16)
            p.inc(g.dma_start(wk_b, wkT[:].rearrange("(c p) e -> p c e", p=128)), "gp", 16)
            p.inc(g.dma_start(wv_b, wvT[:].rearrange("(c p) e -> p c e", p=128)), "gp", 16)
        b.gpsimd(gp_w2)
        gp_ld2 = p.c["gp"]
        act2c = p.c["act"]
        prpe = []
        def pe_pr(t):
            p.wait(t, "sy")
            p.wait(t, "dve")
            p.wait(t, "gp", gp_ld2)
            for j, wt in enumerate((wq_b, wk_b, wv_b)):
                for tn in range(4):
                    g = j * 4 + tn
                    if g >= 2:
                        t.wait_ge(p.s["act"], act2c + (g - 2) + 1)
                    bank = PB[g % 2]
                    for hc in range(NHC):
                        mm = t.matmul(bank[0:HD, :], wt[:, hc, :],
                                      x_fmb[:, hc, tn * 512:(tn + 1) * 512],
                                      start=(hc == 0), stop=(hc == NHC - 1))
                    prpe.append(p.inc(mm, "pe"))
        b.tensor(pe_pr)
        def act_pr(a):
            outs = (qf, kf, vf)
            biases = (bq_t, bk_t, bv_t)
            for j in range(3):
                for tn in range(4):
                    g = j * 4 + tn
                    p.wait(a, "pe", prpe[g])
                    p.selfw(a, "act")
                    p.inc(a.activation(outs[j][:, tn * 512:(tn + 1) * 512],
                                       PB[g % 2][0:HD, :], AF.Identity,
                                       bias=biases[j][:], scale=1.0), "act")
        b.scalar(act_pr)

        _chk(upto, 12)
        dve6 = p.c["dve"]
        vtr = []
        def pe_vt(t):
            p.wait(t, "act")
            for tc in range(NTC):
                if tc >= 2:
                    t.wait_ge(p.s["dve"], dve6 + (tc - 2) + 1)
                tr = t.transpose(PBb[2 + tc % 2][0:128, 0:HD],
                                 vf[0:HD, tc * 128:(tc + 1) * 128], eyeb[0:HD, 0:HD])
                vtr.append(p.inc(tr, "pe"))
        b.tensor(pe_vt)
        def dv_vt(v):
            for tc in range(NTC):
                p.wait(v, "pe", vtr[tc])
                p.inc(v.tensor_copy(v_tmb[:, tc, :], PBb[2 + tc % 2][0:128, 0:HD]),
                      "dve")
        b.vector(dv_vt)

        _chk(upto, 13)
        dve7 = p.c["dve"]
        nrmpe = []
        def dv_n1(v):
            p.wait(v, "act")
            p.selfw(v, "dve")
            p.inc(v.tensor_mul(sqn[:], qf[:], qf[:]), "dve")
        b.vector(dv_n1)
        def pe_nrm(t):
            for j in range(2):
                for tn in range(4):
                    g = j * 4 + tn
                    if g == 0:
                        t.wait_ge(p.s["dve"], dve7 + 1)
                    elif j == 0:
                        t.wait_ge(p.s["dve"], dve7 + 1 + g)
                    elif tn == 0:
                        t.wait_ge(p.s["dve"], dve7 + 7)
                    else:
                        t.wait_ge(p.s["dve"], dve7 + 7 + tn)
                    mm = t.matmul(PB[4][0:1, :], ones_b[0:HD, 0:1],
                                  sqn[:, tn * 512:(tn + 1) * 512], start=True, stop=True)
                    nrmpe.append(p.inc(mm, "pe"))
        b.tensor(pe_nrm)
        def dv_n2(v):
            for tn in range(4):
                p.wait(v, "pe", nrmpe[tn])
                p.selfw(v, "dve")
                p.inc(v.tensor_copy(zrow[0:1, tn * 512:(tn + 1) * 512], PB[4][0:1, :]),
                      "dve")
            p.selfw(v, "dve")
            p.inc(v.tensor_reduce(qkm_t[0:1, 0:1], zrow[:], AX.X, ALU.max), "dve")
            p.selfw(v, "dve")
            p.inc(v.tensor_mul(sqn[:], kf[:], kf[:]), "dve")
            for tn in range(4):
                p.wait(v, "pe", nrmpe[4 + tn])
                p.selfw(v, "dve")
                p.inc(v.tensor_copy(zrow[0:1, tn * 512:(tn + 1) * 512], PB[4][0:1, :]),
                      "dve")
            p.selfw(v, "dve")
            p.inc(v.tensor_reduce(qkm_t[0:1, 1:2], zrow[:], AX.X, ALU.max), "dve")
        b.vector(dv_n2)

        _chk(upto, 14)
        def sy_out(s):
            p.wait(s, "dve")
            p.wait(s, "act")
            p.inc(s.dma_start(q_o[:], qf[:]), "sy", 16)
            p.inc(s.dma_start(k_o[:], kf[:]), "sy", 16)
            p.inc(s.dma_start(v_o[:].rearrange("(c p) d -> p c d", p=128), v_tmb[:]),
                  "sy", 16)
            p.inc(s.dma_start(qkm_o[:], qkm_t[:]), "sy", 16)
            p.wait(s, "sy")
        b.sync(sy_out)
        _sc_d.__exit__(None, None, None)
      except _Stop:
        def sy_tail(s):
            p.wait(s, "dve")
            p.wait(s, "act")
            p.wait(s, "pe")
            p.inc(s.dma_start(x3_o[:].rearrange("(c p) h -> p c h", p=128),
                              x_tm[:]), "sy", 16)
            p.inc(s.dma_start(q_o[:], qf[:]), "sy", 16)
            p.wait(s, "sy")
        b.sync(sy_tail)
    return nc



def build_B():
    nc = bass.Bass("TRN2", target_bir_lowering=False)
    x3 = nc.dram_tensor("x3", [T, D], F32, kind="ExternalInput")
    qfd = nc.dram_tensor("qf", [HD, T], BF16, kind="ExternalInput")
    kfull = nc.dram_tensor("kfull", [HD, NCORE * T], BF16, kind="ExternalInput")
    vfull = nc.dram_tensor("vfull", [NCORE * T, HD], BF16, kind="ExternalInput")
    qkmd = nc.dram_tensor("qkm", [1, 16], F32, kind="ExternalInput")
    w1T = nc.dram_tensor("w1T", [32, 128, NHC, 128], BF16, kind="ExternalInput")
    w2T = nc.dram_tensor("w2T", [32, 128, NHC, 128], BF16, kind="ExternalInput")
    h1T = nc.dram_tensor("h1T", [32, 128, 32, 128], BF16, kind="ExternalInput")
    h2T = nc.dram_tensor("h2T", [32, 128, 32, 128], BF16, kind="ExternalInput")
    w3T = nc.dram_tensor("w3T", [DH, D], BF16, kind="ExternalInput")
    s1d = nc.dram_tensor("s1d", [128, 32], F32, kind="ExternalInput")
    s2d = nc.dram_tensor("s2d", [128, 32], F32, kind="ExternalInput")
    h1sd = nc.dram_tensor("h1sd", [128, 32], F32, kind="ExternalInput")
    h2sd = nc.dram_tensor("h2sd", [128, 32], F32, kind="ExternalInput")
    s3d = nc.dram_tensor("s3d", [1, D], F32, kind="ExternalInput")
    eyef_d = nc.dram_tensor("eyef", [128, 128], F32, kind="ExternalInput")
    out_o = nc.dram_tensor("out_o", [T, D], F32, kind="ExternalOutput")
    x4_d = nc.dram_tensor("x4_d", [T, D], F32, kind="Internal")

    NK = (NCORE * T) // 128

    arena = nc.alloc_sbuf_tensor("arena", [128, 65536], BF16)
    cur = [0]
    def carve(n):
        cur[0] = (cur[0] + 15) // 16 * 16
        off = cur[0]
        cur[0] += n
        return off
    o_k = carve(16384)
    k_sb = arena[:, o_k:o_k + 16384][0:HD, :]
    o_v = carve(NK * 65 + 7)
    v65 = arena[:, o_v:o_v + NK * 65].rearrange("p (c d) -> p c d", d=65)
    o_pt = carve(2048)
    pt_a = arena[:, o_pt:o_pt + 2048].rearrange("p (s t) -> p s t", s=2)
    o_y = carve(2048)
    y_sb = arena[:, o_y:o_y + 2048].bitcast(F32)
    o_q = carve(2048)
    qf_b = arena[:, o_q:o_q + 2048][0:HD, :]
    o_x3 = carve(4096)
    x3t = arena[:, o_x3:o_x3 + 4096].bitcast(F32).rearrange(
        "p (s h) -> p s h", s=2)
    h_a = arena[:, 0:32768].rearrange("p (c t) -> p c t", c=32)
    h_b = arena[:, 32768:65536].rearrange("p (c t) -> p c t", c=32)

    x_fmb = nc.alloc_sbuf_tensor("x_fmb", [128, NHC, T], BF16)
    x4t = nc.alloc_sbuf_tensor("x4t", [128, 1, D], F32)
    xtmb = nc.alloc_sbuf_tensor("xtmb", [128, 1, D], BF16)
    yT_sb = nc.alloc_sbuf_tensor("yT_sb", [128, 65], F32)
    z1 = nc.alloc_sbuf_tensor("z1", [128, 1], F32)
    ybar = nc.alloc_sbuf_tensor("ybar", [128, HD], F32)
    st = st_alloc(nc)
    qkm_t = nc.alloc_sbuf_tensor("qkm_t", [1, 16], F32)
    csc = nc.alloc_sbuf_tensor("cs", [1, 4], F32)
    cb = nc.alloc_sbuf_tensor("cb", [128, 1], F32)
    ncb = nc.alloc_sbuf_tensor("ncb", [128, 1], F32)
    inv32 = nc.alloc_sbuf_tensor("inv32", [128, 1], F32)
    onesf = nc.alloc_sbuf_tensor("onesf", [1, 128], F32)
    eyef = nc.alloc_sbuf_tensor("eyef_t", [128, 128], F32)
    s1_t = nc.alloc_sbuf_tensor("s1_t", [128, 32], F32)
    s2_t = nc.alloc_sbuf_tensor("s2_t", [128, 32], F32)
    h1s_t = nc.alloc_sbuf_tensor("h1s_t", [128, 32], F32)
    h2s_t = nc.alloc_sbuf_tensor("h2s_t", [128, 32], F32)
    s3r = nc.alloc_sbuf_tensor("s3r", [1, D], F32)
    s3_bb = nc.alloc_sbuf_tensor("s3_bb", [128, D], F32)
    hw_b = nc.alloc_sbuf_tensor("hw_b", [128, 2, 32, 128], BF16)
    w1_b = hw_b[:, :, 0:NHC, :]
    w2_b = hw_b[:, :, NHC:2 * NHC, :]
    w3_b = hw_b[:].rearrange("p s c f -> p s (c f)")[:, :, 0:1024]
    silu_s = nc.alloc_sbuf_tensor("silu_s", [128, 1024], F32)
    t2_s = nc.alloc_sbuf_tensor("t2_s", [128, 1024], F32)
    o_sb = silu_s[:]

    psSA = nc.alloc_psum_tensor("psSA", [128, 1024], F32)
    psSB = nc.alloc_psum_tensor("psSB", [128, 1024], F32)
    psY = nc.alloc_psum_tensor("psY", [128, 1024], F32)
    psC = nc.alloc_psum_tensor("psC", [128, 512], F32)
    psT = nc.alloc_psum_tensor("psT", [128, 512], F32)

    p = Prog(nc)
    p.sems("sy", "gp", "pe", "act", "dve")

    with nc.Block() as b:
        boot_clear(b, nc, p)
        def sy0(s):
            p.inc(s.dma_start(k_sb[:, :], kfull[:]), "sy", 16)
            p.inc(s.dma_start(qf_b[:, :], qfd[:]), "sy", 16)
            p.inc(s.dma_start(qkm_t[:], qkmd[:]), "sy", 16)
            p.inc(s.dma_start(eyef[:], eyef_d[:]), "sy", 16)
            p.inc(s.dma_start(s1_t[:], s1d[:]), "sy", 16)
            p.inc(s.dma_start(s2_t[:], s2d[:]), "sy", 16)
            p.inc(s.dma_start(h1s_t[:], h1sd[:]), "sy", 16)
            p.inc(s.dma_start(h2s_t[:], h2sd[:]), "sy", 16)
            p.inc(s.dma_start(s3r[:], s3d[:]), "sy", 16)
        b.sync(sy0)
        def dv0(v):
            p.inc(v.memset(onesf[:], 1.0), "dve")
            p.inc(v.memset(v65[:, :, 64:65], 1.0), "dve")
            p.inc(v.memset(inv32[:], 1.0 / 32.0), "dve")
        b.vector(dv0)
        st_init(p, b, st)
        def sy1(s):
            p.wait(s, "dve")
            p.inc(s.dma_start(v65[:, :, 0:64],
                              vfull[:].rearrange("(c p) d -> p c d", p=128)), "sy", 16)
        b.sync(sy1)
        def dv_c(v):
            p.wait(v, "sy")
            p.inc(v.tensor_reduce(csc[0:1, 0:2],
                                  qkm_t[:].rearrange("o (a c) -> o a c", a=2),
                                  AX.X, ALU.max), "dve")
            p.selfw(v, "dve")
            p.inc(v.tensor_add(csc[0:1, 2:3], csc[0:1, 0:1], csc[0:1, 1:2]), "dve")
            p.selfw(v, "dve")
            p.inc(v.tensor_single_scalar(csc[0:1, 3:4], csc[0:1, 2:3], 1.0 / 64.0,
                                         ALU.mult), "dve")
        b.vector(dv_c)
        def pe_c(t):
            p.wait(t, "dve")
            p.inc(t.matmul(psC[:, 0:1], onesf[0:1, :], csc[0:1, 3:4],
                           start=True, stop=True), "pe")
        b.tensor(pe_c)
        def dv_c2(v):
            p.wait(v, "pe")
            p.inc(v.tensor_copy(cb[:], psC[:, 0:1]), "dve")
            p.selfw(v, "dve")
            p.inc(v.tensor_single_scalar(ncb[:], cb[:], -1.0, ALU.mult), "dve")
        b.vector(dv_c2)

        _sc_e = nc.named_scope("B_attn"); _sc_e.__enter__()
        for th in range(2):
            toff = th * 1024
            act_b = p.c["act"]
            spe, vpe = [], []
            def pe_at(t, toff=toff, act_b=act_b, spe=spe, vpe=vpe):
                p.wait(t, "sy")
                p.wait(t, "dve")
                for kc in range(NK):
                    bank = psSA if kc % 2 == 0 else psSB
                    if kc >= 2:
                        t.wait_ge(p.s["act"], act_b + (kc - 2) + 1)
                    for nn in range(2):
                        mm = t.matmul(bank[:, nn * 512:(nn + 1) * 512],
                                      k_sb[:, kc * 128:(kc + 1) * 128],
                                      qf_b[:, toff + nn * 512:toff + (nn + 1) * 512],
                                      start=True, stop=True)
                    spe.append(p.inc(mm, "pe"))
                    if kc >= 1:
                        pk = kc - 1
                        t.wait_ge(p.s["act"], act_b + pk + 1)
                        for nn in range(2):
                            mm = t.matmul(psY[0:65, nn * 512:(nn + 1) * 512],
                                          v65[:, pk, :],
                                          pt_a[:, pk % 2, nn * 512:(nn + 1) * 512],
                                          start=(pk == 0), stop=False)
                        vpe.append(p.inc(mm, "pe"))
                t.wait_ge(p.s["act"], act_b + NK)
                for nn in range(2):
                    mm = t.matmul(psY[0:65, nn * 512:(nn + 1) * 512], v65[:, NK - 1, :],
                                  pt_a[:, (NK - 1) % 2, nn * 512:(nn + 1) * 512],
                                  start=False, stop=True)
                vpe.append(p.inc(mm, "pe"))
            b.tensor(pe_at)
            def act_at(a, spe=spe, vpe=vpe):
                for kc in range(NK):
                    p.wait(a, "pe", spe[kc])
                    if kc >= 2:
                        p.wait(a, "pe", vpe[kc - 2])
                    bank = psSA if kc % 2 == 0 else psSB
                    p.selfw(a, "act")
                    p.inc(a.activation(pt_a[:, kc % 2, :], bank[:], AF.Exp,
                                       bias=ncb[:], scale=inv32[:]), "act")
            b.scalar(act_at)
            def dv_y(v):
                p.wait(v, "pe")
                p.inc(v.tensor_copy(y_sb[0:65, :], psY[0:65, :]), "dve")
            b.vector(dv_y)

            for tj in range(8):
                tc = th * 8 + tj
                sl = 0
                def sy_ld(s, tc=tc, sl=sl):
                    p.wait(s, "dve")
                    p.inc(s.dma_start(x3t[:, sl, :], x3[tc * 128:(tc + 1) * 128, :]),
                          "sy", 16)
                b.sync(sy_ld)
                def pe_t(t, tj=tj):
                    p.wait(t, "dve")
                    p.inc(t.transpose(psT[:, 0:65],
                                      y_sb[0:65, tj * 128:(tj + 1) * 128],
                                      eyef[0:65, 0:65]), "pe")
                b.tensor(pe_t)
                def dv_t(v, tc=tc, sl=sl):
                    p.wait(v, "pe")
                    p.wait(v, "sy")
                    p.selfw(v, "dve")
                    p.inc(v.tensor_copy(yT_sb[:], psT[:, 0:65]), "dve")
                    p.selfw(v, "dve")
                    p.inc(v.reciprocal(z1[:], yT_sb[:, 64:65]), "dve")
                    p.selfw(v, "dve")
                    p.inc(v.tensor_scalar(ybar[:], yT_sb[:, 0:64], z1[:], 0.0,
                                          ALU.mult, ALU.add), "dve")
                    p.selfw(v, "dve")
                    p.inc(v.tensor_tensor(
                        x4t[:, sl, :].rearrange("p (r d) -> p r d", r=16),
                        x3t[:, sl, :].rearrange("p (r d) -> p r d", r=16),
                        ybar[:].unsqueeze(1).broadcast_to([128, 16, HD]), ALU.add),
                        "dve")
                b.vector(dv_t)
                ln_tile(p, b, x4t[:, sl, :], st, tc % NTC)
                def dv_cp(v, sl=sl):
                    p.wait(v, "sy")
                    p.selfw(v, "dve")
                    p.inc(v.tensor_copy(xtmb[:, sl, :], x4t[:, sl, :]), "dve")
                b.vector(dv_cp)
                def sy_st(s, tc=tc, sl=sl):
                    p.wait(s, "dve")
                    p.inc(s.dma_start(x4_d[tc * 128:(tc + 1) * 128, :], x4t[:, sl, :]),
                          "sy", 16)
                    for hc in range(NHC):
                        p.selfw(s, "sy")
                        p.inc(s.dma_start(x_fmb[:, hc, tc * 128:(tc + 1) * 128],
                                          xtmb[:, sl, hc * 128:(hc + 1) * 128],
                                          transpose=True), "sy", 16)
                b.sync(sy_st)

        _sc_e.__exit__(None, None, None)
        _sc_f = nc.named_scope("B_ffn"); _sc_f.__enter__()
        def pe_s3(t):
            p.wait(t, "dve")
            p.inc(t.matmul(psC[:, :], onesf[0:1, :], s3r[0:1, 0:512],
                           start=True, stop=True), "pe")
            p.inc(t.matmul(psT[:, :], onesf[0:1, :], s3r[0:1, 512:1024],
                           start=True, stop=True), "pe")
        b.tensor(pe_s3)
        def dv_s3(v):
            p.wait(v, "pe")
            p.selfw(v, "dve")
            p.inc(v.tensor_copy(s3_bb[:, 0:512], psC[:]), "dve")
            p.inc(v.tensor_copy(s3_bb[:, 512:1024], psT[:]), "dve")
        b.vector(dv_s3)

        for th in range(2):
            toff = th * 1024

            pe_b = p.c["pe"]
            act_b = p.c["act"]
            dve_b = p.c["dve"]
            gp_b = p.c["gp"]
            def gp_w1(g, pe_b=pe_b):
                for fc in range(32):
                    if fc < 2:
                        if pe_b > 0:
                            g.wait_ge(p.s["pe"], pe_b)
                    else:
                        g.wait_ge(p.s["pe"], pe_b + (fc - 2) + 1)
                    p.inc(g.dma_start(w1_b[:, fc % 2, :, :], w1T[fc]), "gp", 16)
                    p.inc(g.dma_start(w2_b[:, fc % 2, :, :], w2T[fc]), "gp", 16)
            b.gpsimd(gp_w1)
            def pe_s1(t, toff=toff, gp_b=gp_b, act_b=act_b):
                p.wait(t, "sy")
                p.wait(t, "dve")
                for fc in range(32):
                    t.wait_ge(p.s["gp"], gp_b + (fc + 1) * 32)
                    if fc >= 1:
                        t.wait_ge(p.s["act"], act_b + 2 * (fc - 1) + 2)
                    for hc in range(NHC):
                        for nn in range(2):
                            mm = t.matmul(psSA[:, nn * 512:(nn + 1) * 512],
                                          w1_b[:, fc % 2, hc, :],
                                          x_fmb[:, hc, toff + nn * 512:toff + (nn + 1) * 512],
                                          start=(hc == 0), stop=(hc == NHC - 1))
                    for hc in range(NHC):
                        for nn in range(2):
                            mm = t.matmul(psSB[:, nn * 512:(nn + 1) * 512],
                                          w2_b[:, fc % 2, hc, :],
                                          x_fmb[:, hc, toff + nn * 512:toff + (nn + 1) * 512],
                                          start=(hc == 0), stop=(hc == NHC - 1))
                    p.inc(mm, "pe")
            b.tensor(pe_s1)
            s1pe_list = list(range(pe_b + 1, pe_b + 33))
            def act_s1(a, dve_b=dve_b, s1pe_list=s1pe_list):
                for fc in range(32):
                    p.wait(a, "pe", s1pe_list[fc])
                    if fc >= 1:
                        a.wait_ge(p.s["dve"], dve_b + fc)
                    p.selfw(a, "act")
                    p.inc(a.activation(silu_s[:], psSA[:], AF.Silu,
                                       bias=0.0, scale=s1_t[:, fc:fc + 1]), "act")
                    p.inc(a.activation(t2_s[:], psSB[:], AF.Copy,
                                       bias=0.0, scale=s2_t[:, fc:fc + 1]), "act")
            b.scalar(act_s1)
            def dv_s1(v, act_b=act_b):
                for fc in range(32):
                    v.wait_ge(p.s["act"], act_b + 2 * fc + 2)
                    p.selfw(v, "dve")
                    p.inc(v.tensor_mul(h_a[:, fc, :], silu_s[:], t2_s[:]), "dve")
            b.vector(dv_s1)

            for wT, ssc, src, dst in ((h1T, h1s_t, h_a, h_b), (h2T, h2s_t, h_b, h_a)):
                pe_b = p.c["pe"]
                act_b = p.c["act"]
                gp_b = p.c["gp"]
                def gp_wh(g, wT=wT, pe_b=pe_b):
                    for fc in range(32):
                        if fc < 2:
                            g.wait_ge(p.s["pe"], pe_b)
                        else:
                            g.wait_ge(p.s["pe"], pe_b + (fc - 2) + 1)
                        p.inc(g.dma_start(hw_b[:, fc % 2, :, :], wT[fc]), "gp", 16)
                b.gpsimd(gp_wh)
                def pe_h(t, src=src, gp_b=gp_b, act_b=act_b):
                    p.wait(t, "dve")
                    p.wait(t, "act")
                    for fc in range(32):
                        ps = psSA if fc % 2 == 0 else psSB
                        t.wait_ge(p.s["gp"], gp_b + (fc + 1) * 16)
                        if fc >= 2:
                            t.wait_ge(p.s["act"], act_b + fc - 1)
                        for kc in range(32):
                            for nn in range(2):
                                mm = t.matmul(ps[:, nn * 512:(nn + 1) * 512],
                                              hw_b[:, fc % 2, kc, :],
                                              src[:, kc, nn * 512:(nn + 1) * 512],
                                              start=(kc == 0), stop=(kc == 31))
                        p.inc(mm, "pe")
                b.tensor(pe_h)
                hpe_list = list(range(pe_b + 1, pe_b + 33))
                def act_h(a, ssc=ssc, dst=dst, hpe_list=hpe_list):
                    for fc in range(32):
                        ps = psSA if fc % 2 == 0 else psSB
                        p.wait(a, "pe", hpe_list[fc])
                        p.selfw(a, "act")
                        p.inc(a.activation(dst[:, fc, :], ps[:], AF.Gelu,
                                           bias=0.0, scale=ssc[:, fc:fc + 1]), "act")
                b.scalar(act_h)

            for tg in range(2):
                banks = [(psSA[:, 0:512], psSA[:, 512:1024]),
                         (psSB[:, 0:512], psSB[:, 512:1024]),
                         (psY[:, 0:512], psY[:, 512:1024]),
                         (psC[:, :], psT[:, :])]
                pe_b = p.c["pe"]
                gp_b = p.c["gp"]
                def gp_w3(g, tg=tg, pe_b=pe_b):
                    for kc in range(32):
                        if kc < 2:
                            g.wait_ge(p.s["pe"], pe_b)
                        else:
                            g.wait_ge(p.s["pe"], pe_b + (kc - 2) + 1)
                        p.inc(g.dma_start(w3_b[:, kc % 2, :],
                                          w3T[kc * 128:(kc + 1) * 128, :]), "gp", 16)
                b.gpsimd(gp_w3)
                def pe_w3(t, tg=tg, gp_b=gp_b, banks=banks):
                    p.wait(t, "act")
                    p.wait(t, "dve")
                    for kc in range(32):
                        t.wait_ge(p.s["gp"], gp_b + (kc + 1) * 16)
                        for tcc in range(4):
                            lh = h_a[:, kc,
                                     tg * 512 + tcc * 128:tg * 512 + (tcc + 1) * 128]
                            for nn in range(2):
                                mm = t.matmul(banks[tcc][nn], lh,
                                              w3_b[:, kc % 2, nn * 512:(nn + 1) * 512],
                                              start=(kc == 0), stop=(kc == 31))
                        p.inc(mm, "pe")
                b.tensor(pe_w3)
                for tcc in range(4):
                    tc = th * 8 + tg * 4 + tcc
                    sl = 0
                    def sy_l4(s, tc=tc, sl=sl):
                        p.wait(s, "dve")
                        p.inc(s.dma_start(x4t[:, sl, :],
                                          x4_d[tc * 128:(tc + 1) * 128, :]), "sy", 16)
                    b.sync(sy_l4)
                    def dv_o(v, tcc=tcc, sl=sl, banks=banks):
                        p.wait(v, "pe")
                        p.wait(v, "sy")
                        p.selfw(v, "dve")
                        p.inc(v.tensor_copy(o_sb[:, 0:512], banks[tcc][0]), "dve")
                        p.inc(v.tensor_copy(o_sb[:, 512:1024], banks[tcc][1]), "dve")
                        p.selfw(v, "dve")
                        p.inc(v.tensor_mul(o_sb[:], o_sb[:], s3_bb[:]), "dve")
                        p.selfw(v, "dve")
                        p.inc(v.tensor_add(x4t[:, sl, :], x4t[:, sl, :], o_sb[:]), "dve")
                    b.vector(dv_o)
                    ln_tile(p, b, x4t[:, sl, :], st, tc % NTC)
                    def sy_o(s, tc=tc, sl=sl):
                        p.wait(s, "dve")
                        p.inc(s.dma_start(out_o[tc * 128:(tc + 1) * 128, :],
                                          x4t[:, sl, :]), "sy", 16)
                    b.sync(sy_o)

        _sc_f.__exit__(None, None, None)
        def sy_fin(s):
            p.wait(s, "sy")
        b.sync(sy_fin)
    return nc



_bf = ml_dtypes.bfloat16


def _tile_w(w, nkc):
    a = np.asarray(w, np.float32).reshape(32, 128, nkc, 128)
    return np.ascontiguousarray(a.transpose(0, 3, 2, 1)).astype(_bf)


def sel8_host():
    s = np.zeros((NEXP, NEXP * 128), dtype=_bf)
    for e in range(NEXP):
        s[e, e * 128:(e + 1) * 128] = 1.0
    return s


def kernel(L0_attn_x, L0_attn_q_proj, L0_attn_k_proj, L0_ffn_gate_w, L0_ffn_expert_w,
           L1_attn_Wq, L1_attn_bq, L1_attn_Wk, L1_attn_bk, L1_attn_Wv, L1_attn_bv,
           L1_ffn_w1_w, L1_ffn_w1_s, L1_ffn_w2_w, L1_ffn_w2_s,
           L1_ffn_h1_w, L1_ffn_h1_s, L1_ffn_h2_w, L1_ffn_h2_s,
           L1_ffn_w3_w, L1_ffn_w3_s):
    f32 = np.float32
    ct = np.ascontiguousarray
    x = np.asarray(L0_attn_x, f32)
    eyeb = np.eye(128, dtype=_bf)
    eyef = np.eye(128, dtype=f32)

    common_A = {
        "qp": ct(np.asarray(L0_attn_q_proj, f32)),
        "kp": ct(np.asarray(L0_attn_k_proj, f32)),
        "gT": ct(np.asarray(L0_ffn_gate_w, f32).T),
        "ewT": ct(np.asarray(L0_ffn_expert_w, f32).transpose(0, 2, 1)),
        "wqT": ct(np.asarray(L1_attn_Wq, f32).T),
        "wkT": ct(np.asarray(L1_attn_Wk, f32).T),
        "wvT": ct(np.asarray(L1_attn_Wv, f32).T),
        "bq": ct(np.asarray(L1_attn_bq, f32).reshape(HD, 1)),
        "bk": ct(np.asarray(L1_attn_bk, f32).reshape(HD, 1)),
        "bv": ct(np.asarray(L1_attn_bv, f32).reshape(HD, 1)),
        "eyeb": eyeb, "eyef": eyef, "sel8": sel8_host(),
    }
    in_maps_A = []
    for c in range(NCORE):
        m = dict(common_A)
        m["xa"] = ct(x[c])
        m["xaT"] = ct(x[c].T)
        in_maps_A.append(m)

    ncA = build_A()
    resA = run_bass_kernel_spmd(ncA, in_maps_A, core_ids=list(range(NCORE)))
    ra = resA.results

    kfull = ct(np.concatenate([np.asarray(ra[c]["k_o"]) for c in range(NCORE)], axis=1))
    vfull = ct(np.concatenate([np.asarray(ra[c]["v_o"]) for c in range(NCORE)], axis=0))
    qkm = ct(np.concatenate([np.asarray(ra[c]["qkm_o"], f32) for c in range(NCORE)],
                            axis=0).T.reshape(1, 16))

    sc = lambda s: ct(np.asarray(s, f32).reshape(-1, 128).T)
    common_B = {
        "kfull": kfull, "vfull": vfull, "qkm": qkm,
        "w1T": _tile_w(L1_ffn_w1_w, NHC),
        "w2T": _tile_w(L1_ffn_w2_w, NHC),
        "h1T": _tile_w(L1_ffn_h1_w, 32),
        "h2T": _tile_w(L1_ffn_h2_w, 32),
        "w3T": ct(np.asarray(L1_ffn_w3_w, f32).T.astype(_bf)),
        "s1d": sc(L1_ffn_w1_s), "s2d": sc(L1_ffn_w2_s),
        "h1sd": sc(L1_ffn_h1_s), "h2sd": sc(L1_ffn_h2_s),
        "s3d": ct(np.asarray(L1_ffn_w3_s, f32).reshape(1, D)),
        "eyef": eyef,
    }
    in_maps_B = []
    for c in range(NCORE):
        m = dict(common_B)
        m["x3"] = np.asarray(ra[c]["x3_o"], f32)
        m["qf"] = np.asarray(ra[c]["q_o"])
        in_maps_B.append(m)

    ncB = build_B()
    resB = run_bass_kernel_spmd(ncB, in_maps_B, core_ids=list(range(NCORE)))
    out = np.stack([np.asarray(resB.results[c]["out_o"], f32) for c in range(NCORE)],
                   axis=0)
    return out
